# version 13
# baseline (speedup 1.0000x reference)
import math
from contextlib import ExitStack

import numpy as np
import ml_dtypes

import concourse.bass as bass
import concourse.mybir as mybir
from concourse.bass_utils import run_bass_kernel_spmd

F32 = mybir.dt.float32
BF16 = mybir.dt.bfloat16
I32 = mybir.dt.int32
AF = mybir.ActivationFunctionType
ALU = mybir.AluOpType
AX = mybir.AxisListType.X

H = 4096
QKV = 6144
NH, NKV, D = 32, 8, 128
G_HEADS = 4
NCLUST = 64
TEMP_INV = 10.0
THRESH = 0.01
EPS = 1e-5
SCALE = D ** -0.5
C_SHIFT = 20.0
BIG = 1.0e33

N_CORES = 8

ENGS = ("T", "V", "A", "G", "Y")


def _alibi_slopes(n):
    cp2 = 2 ** math.floor(math.log2(n))
    base = 2.0 ** (-8.0 / cp2)
    slopes = base ** np.arange(1, cp2 + 1, dtype=np.float32)
    if cp2 != n:
        extra_base = 2.0 ** (-4.0 / cp2)
        extra = extra_base ** np.arange(1, 1 + 2 * (n - cp2), 2, dtype=np.float32)
        slopes = np.concatenate([slopes, extra])
    return slopes.astype(np.float32)


SLOPES = _alibi_slopes(NH)


def _sd_all(nq):
    ntile = 4 * nq + 4
    out = np.full((128, ntile * 128), BIG, dtype=np.float32)
    qr = np.arange(128)
    kr = np.arange(128)[:, None]
    for didx in range(ntile):
        delta = didx - 3
        rel = delta * 128 + qr[None, :] - kr
        blk = np.where(rel >= 0, np.sqrt(np.maximum(rel, 0).astype(np.float64)), BIG)
        out[:, didx * 128:(didx + 1) * 128] = blk.astype(np.float32)
    return out



def _copy(e, dst, src):
    if isinstance(e, bass.BassScalarEngine):
        return e.copy(dst, src)
    return e.tensor_copy(dst, src)

class Ctx:

    def __init__(self, nc, me, engines, sems):
        self.nc = nc
        self.me = me
        self.engines = engines
        self.sems = sems
        self.cnt = {k: 0 for k in sems}
        self.bar_k = 0

    def c(self, s):
        return self.cnt[s]

    def op(self, eng, fn, waits=(), inc=None, inc_n=1):
        if self.me == eng:
            e = self.engines[eng]
            for sname, val in waits:
                if val > 0:
                    e.wait_ge(self.sems[sname], val)
            inst = fn(e)
            if inc is not None:
                inst.then_inc(self.sems[inc], inc_n)
        if inc is not None:
            self.cnt[inc] += inc_n
        return self.cnt[inc] if inc is not None else None

    def wait(self, eng, *waits):
        if self.me == eng:
            e = self.engines[eng]
            for sname, val in waits:
                if val > 0:
                    e.wait_ge(self.sems[sname], val)

    def dma(self, eng, out, in_, sem, waits=()):
        return self.op(eng, lambda e: e.dma_start(out=out, in_=in_),
                       waits=waits, inc=sem, inc_n=16)

    def barrier(self, self_waits=None):
        self.bar_k += 1
        target = 5 * self.bar_k
        own = {"T": "sT", "V": "sV", "A": "sA", "G": "sG"}
        for eng in ENGS:
            w = list((self_waits or {}).get(eng, ()))
            if eng in own and self.cnt[own[eng]] > 0:
                w.append((own[eng], self.cnt[own[eng]]))
            self.op(eng, lambda e: e.nop(), waits=w, inc="bar")
        for eng in ENGS:
            self.wait(eng, ("bar", target))


def build_nc(S=2048):
    nT = S // 128
    nQ = S // 512
    TOKW = S
    ND = 4 * nQ + 4

    nc = bass.Bass("TRN2", target_bir_lowering=False)

    x0 = nc.dram_tensor("x0", [S, H], F32, kind="ExternalInput").ap()
    w_qkv = nc.dram_tensor("w_qkv", [H, QKV], F32, kind="ExternalInput").ap()
    w_o = nc.dram_tensor("w_o", [H, H], F32, kind="ExternalInput").ap()
    ffn_w = nc.dram_tensor("ffn_w", [H, H], F32, kind="ExternalInput").ap()
    ffn_b = nc.dram_tensor("ffn_b", [1, H], F32, kind="ExternalInput").ap()
    cent = nc.dram_tensor("cent", [NCLUST, H], F32, kind="ExternalInput").ap()
    assign = nc.dram_tensor("assign", [1, H], I32, kind="ExternalInput").ap()
    out = nc.dram_tensor("out", [S, H], F32, kind="ExternalOutput").ap()

    wqkv_r = w_qkv.rearrange("(c p) n -> p c n", p=128)
    wo_r = w_o.rearrange("(c p) n -> p c n", p=128)

    q_scr = nc.dram_tensor("q_scr", [NH, 128, S], BF16)
    k_scr = nc.dram_tensor("k_scr", [NKV, 128, S], BF16)
    v_scr = nc.dram_tensor("v_scr", [NKV, 128, S], BF16)
    x1_scr = nc.dram_tensor("x1_scr", [S, H], F32)
    w_bf_scr = nc.dram_tensor("w_bf_scr", [H, H], BF16)
    rs_bc_scr = nc.dram_tensor("rs_bc_scr", [128, H], BF16)
    cc_in = nc.dram_tensor("cc_in", [NCLUST, 1], F32)
    cc_out = nc.dram_tensor("cc_out", [NCLUST, 1], F32)

    ident_bf = nc.inline_tensor(np.eye(128, dtype=ml_dtypes.bfloat16), "ident_bf")
    ident_f = nc.inline_tensor(np.eye(128, dtype=np.float32), "ident_f")
    sd_const = nc.inline_tensor(_sd_all(nQ), "sd_const")
    ones_const = nc.inline_tensor(np.ones((1, 128), dtype=ml_dtypes.bfloat16), "ones_const")
    iota_const = nc.inline_tensor(
        np.arange(NCLUST, dtype=np.float32).reshape(NCLUST, 1), "iota_const")
    onescol_const = nc.inline_tensor(np.ones((128, 1), dtype=ml_dtypes.bfloat16), "onescol_const")
    onesf_const = nc.inline_tensor(np.ones((1, 128), dtype=np.float32), "onesf_const")

    ident_sb = nc.alloc_sbuf_tensor("ident_sb", [128, 128], BF16)
    ones_sb = nc.alloc_sbuf_tensor("ones_sb", [1, 128], BF16)
    iota_sb = nc.alloc_sbuf_tensor("iota_sb", [NCLUST, 1], F32)
    qs_all = nc.alloc_sbuf_tensor("qs_all", [128, nT], F32)
    lnstat = nc.alloc_sbuf_tensor("lnstat", [128, 8], F32)
    eps_sb = nc.alloc_sbuf_tensor("eps_sb", [128, 1], F32)
    onescol_sb = nc.alloc_sbuf_tensor("onescol_sb", [128, 1], BF16)
    onesf_sb = nc.alloc_sbuf_tensor("onesf_sb", [1, 128], F32)

    esA = ExitStack()
    ln1_fm = esA.enter_context(nc.sbuf_tensor("ln1_fm", [128, 32 * TOKW], BF16))
    xbuf = [esA.enter_context(nc.sbuf_tensor(f"xbuf{i}", [128, H], F32)) for i in range(2)]
    ln1_tm = [esA.enter_context(nc.sbuf_tensor(f"ln1_tm{i}", [128, H], BF16)) for i in range(2)]
    w_sbA = [esA.enter_context(nc.sbuf_tensor(f"w_sbA{i}", [128, 32, 128], BF16)) for i in range(2)]
    qkv_stage = [esA.enter_context(nc.sbuf_tensor(f"qkv_st{i}", [128, 512], BF16)) for i in range(2)]
    vstage = esA.enter_context(nc.sbuf_tensor("vstage", [128, 512], BF16))
    vtr_stage = [esA.enter_context(nc.sbuf_tensor(f"vtr_st{i}", [128, 512], BF16)) for i in range(2)]
    psA = ExitStack()
    ps_tr = [psA.enter_context(nc.psum_tensor(f"ps_tr{i}", [128, 128], BF16)) for i in range(4)]
    ps_qkv = [psA.enter_context(nc.psum_tensor(f"ps_qkv{i}", [128, 512], F32)) for i in range(4)]
    psA.close()
    esA.close()

    esBC = ExitStack()
    attn_fm = esBC.enter_context(nc.sbuf_tensor("attn_fm", [128, 32 * TOKW], BF16))
    esB = ExitStack()
    sd_sb = esB.enter_context(nc.sbuf_tensor("sd_sb", [128, ND * 128], F32))
    k_sb = [esB.enter_context(nc.sbuf_tensor(f"k_sb{i}", [128, S], BF16)) for i in range(2)]
    v_sb = [esB.enter_context(nc.sbuf_tensor(f"v_sb{i}", [128, S], BF16)) for i in range(2)]
    p_all = [esB.enter_context(nc.sbuf_tensor(f"p_all{i}", [128, nT * 512], BF16)) for i in range(2)]
    q_tile = [esB.enter_context(nc.sbuf_tensor(f"q_tile{i}", [128, 512], BF16)) for i in range(4)]
    rec_row = [esB.enter_context(nc.sbuf_tensor(f"rec_row{i}", [1, 512], F32)) for i in range(2)]
    rec_bc = [esB.enter_context(nc.sbuf_tensor(f"rec_bc{i}", [128, 512], F32)) for i in range(2)]
    psB = ExitStack()
    ps_s = [psB.enter_context(nc.psum_tensor(f"ps_s{i}", [128, 512], F32)) for i in range(2)]
    ps_avf = [psB.enter_context(nc.psum_tensor(f"ps_avf{i}", [128, 512], F32)) for i in range(2)]
    ps_den4 = [psB.enter_context(nc.psum_tensor(f"ps_den{i}", [128, 512], F32)) for i in range(4)]
    psB.close()
    esB.close()
    esC = ExitStack()
    wo_sb = [esC.enter_context(nc.sbuf_tensor(f"wo_sb{i}", [128, 32, 512], BF16)) for i in range(2)]
    x0_stage = [esC.enter_context(nc.sbuf_tensor(f"x0_st{i}", [128, 512], F32)) for i in range(2)]
    x1_stage = [esC.enter_context(nc.sbuf_tensor(f"x1_st{i}", [128, 512], F32)) for i in range(2)]
    psC = ExitStack()
    ps_x1 = [psC.enter_context(nc.psum_tensor(f"ps_x1{i}", [128, 512], F32)) for i in range(3)]
    psC.close()
    esC.close()
    esBC.close()

    esR = ExitStack()
    cent_T = esR.enter_context(nc.sbuf_tensor("cent_T", [128, 32 * NCLUST], BF16, side="right"))
    a1_sb = esR.enter_context(nc.sbuf_tensor("a1_sb", [NCLUST, H], BF16, side="right"))
    esD0 = ExitStack()
    cent_sb = esD0.enter_context(nc.sbuf_tensor("cent_sb", [NCLUST, H], BF16))
    asg_i = esD0.enter_context(nc.sbuf_tensor("asg_i", [1, H], I32))
    asg_f = esD0.enter_context(nc.sbuf_tensor("asg_f", [1, H], F32))
    asg_bf = esD0.enter_context(nc.sbuf_tensor("asg_bf", [1, H], BF16))
    esD0.close()
    esD = ExitStack()
    ln2_fm = esD.enter_context(nc.sbuf_tensor("ln2_fm", [128, 32 * TOKW], BF16))
    esD1 = ExitStack()
    x1buf = [esD1.enter_context(nc.sbuf_tensor(f"x1buf{i}", [128, H], F32)) for i in range(2)]
    ln2_tm = [esD1.enter_context(nc.sbuf_tensor(f"ln2_tm{i}", [128, H], BF16)) for i in range(2)]
    identf_sb = esD1.enter_context(nc.sbuf_tensor("identf_sb", [128, 128], F32))
    dots_e = esD1.enter_context(nc.sbuf_tensor("dots_e", [128, NCLUST], F32))
    dots_n = esD1.enter_context(nc.sbuf_tensor("dots_n", [128, NCLUST], F32))
    dstat = esD1.enter_context(nc.sbuf_tensor("dstat", [128, 8], F32))
    cmax_t = esD1.enter_context(nc.sbuf_tensor("cmax_t", [NCLUST, 1], F32))
    run_cmax = esD1.enter_context(nc.sbuf_tensor("run_cmax", [NCLUST, 1], F32))
    gmax_sb = esD1.enter_context(nc.sbuf_tensor("gmax_sb", [NCLUST, 1], F32))
    cs_f = esD1.enter_context(nc.sbuf_tensor("cs_f", [NCLUST, 1], F32))
    cs_bf = esD1.enter_context(nc.sbuf_tensor("cs_bf", [NCLUST, 1], BF16))
    rs_row_bf = esD1.enter_context(nc.sbuf_tensor("rs_row_bf", [1, 512], BF16))
    rs_stage = esD1.enter_context(nc.sbuf_tensor("rs_stage", [128, 512], BF16))
    esD1.close()
    esR.close()
    esD3 = ExitStack()
    wT_sb = [esD3.enter_context(nc.sbuf_tensor(f"wT_sb{i}", [128, 32, 512], BF16)) for i in range(2)]
    rs_bc = [esD3.enter_context(nc.sbuf_tensor(f"rs_bc{i}", [128, 512], BF16)) for i in range(2)]
    biasb_oc = [esD3.enter_context(nc.sbuf_tensor(f"biasb_oc{i}", [1, 512], BF16)) for i in range(2)]
    x1s_stage = [esD3.enter_context(nc.sbuf_tensor(f"x1s_st{i}", [128, 512], F32)) for i in range(2)]
    out_stage = [esD3.enter_context(nc.sbuf_tensor(f"out_st{i}", [128, 512], F32)) for i in range(2)]
    esD3.close()
    esD.close()
    psD = ExitStack()
    ps_trD = [psD.enter_context(nc.psum_tensor(f"ps_trD{i}", [128, 128], BF16)) for i in range(4)]
    ps_dots = psD.enter_context(nc.psum_tensor("ps_dots", [128, 512], F32))
    ps_dt = psD.enter_context(nc.psum_tensor("ps_dt", [NCLUST, 128], F32))
    ps_d = [psD.enter_context(nc.psum_tensor(f"ps_d{i}", [128, 512], F32)) for i in range(2)]
    psD.close()

    sem_names = ["bar", "sT", "sV", "sA", "sG", "dP", "dD", "dCC",
                 "dDg", "dXA0", "dXA1", "dW0", "dW1", "dSQ0", "dSQ1", "dSV0", "dSV1",
                 "dKV0", "dKV1", "dQL0", "dQL1", "dQL2", "dQL3", "dWO0", "dWO1", "dX00", "dX01",
                 "dX1S0", "dX1S1", "dX1L0", "dX1L1", "dWF0", "dWF1", "dWT0", "dWT1",
                 "dX1B0", "dX1B1", "dOUT0", "dOUT1"]
    sems = {n: nc.alloc_semaphore(name=n) for n in sem_names}

    engines = {"T": nc.tensor, "V": nc.vector, "A": nc.scalar,
               "G": nc.gpsimd, "Y": nc.sync}

    def emit(c: Ctx):
        c.dma("Y", ident_sb[:], ident_bf.ap(), "dP")
        c.dma("Y", ones_sb[:], ones_const.ap(), "dP")
        c.dma("Y", iota_sb[:], iota_const.ap(), "dP")
        c.dma("Y", onescol_sb[:], onescol_const.ap(), "dP")
        c.dma("Y", onesf_sb[:], onesf_const.ap(), "dP")
        c.op("V", lambda e: e.memset(eps_sb[:], EPS), inc="sV")
        p0_done = {"Y": [("dP", c.c("dP"))]}
        c.barrier(self_waits=p0_done)

        ln_free_v = {}
        ln_free_a = {}
        lncp = {}
        tr_mark = {}
        var_mark = {}
        for t in range(nT):
            w = []
            if t >= 2:
                w = [("sV", ln_free_v[t - 2]), ("sA", ln_free_a[t - 2])]
            xa = c.dma("Y", xbuf[t % 2][:], x0[t * 128:(t + 1) * 128, :],
                       f"dXA{t % 2}", waits=w)
            xb = xbuf[t % 2]
            lt = ln1_tm[t % 2]
            m0 = c.op("V", lambda e, xb=xb: e.reduce_sum(lnstat[:, 0:1], xb[:], axis=AX),
                      waits=[(f"dXA{t % 2}", xa)], inc="sV")
            wsq = [(f"dXA{t % 2}", xa)]
            if t >= 1:
                wsq.append(("sV", var_mark[t - 1]))
            if t >= 2:
                wsq.append(("sT", tr_mark[(t - 2, 31)]))
            a0 = c.op("A", lambda e, xb=xb, lt=lt: e.activation(
                out=lt[:], in_=xb[:], func=AF.Square, accum_out=lnstat[:, 2:3]),
                waits=wsq, inc="sA")
            ln_free_a[t] = a0
            m1 = c.op("V", lambda e: e.tensor_scalar(lnstat[:, 1:2], lnstat[:, 0:1],
                                                     1.0 / H, None, ALU.mult),
                      waits=[("sV", m0)], inc="sV")
            m1b = c.op("V", lambda e: e.tensor_scalar(lnstat[:, 5:6], lnstat[:, 1:2],
                                                      lnstat[:, 1:2], None, ALU.mult),
                      waits=[("sV", m1)], inc="sV")
            m1c = c.op("V", lambda e: e.tensor_scalar(lnstat[:, 6:7], lnstat[:, 2:3],
                                                      1.0 / H, lnstat[:, 5:6],
                                                      ALU.mult, ALU.subtract),
                      waits=[("sV", m1b), ("sA", a0)], inc="sV")
            var_mark[t] = m1c
            a1 = c.op("A", lambda e: e.activation(out=lnstat[:, 3:4], in_=lnstat[:, 6:7],
                                                  func=AF.Ln, bias=eps_sb[:]),
                      waits=[("sV", m1c)], inc="sA")
            a2 = c.op("A", lambda e: e.activation(out=lnstat[:, 4:5], in_=lnstat[:, 3:4],
                                                  func=AF.Exp, scale=-0.5),
                      waits=[("sA", a1)], inc="sA")
            m3 = c.op("V", lambda e, xb=xb, lt=lt: e.tensor_scalar(
                lt[:], xb[:], lnstat[:, 1:2], lnstat[:, 4:5], ALU.subtract, ALU.mult),
                waits=[("sA", a2)], inc="sV")
            ln_free_v[t] = m3
            for ch in range(32):
                wt = [("sV", m3)] if ch == 0 else []
                if (t, ch - 4) in lncp:
                    e_, cnt_ = lncp[(t, ch - 4)]
                    wt.append(("sV" if e_ == "V" else "sA", cnt_))
                elif t >= 1 and ch < 4:
                    e_, cnt_ = lncp[(t - 1, 28 + ch)]
                    wt.append(("sV" if e_ == "V" else "sA", cnt_))
                tm = c.op("T", lambda e, ch=ch, lt=lt: e.transpose(
                    ps_tr[ch % 4][:], lt[:, ch * 128:(ch + 1) * 128], ident_sb[:]),
                    waits=wt, inc="sT")
                tr_mark[(t, ch)] = tm
                ceng = "V" if (ch % 2 == 0) else "A"
                dst = ln1_fm[:, ch * TOKW + t * 128: ch * TOKW + (t + 1) * 128]
                cm = c.op(ceng, lambda e, dst=dst, ch=ch: _copy(e, dst, ps_tr[ch % 4][:]),
                          waits=[("sT", tm)], inc="sV" if ceng == "V" else "sA")
                lncp[(t, ch)] = (ceng, cm)

        NJG = QKV // 128
        grp_last_mm = {}
        slot_free = {}
        stage_free = {}
        vtr_free = {}
        ps_tr_cons = {sub: lncp[(nT - 1, 28 + sub)] for sub in range(4)}
        ln_chunk_ready = {n: [({"V": "sV", "A": "sA"}[lncp[(4 * n + 3, ch)][0]],
                               lncp[(4 * n + 3, ch)][1]) for ch in (30, 31)]
                          for n in range(nQ)}
        for jg in range(NJG):
            wv = []
            if jg >= 2:
                wv.append(("sT", grp_last_mm[jg - 2]))
            wld = c.dma("G", w_sbA[jg % 2][:], wqkv_r[:, :, jg * 128:(jg + 1) * 128],
                        f"dW{jg % 2}", waits=wv)
            if jg < 32:
                c.dma("G", w_bf_scr.ap()[jg * 128:(jg + 1) * 128, :],
                      ffn_w[jg * 128:(jg + 1) * 128, :], "dDg")
            j_abs = jg
            for n in range(nQ):
                slot = n % 4
                wmm = [(f"dW{jg % 2}", wld)]
                if jg < 2:
                    wmm.extend(ln_chunk_ready[n])
                if slot in slot_free:
                    e_, cnt_ = slot_free[slot]
                    wmm.append(({"V": "sV", "A": "sA", "T": "sT"}[e_], cnt_))
                last = None
                for ch in range(32):
                    last = c.op("T", lambda e, ch=ch, n=n, jg=jg, slot=slot: e.matmul(
                        ps_qkv[slot][:],
                        w_sbA[jg % 2][:, ch, :],
                        ln1_fm[:, ch * TOKW + n * 512: ch * TOKW + (n + 1) * 512],
                        start=(ch == 0), stop=(ch == 31)),
                        waits=wmm if ch == 0 else [],
                        inc="sT" if ch == 31 else None)
                grp_last_mm[jg] = last
                if j_abs < NH:
                    st = n % 2
                    wep = [("sT", last)]
                    if st in stage_free:
                        wep.append((f"dSQ{st}", stage_free[st]))
                    acp = c.op("A", lambda e, st=st, slot=slot: e.activation(
                        out=qkv_stage[st][:], in_=ps_qkv[slot][:], func=AF.Copy,
                        scale=SCALE),
                        waits=wep, inc="sA")
                    slot_free[slot] = ("A", acp)
                    dsp = c.dma("Y", q_scr.ap()[j_abs, :, n * 512:(n + 1) * 512],
                                qkv_stage[st][:], f"dSQ{st}", waits=[("sA", acp)])
                    stage_free[st] = dsp
                elif j_abs < NH + NKV:
                    hk = j_abs - NH
                    st = n % 2
                    wep = [("sT", last)]
                    if st in stage_free:
                        wep.append((f"dSQ{st}", stage_free[st]))
                    acp = c.op("A", lambda e, st=st, slot=slot: e.activation(
                        out=qkv_stage[st][:], in_=ps_qkv[slot][:], func=AF.Copy),
                        waits=wep, inc="sA")
                    slot_free[slot] = ("A", acp)
                    dsp = c.dma("Y", k_scr.ap()[hk, :, n * 512:(n + 1) * 512],
                                qkv_stage[st][:], f"dSQ{st}", waits=[("sA", acp)])
                    stage_free[st] = dsp
                else:
                    hk = j_abs - NH - NKV
                    wep = [("sT", last)]
                    if "vstage" in slot_free:
                        wep.append(("sT", slot_free["vstage"][1]))
                    vcp = c.op("A", lambda e, slot=slot: e.activation(
                        out=vstage[:], in_=ps_qkv[slot][:], func=AF.Copy),
                        waits=wep, inc="sA")
                    slot_free[slot] = ("A", vcp)
                    vslot = n % 2
                    trs = []
                    for sub in range(4):
                        wtr = [("sA", vcp)] if sub == 0 else []
                        if sub in ps_tr_cons:
                            e_, cnt_ = ps_tr_cons[sub]
                            wtr = wtr + [("sV" if e_ == "V" else "sA", cnt_)]
                        tm = c.op("T", lambda e, sub=sub: e.transpose(
                            ps_tr[sub % 4][:], vstage[:, sub * 128:(sub + 1) * 128],
                            ident_sb[:]), waits=wtr, inc="sT")
                        trs.append(tm)
                    slot_free["vstage"] = ("T", trs[-1])
                    wcp_extra = [(f"dSV{vslot}", vtr_free[vslot])] if vslot in vtr_free else []
                    vcps = []
                    for sub in range(4):
                        ceng = "V" if sub % 2 == 0 else "A"
                        cm = c.op(ceng, lambda e, sub=sub, vslot=vslot: _copy(e, vtr_stage[vslot][:, sub * 128:(sub + 1) * 128], ps_tr[sub % 4][:]),
                            waits=[("sT", trs[sub])] + (wcp_extra if sub < 2 else []),
                            inc="sV" if ceng == "V" else "sA")
                        vcps.append((ceng, cm))
                        ps_tr_cons[sub] = (ceng, cm)
                    wd = [({"V": "sV", "A": "sA"}[e_], cnt_) for e_, cnt_ in vcps[-2:]]
                    dsp = c.dma("Y", v_scr.ap()[hk, :, n * 512:(n + 1) * 512],
                                vtr_stage[vslot][:], f"dSV{vslot}", waits=wd)
                    vtr_free[vslot] = dsp
        c.barrier(self_waits={"G": [("dW0", c.c("dW0")), ("dW1", c.c("dW1"))],
                              "Y": [("dSQ0", c.c("dSQ0")), ("dSQ1", c.c("dSQ1")),
                                    ("dSV0", c.c("dSV0")), ("dSV1", c.c("dSV1"))]})

        c.dma("Y", sd_sb[:], sd_const.ap(), "dP")
        sd_ready = c.c("dP")

        kv_last_use = {}
        q_last = {}
        avf_free = {}
        bcden_free = {}
        pall_free = {}
        prev_exp = {}
        pend_ep = [None]

        def flush_ep():
            ep = pend_ep[0]
            if ep is None:
                return
            pend_ep[0] = None
            hs, np_, par, avf_stop, den_stop = ep
            marks = []
            for st_ in range(2):
                dt = ps_den4[par * 2 + st_]
                ra = c.op("A", lambda e, dt=dt, st_=st_: e.activation(
                    out=rec_row[st_][:], in_=dt[0:1, :], func=AF.Ln),
                    waits=[("sT", den_stop)] if st_ == 0 else [], inc="sA")
                r0 = c.op("A", lambda e, st_=st_: e.activation(
                    out=rec_row[st_][:], in_=rec_row[st_][:], func=AF.Exp, scale=-1.0),
                    waits=[("sA", ra)], inc="sA")
                bc = c.op("T", lambda e, dt=dt, st_=st_: e.matmul(
                    dt[:, :], onesf_sb[:], rec_row[st_][:], start=True, stop=True),
                    waits=[("sA", r0)], inc="sT")
                wcp = [("sT", bc)]
                if st_ in avf_free:
                    wcp.append(("sV", avf_free[st_]))
                cpm = c.op("A", lambda e, dt=dt, st_=st_: _copy(
                    e, rec_bc[st_][:], dt[:, :]), waits=wcp, inc="sA")
                bcden_free[par * 2 + st_] = cpm
                marks.append(cpm)
            for st_ in range(2):
                dst = attn_fm[:, hs[st_] * TOKW + np_ * 512: hs[st_] * TOKW + (np_ + 1) * 512]
                tt = c.op("V", lambda e, dst=dst, st_=st_: e.tensor_tensor(
                    dst, ps_avf[st_][:], rec_bc[st_][:], ALU.mult),
                    waits=[("sT", avf_stop), ("sA", marks[st_])], inc="sV")
                avf_free[st_] = tt

        pairs = [(hk, n, pr) for hk in range(NKV) for n in range(nQ) for pr in range(2)]
        q_rdy = {}

        def q_load(pi):
            hk, n, pr = pairs[pi]
            for st_ in range(2):
                h = hk * G_HEADS + 2 * pr + st_
                qslot = (2 * pi + st_) % 4
                wq = []
                if qslot in q_last:
                    wq.append(("sT", q_last[qslot]))
                c.dma("Y", q_tile[qslot][:], q_scr.ap()[h, :, n * 512:(n + 1) * 512],
                      f"dQL{qslot}", waits=wq)
                q_rdy[(pi, st_)] = (f"dQL{qslot}", c.c(f"dQL{qslot}"))

        kv_rdy = {}
        for pi, (hk, n, pr) in enumerate(pairs):
            if n == 0 and pr == 0:
                wkv = []
                if hk >= 2:
                    wkv.append(("sT", kv_last_use[hk % 2]))
                c.dma("Y", k_sb[hk % 2][:], k_scr.ap()[hk, :, :], f"dKV{hk % 2}",
                      waits=wkv)
                c.dma("Y", v_sb[hk % 2][:], v_scr.ap()[hk, :, :], f"dKV{hk % 2}")
                kv_rdy[hk] = (f"dKV{hk % 2}", c.c(f"dKV{hk % 2}"))
            if pi == 0:
                q_load(0)
            kvw = kv_rdy[hk]
            par = pi % 2
            h0 = hk * G_HEADS + 2 * pr
            slopes2 = (-float(SLOPES[h0]), -float(SLOPES[h0 + 1]))
            qsl = [(2 * pi + st_) % 4 for st_ in range(2)]
            nkt = 4 * n + 4

            def score(st_, kt):
                bank = st_
                wsc = [kvw, q_rdy[(pi, st_)]]
                if bank in prev_exp:
                    wsc.append(("sA", prev_exp[bank]))
                sm = c.op("T", lambda e, kt=kt, hk=hk, st_=st_, bank=bank: e.matmul(
                    ps_s[bank][:], k_sb[hk % 2][:, kt * 128:(kt + 1) * 128],
                    q_tile[qsl[st_]][:], start=True, stop=True),
                    waits=wsc, inc="sT")
                q_last[qsl[st_]] = sm
                kv_last_use[hk % 2] = sm
                return sm

            sms = {}
            for st_ in range(2):
                sms[(st_, 0)] = score(st_, 0)
            for kt in range(nkt):
                didx = 4 * n - kt + 3
                tts = {}
                for st_ in range(2):
                    bank = st_
                    wst = [("sT", sms[(st_, kt)])]
                    if kt == 0 and st_ == 0:
                        wst.append(("dP", sd_ready))
                    tts[st_] = c.op("V", lambda e, bank=bank, didx=didx, st_=st_: e.scalar_tensor_tensor(
                        ps_s[bank][:], sd_sb[:, didx * 128: didx * 128 + 512],
                        slopes2[st_], ps_s[bank][:], ALU.mult, ALU.add),
                        waits=wst, inc="sV")
                ems = {}
                for st_ in range(2):
                    bank = st_
                    wex = [("sV", tts[st_])]
                    if kt == 0 and st_ in pall_free:
                        wex.append(("sT", pall_free[st_]))
                    ems[st_] = c.op("A", lambda e, kt=kt, bank=bank, st_=st_: e.activation(
                        out=p_all[st_][:, kt * 512:(kt + 1) * 512], in_=ps_s[bank][:],
                        func=AF.Exp), waits=wex, inc="sA")
                    prev_exp[bank] = ems[st_]
                if kt == 0:
                    flush_ep()
                    if pi + 1 < len(pairs):
                        q_load(pi + 1)
                if kt + 1 < nkt:
                    for st_ in range(2):
                        sms[(st_, kt + 1)] = score(st_, kt + 1)
                for st_ in range(2):
                    wav = [("sA", ems[st_])]
                    if kt == 0 and st_ in avf_free:
                        wav.append(("sV", avf_free[st_]))
                    c.op("T", lambda e, kt=kt, hk=hk, st_=st_, nkt=nkt: e.matmul(
                        ps_avf[st_][:], v_sb[hk % 2][:, kt * 128:(kt + 1) * 128],
                        p_all[st_][:, kt * 512:(kt + 1) * 512],
                        start=(kt == 0), stop=(kt == nkt - 1)),
                        waits=wav, inc="sT")
                    wdn = []
                    if kt == 0 and (par * 2 + st_) in bcden_free:
                        wdn.append(("sA", bcden_free[par * 2 + st_]))
                    c.op("T", lambda e, kt=kt, par=par, st_=st_, nkt=nkt: e.matmul(
                        ps_den4[par * 2 + st_][0:1, :], onescol_sb[:],
                        p_all[st_][:, kt * 512:(kt + 1) * 512],
                        start=(kt == 0), stop=(kt == nkt - 1)),
                        waits=wdn, inc="sT")
                if kt == nkt - 1:
                    last_T = c.c("sT")
                    for st_ in range(2):
                        pall_free[st_] = last_T
                    pend_ep[0] = ((h0, h0 + 1), n, par, last_T, last_T)
        flush_ep()
        c.barrier(self_waits={"Y": [("dKV0", c.c("dKV0")), ("dKV1", c.c("dKV1")),
                                    ("dQL0", c.c("dQL0")), ("dQL1", c.c("dQL1")),
                                    ("dQL2", c.c("dQL2")), ("dQL3", c.c("dQL3"))]})

        wo_last_mm = {}
        cslot_tt = {}
        x0s_free = {}
        x1s_free = {}
        x0_rdy = {}

        def c_x0(oc, t):
            sslot = (oc * nT + t) % 2
            wx0 = [("sV", x0s_free[sslot])] if sslot in x0s_free else []
            c.dma("Y", x0_stage[sslot][:],
                  x0[t * 128:(t + 1) * 128, oc * 512:(oc + 1) * 512], f"dX0{sslot}",
                  waits=wx0)
            x0_rdy[(oc, t)] = c.c(f"dX0{sslot}")

        for oc in range(8):
            wv = []
            if oc >= 2:
                wv.append(("sT", wo_last_mm[oc - 2]))
            wold = c.dma("G", wo_sb[oc % 2][:], wo_r[:, :, oc * 512:(oc + 1) * 512],
                         f"dWO{oc % 2}", waits=wv)
            for t in range(nT):
                slot = (oc * nT + t) % 3
                sslot = (oc * nT + t) % 2
                if oc == 0 and t == 0:
                    c_x0(0, 0)
                wmm = [(f"dWO{oc % 2}", wold)]
                if slot in cslot_tt:
                    wmm.append(("sV", cslot_tt[slot]))
                last = None
                for ch in range(32):
                    last = c.op("T", lambda e, ch=ch, oc=oc, t=t, slot=slot: e.matmul(
                        ps_x1[slot][:],
                        attn_fm[:, ch * TOKW + t * 128: ch * TOKW + (t + 1) * 128],
                        wo_sb[oc % 2][:, ch, :],
                        start=(ch == 0), stop=(ch == 31)),
                        waits=wmm if ch == 0 else [],
                        inc="sT" if ch == 31 else None)
                wo_last_mm[oc] = last
                if t + 1 < nT:
                    c_x0(oc, t + 1)
                elif oc + 1 < 8:
                    c_x0(oc + 1, 0)
                wtt = [("sT", last), (f"dX0{sslot}", x0_rdy[(oc, t)])]
                if sslot in x1s_free:
                    wtt.append((f"dX1S{sslot}", x1s_free[sslot]))
                vt = c.op("V", lambda e, slot=slot, sslot=sslot: e.tensor_tensor(
                    x1_stage[sslot][:], ps_x1[slot][:], x0_stage[sslot][:], ALU.add),
                    waits=wtt, inc="sV")
                cslot_tt[slot] = vt
                x0s_free[sslot] = vt
                d1 = c.dma("Y", x1_scr.ap()[t * 128:(t + 1) * 128, oc * 512:(oc + 1) * 512],
                           x1_stage[sslot][:], f"dX1S{sslot}", waits=[("sV", vt)])
                x1s_free[sslot] = d1
        c.barrier(self_waits={"G": [("dWO0", c.c("dWO0")), ("dWO1", c.c("dWO1")),
                                    ("dDg", c.c("dDg"))],
                              "Y": [("dX1S0", c.c("dX1S0")), ("dX1S1", c.c("dX1S1")),
                                    ("dX00", c.c("dX00")), ("dX01", c.c("dX01"))]})

        c.dma("G", cent_sb[:], cent, "dDg")
        c.dma("Y", asg_i[:], assign, "dD")
        dD0 = c.c("dD")
        dDg0 = c.c("dDg")
        v0 = c.op("V", lambda e: e.tensor_copy(asg_f[:], asg_i[:]),
                  waits=[("dD", dD0)], inc="sV")
        v1 = c.op("V", lambda e: e.tensor_copy(asg_bf[:], asg_f[:]),
                  waits=[("sV", v0)], inc="sV")
        for ocb in range(8):
            tb = c.op("T", lambda e, ocb=ocb: e.matmul(
                ps_d[0][:NCLUST, :], ones_sb[0:1, 0:NCLUST],
                asg_bf[0:1, ocb * 512:(ocb + 1) * 512], start=True, stop=True),
                waits=[("sV", v1)] if ocb == 0 else [("sV", c.c("sV"))], inc="sT")
            c.op("V", lambda e, ocb=ocb: e.tensor_scalar(
                a1_sb[:, ocb * 512:(ocb + 1) * 512], ps_d[0][:NCLUST, :],
                iota_sb[:], None, ALU.is_equal),
                waits=[("sT", tb)], inc="sV")
        ctp = {}
        for ch in range(32):
            wt = [("dDg", dDg0)] if ch == 0 else []
            if ch >= 4:
                e_, cnt_ = ctp[ch - 4]
                wt.append(("sA" if e_ == "A" else "sV", cnt_))
            tm = c.op("T", lambda e, ch=ch: e.transpose(
                ps_trD[ch % 4][:, 0:NCLUST],
                cent_sb[0:NCLUST, ch * 128:(ch + 1) * 128],
                ident_sb[0:NCLUST, 0:NCLUST]),
                waits=wt, inc="sT")
            ceng = "A" if ch % 2 == 0 else "V"
            cm = c.op(ceng, lambda e, ch=ch: _copy(e, cent_T[:, ch * NCLUST:(ch + 1) * NCLUST], ps_trD[ch % 4][:, 0:NCLUST]),
                waits=[("sT", tm)], inc="sA" if ceng == "A" else "sV")
            ctp[ch] = (ceng, cm)
        c.barrier(self_waits={"Y": [("dD", c.c("dD"))], "G": [("dDg", c.c("dDg"))]})

        c.dma("Y", identf_sb[:], ident_f.ap(), "dP")
        idf_ready = c.c("dP")
        restage_done = dDg0
        c.op("V", lambda e: e.memset(run_cmax[:], -1e30), inc="sV")
        ln_free_v.clear()
        ln_free_a.clear()
        var_mark.clear()
        lncp2 = {}
        tr_mark2 = {}
        dt_done = {}
        for t in range(nT):
            w = []
            if t >= 2:
                w = [("sV", ln_free_v[t - 2]), ("sA", ln_free_a[t - 2])]
            x1r = c.dma("Y", x1buf[t % 2][:], x1_scr.ap()[t * 128:(t + 1) * 128, :],
                        f"dX1L{t % 2}", waits=w)
            xb = x1buf[t % 2]
            lt = ln2_tm[t % 2]
            m0 = c.op("V", lambda e, xb=xb: e.reduce_sum(lnstat[:, 0:1], xb[:], axis=AX),
                      waits=[(f"dX1L{t % 2}", x1r)], inc="sV")
            wsq = [(f"dX1L{t % 2}", x1r)]
            if t >= 1:
                wsq.append(("sV", var_mark[t - 1]))
            if t >= 2:
                wsq.append(("sT", tr_mark2[(t - 2, 31)]))
            a0 = c.op("A", lambda e, xb=xb, lt=lt: e.activation(
                out=lt[:], in_=xb[:], func=AF.Square, accum_out=lnstat[:, 2:3]),
                waits=wsq, inc="sA")
            ln_free_a[t] = a0
            m1 = c.op("V", lambda e: e.tensor_scalar(lnstat[:, 1:2], lnstat[:, 0:1],
                                                     1.0 / H, None, ALU.mult),
                      waits=[("sV", m0)], inc="sV")
            m1b = c.op("V", lambda e: e.tensor_scalar(lnstat[:, 5:6], lnstat[:, 1:2],
                                                      lnstat[:, 1:2], None, ALU.mult),
                      waits=[("sV", m1)], inc="sV")
            m1c = c.op("V", lambda e: e.tensor_scalar(lnstat[:, 6:7], lnstat[:, 2:3],
                                                      1.0 / H, lnstat[:, 5:6],
                                                      ALU.mult, ALU.subtract),
                      waits=[("sV", m1b), ("sA", a0)], inc="sV")
            var_mark[t] = m1c
            a1 = c.op("A", lambda e: e.activation(out=lnstat[:, 3:4], in_=lnstat[:, 6:7],
                                                  func=AF.Ln, bias=eps_sb[:]),
                      waits=[("sV", m1c)], inc="sA")
            a2 = c.op("A", lambda e: e.activation(out=lnstat[:, 4:5], in_=lnstat[:, 3:4],
                                                  func=AF.Exp, scale=-0.5),
                      waits=[("sA", a1)], inc="sA")
            m3 = c.op("V", lambda e, xb=xb, lt=lt: e.tensor_scalar(
                lt[:], xb[:], lnstat[:, 1:2], lnstat[:, 4:5], ALU.subtract, ALU.mult),
                waits=[("sA", a2)], inc="sV")
            ln_free_v[t] = m3
            for ch in range(32):
                wt = [("sV", m3)] if ch == 0 else []
                if (t, ch - 4) in lncp2:
                    e_, cnt_ = lncp2[(t, ch - 4)]
                    wt.append(("sV" if e_ == "V" else "sA", cnt_))
                elif t >= 1 and ch < 4:
                    e_, cnt_ = lncp2[(t - 1, 28 + ch)]
                    wt.append(("sV" if e_ == "V" else "sA", cnt_))
                tm = c.op("T", lambda e, ch=ch, lt=lt: e.transpose(
                    ps_trD[ch % 4][:], lt[:, ch * 128:(ch + 1) * 128], ident_sb[:]),
                    waits=wt, inc="sT")
                tr_mark2[(t, ch)] = tm
                ceng = "V" if (ch % 2 == 0) else "A"
                dst = ln2_fm[:, ch * TOKW + t * 128: ch * TOKW + (t + 1) * 128]
                cm = c.op(ceng, lambda e, dst=dst, ch=ch: _copy(e, dst, ps_trD[ch % 4][:]),
                          waits=[("sT", tm)], inc="sV" if ceng == "V" else "sA")
                lncp2[(t, ch)] = (ceng, cm)
            wdots = [({"V": "sV", "A": "sA"}[lncp2[(t, ch)][0]], lncp2[(t, ch)][1])
                     for ch in (30, 31)]
            if t >= 1:
                wdots.append(("sV", dt_done[t - 1]))
            lastd = None
            for ch in range(32):
                lastd = c.op("T", lambda e, ch=ch, t=t: e.matmul(
                    ps_dots[:, 0:NCLUST], ln2_fm[:, ch * TOKW + t * 128: ch * TOKW + (t + 1) * 128],
                    cent_T[:, ch * NCLUST:(ch + 1) * NCLUST],
                    start=(ch == 0), stop=(ch == 31)),
                    waits=wdots if ch == 0 else [], inc="sT" if ch == 31 else None)
            d0 = c.op("V", lambda e: e.reduce_max(dstat[:, 0:1], ps_dots[:, 0:NCLUST], axis=AX),
                      waits=[("sT", lastd)], inc="sV")
            d1 = c.op("V", lambda e: e.tensor_scalar(dstat[:, 1:2], dstat[:, 0:1],
                                                     -TEMP_INV, None, ALU.mult),
                      waits=[("sV", d0)], inc="sV")
            d2 = c.op("A", lambda e: e.activation(out=dots_e[:], in_=ps_dots[:, 0:NCLUST], func=AF.Exp,
                                                  bias=dstat[:, 1:2], scale=TEMP_INV),
                      waits=[("sV", d1)], inc="sA")
            d3 = c.op("V", lambda e: e.reduce_sum(dstat[:, 2:3], dots_e[:], axis=AX),
                      waits=[("sA", d2)], inc="sV")
            d4 = c.op("V", lambda e: e.reciprocal(dstat[:, 3:4], dstat[:, 2:3]),
                      waits=[("sV", d3)], inc="sV")
            d5 = c.op("V", lambda e: e.tensor_scalar(dots_n[:], dots_e[:], dstat[:, 3:4],
                                                     None, ALU.mult),
                      waits=[("sV", d4)], inc="sV")
            d6 = c.op("V", lambda e: e.reduce_max(dstat[:, 4:5], dots_n[:], axis=AX),
                      waits=[("sV", d5)], inc="sV")
            c.op("V", lambda e, t=t: e.tensor_single_scalar(
                qs_all[:, t:t + 1], dstat[:, 4:5], THRESH, ALU.is_gt),
                waits=[("sV", d6)], inc="sV")
            td = c.op("T", lambda e: e.transpose(ps_dt[:], dots_n[:], identf_sb[:]),
                      waits=[("sV", d5), ("dP", idf_ready)], inc="sT")
            d8 = c.op("V", lambda e: e.reduce_max(cmax_t[:], ps_dt[:], axis=AX),
                      waits=[("sT", td)], inc="sV")
            d9 = c.op("V", lambda e: e.tensor_tensor(run_cmax[:], run_cmax[:], cmax_t[:],
                                                     ALU.max),
                      waits=[("sV", d8)], inc="sV")
            dt_done[t] = d9
        wt0w = [("dDg", restage_done),
                ("sV", ln_free_v[nT - 1]), ("sA", ln_free_a[nT - 1]),
                ("sV", ln_free_v[nT - 2]), ("sA", ln_free_a[nT - 2])]
        for ch in range(32):
            c.op("Y", lambda e, ch=ch: e.dma_start(
                out=wT_sb[0][:, ch, :],
                in_=w_bf_scr.ap()[0:512, ch * 128:(ch + 1) * 128],
                transpose=True),
                waits=wt0w if ch == 0 else [], inc="dWT0", inc_n=16)
        c.dma("G", cc_in.ap(), run_cmax[:], "dCC", waits=[("sV", dt_done[nT - 1])])
        ccm = c.op("G", lambda e: e.collective_compute(
            "AllReduce", ALU.max, replica_groups=[list(range(N_CORES))],
            ins=[cc_in.ap().opt()], outs=[cc_out.ap().opt()]),
            waits=[("dCC", c.c("dCC"))], inc="sG")
        c.dma("G", gmax_sb[:], cc_out.ap(), "dCC", waits=[("sG", ccm)])
        dcc2 = c.c("dCC")
        e0 = c.op("V", lambda e: e.tensor_single_scalar(cs_f[:], gmax_sb[:], THRESH,
                                                        ALU.is_gt),
                  waits=[("dCC", dcc2)], inc="sV")
        e1 = c.op("V", lambda e: e.tensor_copy(cs_bf[:], cs_f[:]),
                  waits=[("sV", e0)], inc="sV")
        for ocb in range(8):
            tb = c.op("T", lambda e, ocb=ocb: e.matmul(
                ps_dots[0:1, :], cs_bf[:], a1_sb[:, ocb * 512:(ocb + 1) * 512],
                start=True, stop=True),
                waits=[("sV", e1)] if ocb == 0 else [("sV", c.c("sV"))], inc="sT")
            vcp0 = c.op("V", lambda e: e.tensor_copy(rs_row_bf[:], ps_dots[0:1, :]),
                        waits=[("sT", tb)], inc="sV")
            tb2 = c.op("T", lambda e: e.matmul(
                ps_dots[:, :], ones_sb[0:1, :], rs_row_bf[:], start=True, stop=True),
                waits=[("sV", vcp0)], inc="sT")
            vcp1 = c.op("V", lambda e: e.tensor_copy(rs_stage[:], ps_dots[:, :]),
                        waits=[("sT", tb2), ("dDg", c.c("dDg"))], inc="sV")
            c.dma("G", rs_bc_scr.ap()[:, ocb * 512:(ocb + 1) * 512], rs_stage[:],
                  "dDg", waits=[("sV", vcp1)])
        wt1w = [("dDg", c.c("dDg")), ("sV", c.c("sV")), ("sA", c.c("sA")),
                ("sT", c.c("sT"))]
        for ch in range(32):
            c.op("Y", lambda e, ch=ch: e.dma_start(
                out=wT_sb[1][:, ch, :],
                in_=w_bf_scr.ap()[512:1024, ch * 128:(ch + 1) * 128],
                transpose=True),
                waits=wt1w if ch == 0 else [], inc="dWT1", inc_n=16)

        d_last_mm = {}
        dslot_v = {}
        x1b_free = {}
        outst_free = {}
        rs_used = {}

        def load_wT(oc, queue):
            par = oc % 2
            wY = [("dDg", restage_done)]
            if oc >= 2:
                wY.append(("sT", d_last_mm[oc - 2]))
            for ch in range(32):
                c.op(queue, lambda e, oc=oc, ch=ch, par=par: e.dma_start(
                    out=wT_sb[par][:, ch, :],
                    in_=w_bf_scr.ap()[oc * 512:(oc + 1) * 512, ch * 128:(ch + 1) * 128],
                    transpose=True),
                    waits=wY if ch == 0 else [], inc=f"dWT{par}", inc_n=16)

        def load_rs(oc):
            par = oc % 2
            wG = [("dDg", c.c("dDg"))]
            if par in rs_used:
                wG.append(("sV", rs_used[par]))
            if oc >= 2:
                wG.append(("sT", d_last_mm[oc - 2]))
            c.dma("G", rs_bc[par][:], rs_bc_scr.ap()[:, oc * 512:(oc + 1) * 512],
                  f"dWF{par}", waits=wG)
            c.dma("G", biasb_oc[par][:], ffn_b[0:1, oc * 512:(oc + 1) * 512],
                  f"dWF{par}")

        load_rs(0)
        load_rs(1)
        for oc in range(8):
            par = oc % 2
            if oc >= 1 and oc + 1 < 8:
                load_wT(oc + 1, "Y")
            wt_ready = c.c(f"dWT{par}")
            for t in range(nT):
                slot = (oc * nT + t) % 2
                sslot = (oc * nT + t) % 2
                wmm = []
                if t == 0:
                    wmm.append((f"dWT{par}", wt_ready))
                    wmm.append((f"dWF{par}", c.c(f"dWF{par}")))
                if oc == 0:
                    wmm.extend([({"V": "sV", "A": "sA"}[lncp2[(t, ch)][0]],
                                 lncp2[(t, ch)][1]) for ch in (30, 31)])
                if slot in dslot_v:
                    wmm.append(("sV", dslot_v[slot]))
                for ch in range(32):
                    c.op("T", lambda e, ch=ch, t=t, slot=slot, par=par: e.matmul(
                        ps_d[slot][:],
                        ln2_fm[:, ch * TOKW + t * 128: ch * TOKW + (t + 1) * 128],
                        wT_sb[par][:, ch, :],
                        start=(ch == 0), stop=False),
                        waits=wmm if ch == 0 else [], inc=None)
                last = c.op("T", lambda e, par=par, slot=slot: e.matmul(
                    ps_d[slot][:], ones_sb[0:1, 0:128], biasb_oc[par][:],
                    start=False, stop=True), inc="sT")
                d_last_mm[oc] = last
                wx1 = [("sV", x1b_free[sslot])] if sslot in x1b_free else []
                c.dma("Y", x1s_stage[sslot][:],
                      x1_scr.ap()[t * 128:(t + 1) * 128, oc * 512:(oc + 1) * 512],
                      f"dX1B{sslot}", waits=wx1)
                x1_ready = c.c(f"dX1B{sslot}")
                wmsk = [("sT", last)]
                if sslot in outst_free:
                    wmsk.append((f"dOUT{sslot}", outst_free[sslot]))
                v0_ = c.op("V", lambda e, t=t, slot=slot, sslot=sslot, par=par: e.scalar_tensor_tensor(
                    out_stage[sslot][:], ps_d[slot][:], qs_all[:, t:t + 1],
                    rs_bc[par][:], ALU.mult, ALU.mult),
                    waits=wmsk, inc="sV")
                dslot_v[slot] = v0_
                rs_used[par] = v0_
                v1_ = c.op("V", lambda e, sslot=sslot: e.tensor_tensor(
                    out_stage[sslot][:], out_stage[sslot][:], x1s_stage[sslot][:],
                    ALU.add), waits=[("sV", v0_), (f"dX1B{sslot}", x1_ready)], inc="sV")
                x1b_free[sslot] = v1_
                do = c.dma("Y", out[t * 128:(t + 1) * 128, oc * 512:(oc + 1) * 512],
                           out_stage[sslot][:], f"dOUT{sslot}", waits=[("sV", v1_)])
                outst_free[sslot] = do
            if oc + 2 < 8:
                load_rs(oc + 2)
        c.wait("Y", ("dOUT0", c.c("dOUT0")), ("dOUT1", c.c("dOUT1")))
        c.barrier(self_waits={"Y": [("dOUT0", c.c("dOUT0")), ("dOUT1", c.c("dOUT1")),
                                    ("dX1B0", c.c("dX1B0")), ("dX1B1", c.c("dX1B1")),
                                    ("dWT0", c.c("dWT0")), ("dWT1", c.c("dWT1"))],
                              "G": [("dWF0", c.c("dWF0")), ("dWF1", c.c("dWF1"))]})

    with nc.Block() as block:
        @block.tensor
        def _(eng):
            emit(Ctx(nc, "T", engines, sems))

        @block.vector
        def _(eng):
            emit(Ctx(nc, "V", engines, sems))

        @block.scalar
        def _(eng):
            emit(Ctx(nc, "A", engines, sems))

        @block.gpsimd
        def _(eng):
            emit(Ctx(nc, "G", engines, sems))

        @block.sync
        def _(eng):
            emit(Ctx(nc, "Y", engines, sems))

    return nc


def kernel(L0_attn_hidden_states, L0_attn_w_qkv, L0_attn_w_o, L0_ffn_weight,
           L0_ffn_bias, L0_ffn_centroids, L0_ffn_assignments):
    x = np.ascontiguousarray(np.asarray(L0_attn_hidden_states, dtype=np.float32))
    wq = np.ascontiguousarray(np.asarray(L0_attn_w_qkv, dtype=np.float32))
    wo = np.ascontiguousarray(np.asarray(L0_attn_w_o, dtype=np.float32))
    fw = np.ascontiguousarray(np.asarray(L0_ffn_weight, dtype=np.float32))
    fb = np.ascontiguousarray(np.asarray(L0_ffn_bias, dtype=np.float32)).reshape(1, -1)
    ce = np.ascontiguousarray(np.asarray(L0_ffn_centroids, dtype=np.float32))
    asg = np.ascontiguousarray(np.asarray(L0_ffn_assignments, dtype=np.int32)).reshape(1, -1)

    B, S, H_ = x.shape
    assert B == N_CORES and H_ == H

    nc = build_nc(S=S)
    in_maps = [{"x0": x[b], "w_qkv": wq, "w_o": wo, "ffn_w": fw,
                "ffn_b": fb, "cent": ce, "assign": asg} for b in range(B)]
    res = run_bass_kernel_spmd(nc, in_maps, core_ids=list(range(N_CORES)))
    return np.stack([res.results[b]["out"] for b in range(B)], axis=0)


# version 14
# speedup vs baseline: 1.0491x; 1.0491x over previous
import math
from contextlib import ExitStack

import numpy as np
import ml_dtypes

import concourse.bass as bass
import concourse.mybir as mybir
from concourse.bass_utils import run_bass_kernel_spmd

F32 = mybir.dt.float32
BF16 = mybir.dt.bfloat16
I32 = mybir.dt.int32
AF = mybir.ActivationFunctionType
ALU = mybir.AluOpType
AX = mybir.AxisListType.X

H = 4096
QKV = 6144
NH, NKV, D = 32, 8, 128
G_HEADS = 4
NCLUST = 64
TEMP_INV = 10.0
THRESH = 0.01
EPS = 1e-5
SCALE = D ** -0.5
C_SHIFT = 20.0
BIG = 1.0e33

N_CORES = 8

ENGS = ("T", "V", "A", "G", "Y")


def _alibi_slopes(n):
    cp2 = 2 ** math.floor(math.log2(n))
    base = 2.0 ** (-8.0 / cp2)
    slopes = base ** np.arange(1, cp2 + 1, dtype=np.float32)
    if cp2 != n:
        extra_base = 2.0 ** (-4.0 / cp2)
        extra = extra_base ** np.arange(1, 1 + 2 * (n - cp2), 2, dtype=np.float32)
        slopes = np.concatenate([slopes, extra])
    return slopes.astype(np.float32)


SLOPES = _alibi_slopes(NH)


def _sd_all(nq):
    ntile = 4 * nq + 4
    out = np.full((128, ntile * 128), BIG, dtype=np.float32)
    qr = np.arange(128)
    kr = np.arange(128)[:, None]
    for didx in range(ntile):
        delta = didx - 3
        rel = delta * 128 + qr[None, :] - kr
        blk = np.where(rel >= 0, np.sqrt(np.maximum(rel, 0).astype(np.float64)), BIG)
        out[:, didx * 128:(didx + 1) * 128] = blk.astype(np.float32)
    return out



def _copy(e, dst, src):
    if isinstance(e, bass.BassScalarEngine):
        return e.copy(dst, src)
    return e.tensor_copy(dst, src)

class Ctx:

    def __init__(self, nc, me, engines, sems):
        self.nc = nc
        self.me = me
        self.engines = engines
        self.sems = sems
        self.cnt = {k: 0 for k in sems}
        self.bar_k = 0

    def c(self, s):
        return self.cnt[s]

    def op(self, eng, fn, waits=(), inc=None, inc_n=1):
        if self.me == eng:
            e = self.engines[eng]
            for sname, val in waits:
                if val > 0:
                    e.wait_ge(self.sems[sname], val)
            inst = fn(e)
            if inc is not None:
                inst.then_inc(self.sems[inc], inc_n)
        if inc is not None:
            self.cnt[inc] += inc_n
        return self.cnt[inc] if inc is not None else None

    def wait(self, eng, *waits):
        if self.me == eng:
            e = self.engines[eng]
            for sname, val in waits:
                if val > 0:
                    e.wait_ge(self.sems[sname], val)

    def dma(self, eng, out, in_, sem, waits=()):
        return self.op(eng, lambda e: e.dma_start(out=out, in_=in_),
                       waits=waits, inc=sem, inc_n=16)

    def barrier(self, self_waits=None):
        self.bar_k += 1
        target = 5 * self.bar_k
        own = {"T": "sT", "V": "sV", "A": "sA", "G": "sG"}
        for eng in ENGS:
            w = list((self_waits or {}).get(eng, ()))
            if eng in own and self.cnt[own[eng]] > 0:
                w.append((own[eng], self.cnt[own[eng]]))
            self.op(eng, lambda e: e.nop(), waits=w, inc="bar")
        for eng in ENGS:
            self.wait(eng, ("bar", target))


def build_nc(S=2048):
    nT = S // 128
    nQ = S // 512
    TOKW = S
    ND = 4 * nQ + 4

    nc = bass.Bass("TRN2", target_bir_lowering=False)

    x0 = nc.dram_tensor("x0", [S, H], F32, kind="ExternalInput").ap()
    w_qkv = nc.dram_tensor("w_qkv", [H, QKV], F32, kind="ExternalInput").ap()
    w_o = nc.dram_tensor("w_o", [H, H], F32, kind="ExternalInput").ap()
    ffn_w = nc.dram_tensor("ffn_w", [H, H], F32, kind="ExternalInput").ap()
    ffn_b = nc.dram_tensor("ffn_b", [1, H], F32, kind="ExternalInput").ap()
    cent = nc.dram_tensor("cent", [NCLUST, H], F32, kind="ExternalInput").ap()
    assign = nc.dram_tensor("assign", [1, H], I32, kind="ExternalInput").ap()
    out = nc.dram_tensor("out", [S, H], F32, kind="ExternalOutput").ap()

    wqkv_r = w_qkv.rearrange("(c p) n -> p c n", p=128)
    wo_r = w_o.rearrange("(c p) n -> p c n", p=128)

    q_scr = nc.dram_tensor("q_scr", [NH, 128, S], BF16)
    k_scr = nc.dram_tensor("k_scr", [NKV, 128, S], BF16)
    v_scr = nc.dram_tensor("v_scr", [NKV, 128, S], BF16)
    x1_scr = nc.dram_tensor("x1_scr", [S, H], F32)
    w_bf_scr = nc.dram_tensor("w_bf_scr", [H, H], BF16)
    rs_bc_scr = nc.dram_tensor("rs_bc_scr", [128, H], BF16)
    cc_in = nc.dram_tensor("cc_in", [NCLUST, 1], F32)
    cc_out = nc.dram_tensor("cc_out", [NCLUST, 1], F32)

    ident_bf = nc.inline_tensor(np.eye(128, dtype=ml_dtypes.bfloat16), "ident_bf")
    ident_f = nc.inline_tensor(np.eye(128, dtype=np.float32), "ident_f")
    sd_const = nc.inline_tensor(_sd_all(nQ), "sd_const")
    ones_const = nc.inline_tensor(np.ones((1, 128), dtype=ml_dtypes.bfloat16), "ones_const")
    iota_const = nc.inline_tensor(
        np.arange(NCLUST, dtype=np.float32).reshape(NCLUST, 1), "iota_const")
    onescol_const = nc.inline_tensor(np.ones((128, 1), dtype=ml_dtypes.bfloat16), "onescol_const")
    onesf_const = nc.inline_tensor(np.ones((1, 128), dtype=np.float32), "onesf_const")

    ident_sb = nc.alloc_sbuf_tensor("ident_sb", [128, 128], BF16)
    ones_sb = nc.alloc_sbuf_tensor("ones_sb", [1, 128], BF16)
    iota_sb = nc.alloc_sbuf_tensor("iota_sb", [NCLUST, 1], F32)
    qs_all = nc.alloc_sbuf_tensor("qs_all", [128, nT], F32)
    lnstat = nc.alloc_sbuf_tensor("lnstat", [128, 8], F32)
    eps_sb = nc.alloc_sbuf_tensor("eps_sb", [128, 1], F32)
    onescol_sb = nc.alloc_sbuf_tensor("onescol_sb", [128, 1], BF16)
    onesf_sb = nc.alloc_sbuf_tensor("onesf_sb", [1, 128], F32)

    esA = ExitStack()
    ln1_fm = esA.enter_context(nc.sbuf_tensor("ln1_fm", [128, 32 * TOKW], BF16))
    xbuf = [esA.enter_context(nc.sbuf_tensor(f"xbuf{i}", [128, H], F32)) for i in range(2)]
    ln1_tm = [esA.enter_context(nc.sbuf_tensor(f"ln1_tm{i}", [128, H], BF16)) for i in range(2)]
    w_sbA = [esA.enter_context(nc.sbuf_tensor(f"w_sbA{i}", [128, 32, 128], BF16)) for i in range(2)]
    qkv_stage = [esA.enter_context(nc.sbuf_tensor(f"qkv_st{i}", [128, 512], BF16)) for i in range(2)]
    vstage = esA.enter_context(nc.sbuf_tensor("vstage", [128, 512], BF16))
    vtr_stage = [esA.enter_context(nc.sbuf_tensor(f"vtr_st{i}", [128, 512], BF16)) for i in range(2)]
    psA = ExitStack()
    ps_tr = [psA.enter_context(nc.psum_tensor(f"ps_tr{i}", [128, 128], BF16)) for i in range(4)]
    ps_qkv = [psA.enter_context(nc.psum_tensor(f"ps_qkv{i}", [128, 512], F32)) for i in range(4)]
    psA.close()
    esA.close()

    esBC = ExitStack()
    attn_fm = esBC.enter_context(nc.sbuf_tensor("attn_fm", [128, 32 * TOKW], BF16))
    esB = ExitStack()
    sd_sb = esB.enter_context(nc.sbuf_tensor("sd_sb", [128, ND * 128], F32))
    k_sb = [esB.enter_context(nc.sbuf_tensor(f"k_sb{i}", [128, S], BF16)) for i in range(2)]
    v_sb = [esB.enter_context(nc.sbuf_tensor(f"v_sb{i}", [128, S], BF16)) for i in range(2)]
    p_all = esB.enter_context(nc.sbuf_tensor("p_all", [128, nT * 512], BF16))
    q_tile = [esB.enter_context(nc.sbuf_tensor(f"q_tile{i}", [128, 512], BF16)) for i in range(2)]
    rec_row = esB.enter_context(nc.sbuf_tensor("rec_row", [1, 512], F32))
    rec_bc = [esB.enter_context(nc.sbuf_tensor(f"rec_bc{i}", [128, 512], F32)) for i in range(2)]
    psB = ExitStack()
    ps_s = [psB.enter_context(nc.psum_tensor(f"ps_s{i}", [128, 512], F32)) for i in range(4)]
    ps_avf = [psB.enter_context(nc.psum_tensor(f"ps_avf{i}", [128, 512], F32)) for i in range(2)]
    ps_dbc = [psB.enter_context(nc.psum_tensor(f"ps_dbc{i}", [128, 512], F32)) for i in range(2)]
    psB.close()
    esB.close()
    esC = ExitStack()
    wo_sb = [esC.enter_context(nc.sbuf_tensor(f"wo_sb{i}", [128, 32, 512], BF16)) for i in range(2)]
    x0_stage = [esC.enter_context(nc.sbuf_tensor(f"x0_st{i}", [128, 512], F32)) for i in range(2)]
    x1_stage = [esC.enter_context(nc.sbuf_tensor(f"x1_st{i}", [128, 512], F32)) for i in range(2)]
    psC = ExitStack()
    ps_x1 = [psC.enter_context(nc.psum_tensor(f"ps_x1{i}", [128, 512], F32)) for i in range(3)]
    psC.close()
    esC.close()
    esBC.close()

    esR = ExitStack()
    cent_T = esR.enter_context(nc.sbuf_tensor("cent_T", [128, 32 * NCLUST], BF16, side="right"))
    a1_sb = esR.enter_context(nc.sbuf_tensor("a1_sb", [NCLUST, H], BF16, side="right"))
    esD0 = ExitStack()
    cent_sb = esD0.enter_context(nc.sbuf_tensor("cent_sb", [NCLUST, H], BF16))
    asg_i = esD0.enter_context(nc.sbuf_tensor("asg_i", [1, H], I32))
    asg_f = esD0.enter_context(nc.sbuf_tensor("asg_f", [1, H], F32))
    asg_bf = esD0.enter_context(nc.sbuf_tensor("asg_bf", [1, H], BF16))
    esD0.close()
    esD = ExitStack()
    ln2_fm = esD.enter_context(nc.sbuf_tensor("ln2_fm", [128, 32 * TOKW], BF16))
    esD1 = ExitStack()
    x1buf = [esD1.enter_context(nc.sbuf_tensor(f"x1buf{i}", [128, H], F32)) for i in range(2)]
    ln2_tm = [esD1.enter_context(nc.sbuf_tensor(f"ln2_tm{i}", [128, H], BF16)) for i in range(2)]
    identf_sb = esD1.enter_context(nc.sbuf_tensor("identf_sb", [128, 128], F32))
    dots_e = esD1.enter_context(nc.sbuf_tensor("dots_e", [128, NCLUST], F32))
    dots_n = esD1.enter_context(nc.sbuf_tensor("dots_n", [128, NCLUST], F32))
    dstat = esD1.enter_context(nc.sbuf_tensor("dstat", [128, 8], F32))
    cmax_t = esD1.enter_context(nc.sbuf_tensor("cmax_t", [NCLUST, 1], F32))
    run_cmax = esD1.enter_context(nc.sbuf_tensor("run_cmax", [NCLUST, 1], F32))
    gmax_sb = esD1.enter_context(nc.sbuf_tensor("gmax_sb", [NCLUST, 1], F32))
    cs_f = esD1.enter_context(nc.sbuf_tensor("cs_f", [NCLUST, 1], F32))
    cs_bf = esD1.enter_context(nc.sbuf_tensor("cs_bf", [NCLUST, 1], BF16))
    rs_row_bf = esD1.enter_context(nc.sbuf_tensor("rs_row_bf", [1, 512], BF16))
    rs_stage = esD1.enter_context(nc.sbuf_tensor("rs_stage", [128, 512], BF16))
    esD1.close()
    esR.close()
    esD3 = ExitStack()
    wT_sb = [esD3.enter_context(nc.sbuf_tensor(f"wT_sb{i}", [128, 32, 512], BF16)) for i in range(2)]
    rs_bc = [esD3.enter_context(nc.sbuf_tensor(f"rs_bc{i}", [128, 512], BF16)) for i in range(2)]
    biasb_oc = [esD3.enter_context(nc.sbuf_tensor(f"biasb_oc{i}", [1, 512], BF16)) for i in range(2)]
    x1s_stage = [esD3.enter_context(nc.sbuf_tensor(f"x1s_st{i}", [128, 512], F32)) for i in range(2)]
    out_stage = [esD3.enter_context(nc.sbuf_tensor(f"out_st{i}", [128, 512], F32)) for i in range(2)]
    esD3.close()
    esD.close()
    psD = ExitStack()
    ps_trD = [psD.enter_context(nc.psum_tensor(f"ps_trD{i}", [128, 128], BF16)) for i in range(4)]
    ps_dots = psD.enter_context(nc.psum_tensor("ps_dots", [128, 512], F32))
    ps_dt = psD.enter_context(nc.psum_tensor("ps_dt", [NCLUST, 128], F32))
    ps_d = [psD.enter_context(nc.psum_tensor(f"ps_d{i}", [128, 512], F32)) for i in range(2)]
    psD.close()

    sem_names = ["bar", "sT", "sV", "sA", "sG", "dP", "dD", "dCC",
                 "dDg", "dXA0", "dXA1", "dW0", "dW1", "dSQ0", "dSQ1", "dSV0", "dSV1",
                 "dKV0", "dKV1", "dQL0", "dQL1", "dWO0", "dWO1", "dX00", "dX01",
                 "dX1S0", "dX1S1", "dX1L0", "dX1L1", "dWF0", "dWF1", "dWT0", "dWT1",
                 "dX1B0", "dX1B1", "dOUT0", "dOUT1"]
    sems = {n: nc.alloc_semaphore(name=n) for n in sem_names}

    engines = {"T": nc.tensor, "V": nc.vector, "A": nc.scalar,
               "G": nc.gpsimd, "Y": nc.sync}

    def emit(c: Ctx):
        c.dma("Y", ident_sb[:], ident_bf.ap(), "dP")
        c.dma("Y", ones_sb[:], ones_const.ap(), "dP")
        c.dma("Y", iota_sb[:], iota_const.ap(), "dP")
        c.dma("Y", onescol_sb[:], onescol_const.ap(), "dP")
        c.dma("Y", onesf_sb[:], onesf_const.ap(), "dP")
        c.op("V", lambda e: e.memset(eps_sb[:], EPS), inc="sV")
        p0_done = {"Y": [("dP", c.c("dP"))]}
        c.barrier(self_waits=p0_done)

        ln_free_v = {}
        ln_free_a = {}
        lncp = {}
        tr_mark = {}
        var_mark = {}
        for t in range(nT):
            w = []
            if t >= 2:
                w = [("sV", ln_free_v[t - 2]), ("sA", ln_free_a[t - 2])]
            xa = c.dma("Y", xbuf[t % 2][:], x0[t * 128:(t + 1) * 128, :],
                       f"dXA{t % 2}", waits=w)
            xb = xbuf[t % 2]
            lt = ln1_tm[t % 2]
            m0 = c.op("V", lambda e, xb=xb: e.reduce_sum(lnstat[:, 0:1], xb[:], axis=AX),
                      waits=[(f"dXA{t % 2}", xa)], inc="sV")
            wsq = [(f"dXA{t % 2}", xa)]
            if t >= 1:
                wsq.append(("sV", var_mark[t - 1]))
            if t >= 2:
                wsq.append(("sT", tr_mark[(t - 2, 31)]))
            a0 = c.op("A", lambda e, xb=xb, lt=lt: e.activation(
                out=lt[:], in_=xb[:], func=AF.Square, accum_out=lnstat[:, 2:3]),
                waits=wsq, inc="sA")
            ln_free_a[t] = a0
            m1 = c.op("V", lambda e: e.tensor_scalar(lnstat[:, 1:2], lnstat[:, 0:1],
                                                     1.0 / H, None, ALU.mult),
                      waits=[("sV", m0)], inc="sV")
            m1b = c.op("V", lambda e: e.tensor_scalar(lnstat[:, 5:6], lnstat[:, 1:2],
                                                      lnstat[:, 1:2], None, ALU.mult),
                      waits=[("sV", m1)], inc="sV")
            m1c = c.op("V", lambda e: e.tensor_scalar(lnstat[:, 6:7], lnstat[:, 2:3],
                                                      1.0 / H, lnstat[:, 5:6],
                                                      ALU.mult, ALU.subtract),
                      waits=[("sV", m1b), ("sA", a0)], inc="sV")
            var_mark[t] = m1c
            a1 = c.op("A", lambda e: e.activation(out=lnstat[:, 3:4], in_=lnstat[:, 6:7],
                                                  func=AF.Ln, bias=eps_sb[:]),
                      waits=[("sV", m1c)], inc="sA")
            a2 = c.op("A", lambda e: e.activation(out=lnstat[:, 4:5], in_=lnstat[:, 3:4],
                                                  func=AF.Exp, scale=-0.5),
                      waits=[("sA", a1)], inc="sA")
            m3 = c.op("V", lambda e, xb=xb, lt=lt: e.tensor_scalar(
                lt[:], xb[:], lnstat[:, 1:2], lnstat[:, 4:5], ALU.subtract, ALU.mult),
                waits=[("sA", a2)], inc="sV")
            ln_free_v[t] = m3
            for ch in range(32):
                wt = [("sV", m3)] if ch == 0 else []
                if (t, ch - 4) in lncp:
                    e_, cnt_ = lncp[(t, ch - 4)]
                    wt.append(("sV" if e_ == "V" else "sA", cnt_))
                elif t >= 1 and ch < 4:
                    e_, cnt_ = lncp[(t - 1, 28 + ch)]
                    wt.append(("sV" if e_ == "V" else "sA", cnt_))
                tm = c.op("T", lambda e, ch=ch, lt=lt: e.transpose(
                    ps_tr[ch % 4][:], lt[:, ch * 128:(ch + 1) * 128], ident_sb[:]),
                    waits=wt, inc="sT")
                tr_mark[(t, ch)] = tm
                ceng = "V" if (ch % 2 == 0) else "A"
                dst = ln1_fm[:, ch * TOKW + t * 128: ch * TOKW + (t + 1) * 128]
                cm = c.op(ceng, lambda e, dst=dst, ch=ch: _copy(e, dst, ps_tr[ch % 4][:]),
                          waits=[("sT", tm)], inc="sV" if ceng == "V" else "sA")
                lncp[(t, ch)] = (ceng, cm)

        NJG = QKV // 128
        grp_last_mm = {}
        slot_free = {}
        stage_free = {}
        vtr_free = {}
        ps_tr_cons = {sub: lncp[(nT - 1, 28 + sub)] for sub in range(4)}
        ln_chunk_ready = {n: [({"V": "sV", "A": "sA"}[lncp[(4 * n + 3, ch)][0]],
                               lncp[(4 * n + 3, ch)][1]) for ch in (30, 31)]
                          for n in range(nQ)}
        for jg in range(NJG):
            wv = []
            if jg >= 2:
                wv.append(("sT", grp_last_mm[jg - 2]))
            wld = c.dma("G", w_sbA[jg % 2][:], wqkv_r[:, :, jg * 128:(jg + 1) * 128],
                        f"dW{jg % 2}", waits=wv)
            if jg < 32:
                c.dma("G", w_bf_scr.ap()[jg * 128:(jg + 1) * 128, :],
                      ffn_w[jg * 128:(jg + 1) * 128, :], "dDg")
            j_abs = jg
            for n in range(nQ):
                slot = n % 4
                wmm = [(f"dW{jg % 2}", wld)]
                if jg < 2:
                    wmm.extend(ln_chunk_ready[n])
                if slot in slot_free:
                    e_, cnt_ = slot_free[slot]
                    wmm.append(({"V": "sV", "A": "sA", "T": "sT"}[e_], cnt_))
                last = None
                for ch in range(32):
                    last = c.op("T", lambda e, ch=ch, n=n, jg=jg, slot=slot: e.matmul(
                        ps_qkv[slot][:],
                        w_sbA[jg % 2][:, ch, :],
                        ln1_fm[:, ch * TOKW + n * 512: ch * TOKW + (n + 1) * 512],
                        start=(ch == 0), stop=(ch == 31)),
                        waits=wmm if ch == 0 else [],
                        inc="sT" if ch == 31 else None)
                grp_last_mm[jg] = last
                if j_abs < NH:
                    st = n % 2
                    wep = [("sT", last)]
                    if st in stage_free:
                        wep.append((f"dSQ{st}", stage_free[st]))
                    acp = c.op("A", lambda e, st=st, slot=slot: e.activation(
                        out=qkv_stage[st][:], in_=ps_qkv[slot][:], func=AF.Copy,
                        scale=SCALE),
                        waits=wep, inc="sA")
                    slot_free[slot] = ("A", acp)
                    dsp = c.dma("Y", q_scr.ap()[j_abs, :, n * 512:(n + 1) * 512],
                                qkv_stage[st][:], f"dSQ{st}", waits=[("sA", acp)])
                    stage_free[st] = dsp
                elif j_abs < NH + NKV:
                    hk = j_abs - NH
                    st = n % 2
                    wep = [("sT", last)]
                    if st in stage_free:
                        wep.append((f"dSQ{st}", stage_free[st]))
                    acp = c.op("A", lambda e, st=st, slot=slot: e.activation(
                        out=qkv_stage[st][:], in_=ps_qkv[slot][:], func=AF.Copy),
                        waits=wep, inc="sA")
                    slot_free[slot] = ("A", acp)
                    dsp = c.dma("Y", k_scr.ap()[hk, :, n * 512:(n + 1) * 512],
                                qkv_stage[st][:], f"dSQ{st}", waits=[("sA", acp)])
                    stage_free[st] = dsp
                else:
                    hk = j_abs - NH - NKV
                    wep = [("sT", last)]
                    if "vstage" in slot_free:
                        wep.append(("sT", slot_free["vstage"][1]))
                    vcp = c.op("A", lambda e, slot=slot: e.activation(
                        out=vstage[:], in_=ps_qkv[slot][:], func=AF.Copy),
                        waits=wep, inc="sA")
                    slot_free[slot] = ("A", vcp)
                    vslot = n % 2
                    trs = []
                    for sub in range(4):
                        wtr = [("sA", vcp)] if sub == 0 else []
                        if sub in ps_tr_cons:
                            e_, cnt_ = ps_tr_cons[sub]
                            wtr = wtr + [("sV" if e_ == "V" else "sA", cnt_)]
                        tm = c.op("T", lambda e, sub=sub: e.transpose(
                            ps_tr[sub % 4][:], vstage[:, sub * 128:(sub + 1) * 128],
                            ident_sb[:]), waits=wtr, inc="sT")
                        trs.append(tm)
                    slot_free["vstage"] = ("T", trs[-1])
                    wcp_extra = [(f"dSV{vslot}", vtr_free[vslot])] if vslot in vtr_free else []
                    vcps = []
                    for sub in range(4):
                        ceng = "V" if sub % 2 == 0 else "A"
                        cm = c.op(ceng, lambda e, sub=sub, vslot=vslot: _copy(e, vtr_stage[vslot][:, sub * 128:(sub + 1) * 128], ps_tr[sub % 4][:]),
                            waits=[("sT", trs[sub])] + (wcp_extra if sub < 2 else []),
                            inc="sV" if ceng == "V" else "sA")
                        vcps.append((ceng, cm))
                        ps_tr_cons[sub] = (ceng, cm)
                    wd = [({"V": "sV", "A": "sA"}[e_], cnt_) for e_, cnt_ in vcps[-2:]]
                    dsp = c.dma("Y", v_scr.ap()[hk, :, n * 512:(n + 1) * 512],
                                vtr_stage[vslot][:], f"dSV{vslot}", waits=wd)
                    vtr_free[vslot] = dsp
        c.barrier(self_waits={"G": [("dW0", c.c("dW0")), ("dW1", c.c("dW1"))],
                              "Y": [("dSQ0", c.c("dSQ0")), ("dSQ1", c.c("dSQ1")),
                                    ("dSV0", c.c("dSV0")), ("dSV1", c.c("dSV1"))]})

        c.dma("Y", sd_sb[:], sd_const.ap(), "dP")
        sd_ready = c.c("dP")

        kv_last_use = {}
        q_last = {}
        avf_free = {}
        den_free = {}
        prev_exp = {}
        pall_av_last = None
        pend_ep = [None]

        def flush_ep1():
            ep = pend_ep[0]
            if ep is None:
                return
            hp, np_, par, avf_stop, den_stop = ep
            ra = c.op("A", lambda e, par=par: e.activation(
                out=rec_row[:], in_=ps_dbc[par][0:1, :], func=AF.Ln),
                waits=[("sT", den_stop)], inc="sA")
            r0 = c.op("A", lambda e: e.activation(
                out=rec_row[:], in_=rec_row[:], func=AF.Exp, scale=-1.0),
                waits=[("sA", ra)], inc="sA")
            bc = c.op("T", lambda e, par=par: e.matmul(
                ps_dbc[par][:, :], onesf_sb[:], rec_row[:], start=True, stop=True),
                waits=[("sA", r0)], inc="sT")
            pend_ep[0] = (hp, np_, par, avf_stop, bc)

        def flush_ep2():
            ep = pend_ep[0]
            if ep is None:
                return
            pend_ep[0] = None
            hp, np_, par, avf_stop, bc = ep
            wcp = [("sT", bc)]
            if par in avf_free:
                wcp.append(("sV", avf_free[par]))
            cpm = c.op("A", lambda e, par=par: _copy(e, rec_bc[par][:], ps_dbc[par][:, :]),
                       waits=wcp, inc="sA")
            den_free[par] = ("sA", cpm)
            dst = attn_fm[:, hp * TOKW + np_ * 512: hp * TOKW + (np_ + 1) * 512]
            tt = c.op("V", lambda e, dst=dst, par=par: e.tensor_tensor(
                dst, ps_avf[par][:], rec_bc[par][:], ALU.mult),
                waits=[("sT", avf_stop), ("sA", cpm)], inc="sV")
            avf_free[par] = tt

        chunks = [(hk, g, n) for hk in range(NKV) for g in range(G_HEADS)
                  for n in range(nQ)]
        q_rdy = {}

        def q_load(ci):
            hk, g, n = chunks[ci]
            h = hk * G_HEADS + g
            qslot = ci % 2
            wq = []
            if qslot in q_last:
                wq.append(("sT", q_last[qslot]))
            c.dma("Y", q_tile[qslot][:], q_scr.ap()[h, :, n * 512:(n + 1) * 512],
                  f"dQL{qslot}", waits=wq)
            q_rdy[ci] = c.c(f"dQL{qslot}")

        kv_rdy = {}
        for ci, (hk, g, n) in enumerate(chunks):
            if g == 0 and n == 0:
                wkv = []
                if hk >= 2:
                    wkv.append(("sT", kv_last_use[hk % 2]))
                c.dma("Y", k_sb[hk % 2][:], k_scr.ap()[hk, :, :], f"dKV{hk % 2}",
                      waits=wkv)
                c.dma("Y", v_sb[hk % 2][:], v_scr.ap()[hk, :, :], f"dKV{hk % 2}")
                kv_rdy[hk] = c.c(f"dKV{hk % 2}")
            kv_sem = f"dKV{hk % 2}"
            kv_ready = kv_rdy[hk]
            if ci == 0:
                q_load(0)
            h = hk * G_HEADS + g
            slope = -float(SLOPES[h])
            qslot = ci % 2
            q_sem = f"dQL{qslot}"
            q_ready = q_rdy[ci]
            par = ci % 2
            nkt = 4 * n + 4

            def score(kt):
                wsc = [(kv_sem, kv_ready), (q_sem, q_ready)]
                if (kt % 4) in prev_exp:
                    wsc.append(("sA", prev_exp[kt % 4]))
                sm = c.op("T", lambda e, kt=kt, hk=hk, qslot=qslot: e.matmul(
                    ps_s[kt % 4][:], k_sb[hk % 2][:, kt * 128:(kt + 1) * 128],
                    q_tile[qslot][:], start=True, stop=True),
                    waits=wsc, inc="sT")
                q_last[qslot] = sm
                kv_last_use[hk % 2] = sm
                return sm

            sms = {}
            for kt0 in range(min(3, nkt)):
                sms[kt0] = score(kt0)
            for kt in range(nkt):
                if kt + 3 < nkt:
                    sms[kt + 3] = score(kt + 3)
                didx = 4 * n - kt + 3
                wst = [("sT", sms[kt])]
                if kt == 0:
                    wst.append(("dP", sd_ready))
                tt = c.op("V", lambda e, kt=kt, didx=didx, slope=slope: e.scalar_tensor_tensor(
                    ps_s[kt % 4][:], sd_sb[:, didx * 128: didx * 128 + 512],
                    slope, ps_s[kt % 4][:], ALU.mult, ALU.add),
                    waits=wst, inc="sV")
                wex = [("sV", tt)]
                if kt == 0 and pall_av_last is not None:
                    wex.append(("sT", pall_av_last))
                em = c.op("A", lambda e, kt=kt: e.activation(
                    out=p_all[:, kt * 512:(kt + 1) * 512], in_=ps_s[kt % 4][:],
                    func=AF.Exp), waits=wex, inc="sA")
                prev_exp[kt % 4] = em
                if kt == 0 and ci + 1 < len(chunks):
                    q_load(ci + 1)
                if kt == 1:
                    flush_ep1()
                if kt == 3:
                    flush_ep2()
                wav = [("sA", em)]
                if kt == 0 and par in avf_free:
                    wav.append(("sV", avf_free[par]))
                c.op("T", lambda e, kt=kt, hk=hk, par=par, nkt=nkt: e.matmul(
                    ps_avf[par][:], v_sb[hk % 2][:, kt * 128:(kt + 1) * 128],
                    p_all[:, kt * 512:(kt + 1) * 512],
                    start=(kt == 0), stop=(kt == nkt - 1)),
                    waits=wav, inc="sT")
                avf_stop = c.c("sT")
                wdn = []
                if kt == 0 and par in den_free:
                    wdn.append(den_free[par])
                c.op("T", lambda e, kt=kt, par=par, nkt=nkt: e.matmul(
                    ps_dbc[par][0:1, :], onescol_sb[:], p_all[:, kt * 512:(kt + 1) * 512],
                    start=(kt == 0), stop=(kt == nkt - 1)),
                    waits=wdn, inc="sT")
                den_stop = c.c("sT")
                kv_last_use[hk % 2] = c.c("sT")
                if kt == nkt - 1:
                    pall_av_last = c.c("sT")
                    pend_ep[0] = (h, n, par, avf_stop, den_stop)
            if nkt == 4:
                pass
        flush_ep1()
        flush_ep2()
        c.barrier(self_waits={"Y": [("dKV0", c.c("dKV0")), ("dKV1", c.c("dKV1")),
                                    ("dQL0", c.c("dQL0")), ("dQL1", c.c("dQL1"))]})

        wo_last_mm = {}
        cslot_tt = {}
        x0s_free = {}
        x1s_free = {}
        x0_rdy = {}

        def c_x0(oc, t):
            sslot = (oc * nT + t) % 2
            wx0 = [("sV", x0s_free[sslot])] if sslot in x0s_free else []
            c.dma("Y", x0_stage[sslot][:],
                  x0[t * 128:(t + 1) * 128, oc * 512:(oc + 1) * 512], f"dX0{sslot}",
                  waits=wx0)
            x0_rdy[(oc, t)] = c.c(f"dX0{sslot}")

        for oc in range(8):
            wv = []
            if oc >= 2:
                wv.append(("sT", wo_last_mm[oc - 2]))
            wold = c.dma("G", wo_sb[oc % 2][:], wo_r[:, :, oc * 512:(oc + 1) * 512],
                         f"dWO{oc % 2}", waits=wv)
            for t in range(nT):
                slot = (oc * nT + t) % 3
                sslot = (oc * nT + t) % 2
                if oc == 0 and t == 0:
                    c_x0(0, 0)
                wmm = [(f"dWO{oc % 2}", wold)]
                if slot in cslot_tt:
                    wmm.append(("sV", cslot_tt[slot]))
                last = None
                for ch in range(32):
                    last = c.op("T", lambda e, ch=ch, oc=oc, t=t, slot=slot: e.matmul(
                        ps_x1[slot][:],
                        attn_fm[:, ch * TOKW + t * 128: ch * TOKW + (t + 1) * 128],
                        wo_sb[oc % 2][:, ch, :],
                        start=(ch == 0), stop=(ch == 31)),
                        waits=wmm if ch == 0 else [],
                        inc="sT" if ch == 31 else None)
                wo_last_mm[oc] = last
                if t + 1 < nT:
                    c_x0(oc, t + 1)
                elif oc + 1 < 8:
                    c_x0(oc + 1, 0)
                wtt = [("sT", last), (f"dX0{sslot}", x0_rdy[(oc, t)])]
                if sslot in x1s_free:
                    wtt.append((f"dX1S{sslot}", x1s_free[sslot]))
                vt = c.op("V", lambda e, slot=slot, sslot=sslot: e.tensor_tensor(
                    x1_stage[sslot][:], ps_x1[slot][:], x0_stage[sslot][:], ALU.add),
                    waits=wtt, inc="sV")
                cslot_tt[slot] = vt
                x0s_free[sslot] = vt
                d1 = c.dma("Y", x1_scr.ap()[t * 128:(t + 1) * 128, oc * 512:(oc + 1) * 512],
                           x1_stage[sslot][:], f"dX1S{sslot}", waits=[("sV", vt)])
                x1s_free[sslot] = d1
        c.barrier(self_waits={"G": [("dWO0", c.c("dWO0")), ("dWO1", c.c("dWO1")),
                                    ("dDg", c.c("dDg"))],
                              "Y": [("dX1S0", c.c("dX1S0")), ("dX1S1", c.c("dX1S1")),
                                    ("dX00", c.c("dX00")), ("dX01", c.c("dX01"))]})

        c.dma("G", cent_sb[:], cent, "dDg")
        c.dma("Y", asg_i[:], assign, "dD")
        dD0 = c.c("dD")
        dDg0 = c.c("dDg")
        v0 = c.op("V", lambda e: e.tensor_copy(asg_f[:], asg_i[:]),
                  waits=[("dD", dD0)], inc="sV")
        v1 = c.op("V", lambda e: e.tensor_copy(asg_bf[:], asg_f[:]),
                  waits=[("sV", v0)], inc="sV")
        for ocb in range(8):
            tb = c.op("T", lambda e, ocb=ocb: e.matmul(
                ps_d[0][:NCLUST, :], ones_sb[0:1, 0:NCLUST],
                asg_bf[0:1, ocb * 512:(ocb + 1) * 512], start=True, stop=True),
                waits=[("sV", v1)] if ocb == 0 else [("sV", c.c("sV"))], inc="sT")
            c.op("V", lambda e, ocb=ocb: e.tensor_scalar(
                a1_sb[:, ocb * 512:(ocb + 1) * 512], ps_d[0][:NCLUST, :],
                iota_sb[:], None, ALU.is_equal),
                waits=[("sT", tb)], inc="sV")
        ctp = {}
        for ch in range(32):
            wt = [("dDg", dDg0)] if ch == 0 else []
            if ch >= 4:
                e_, cnt_ = ctp[ch - 4]
                wt.append(("sA" if e_ == "A" else "sV", cnt_))
            tm = c.op("T", lambda e, ch=ch: e.transpose(
                ps_trD[ch % 4][:, 0:NCLUST],
                cent_sb[0:NCLUST, ch * 128:(ch + 1) * 128],
                ident_sb[0:NCLUST, 0:NCLUST]),
                waits=wt, inc="sT")
            ceng = "A" if ch % 2 == 0 else "V"
            cm = c.op(ceng, lambda e, ch=ch: _copy(e, cent_T[:, ch * NCLUST:(ch + 1) * NCLUST], ps_trD[ch % 4][:, 0:NCLUST]),
                waits=[("sT", tm)], inc="sA" if ceng == "A" else "sV")
            ctp[ch] = (ceng, cm)
        c.barrier(self_waits={"Y": [("dD", c.c("dD"))], "G": [("dDg", c.c("dDg"))]})

        c.dma("Y", identf_sb[:], ident_f.ap(), "dP")
        idf_ready = c.c("dP")
        restage_done = dDg0
        c.op("V", lambda e: e.memset(run_cmax[:], -1e30), inc="sV")
        ln_free_v.clear()
        ln_free_a.clear()
        var_mark.clear()
        lncp2 = {}
        tr_mark2 = {}
        dt_done = {}
        for t in range(nT):
            w = []
            if t >= 2:
                w = [("sV", ln_free_v[t - 2]), ("sA", ln_free_a[t - 2])]
            x1r = c.dma("Y", x1buf[t % 2][:], x1_scr.ap()[t * 128:(t + 1) * 128, :],
                        f"dX1L{t % 2}", waits=w)
            xb = x1buf[t % 2]
            lt = ln2_tm[t % 2]
            m0 = c.op("V", lambda e, xb=xb: e.reduce_sum(lnstat[:, 0:1], xb[:], axis=AX),
                      waits=[(f"dX1L{t % 2}", x1r)], inc="sV")
            wsq = [(f"dX1L{t % 2}", x1r)]
            if t >= 1:
                wsq.append(("sV", var_mark[t - 1]))
            if t >= 2:
                wsq.append(("sT", tr_mark2[(t - 2, 31)]))
            a0 = c.op("A", lambda e, xb=xb, lt=lt: e.activation(
                out=lt[:], in_=xb[:], func=AF.Square, accum_out=lnstat[:, 2:3]),
                waits=wsq, inc="sA")
            ln_free_a[t] = a0
            m1 = c.op("V", lambda e: e.tensor_scalar(lnstat[:, 1:2], lnstat[:, 0:1],
                                                     1.0 / H, None, ALU.mult),
                      waits=[("sV", m0)], inc="sV")
            m1b = c.op("V", lambda e: e.tensor_scalar(lnstat[:, 5:6], lnstat[:, 1:2],
                                                      lnstat[:, 1:2], None, ALU.mult),
                      waits=[("sV", m1)], inc="sV")
            m1c = c.op("V", lambda e: e.tensor_scalar(lnstat[:, 6:7], lnstat[:, 2:3],
                                                      1.0 / H, lnstat[:, 5:6],
                                                      ALU.mult, ALU.subtract),
                      waits=[("sV", m1b), ("sA", a0)], inc="sV")
            var_mark[t] = m1c
            a1 = c.op("A", lambda e: e.activation(out=lnstat[:, 3:4], in_=lnstat[:, 6:7],
                                                  func=AF.Ln, bias=eps_sb[:]),
                      waits=[("sV", m1c)], inc="sA")
            a2 = c.op("A", lambda e: e.activation(out=lnstat[:, 4:5], in_=lnstat[:, 3:4],
                                                  func=AF.Exp, scale=-0.5),
                      waits=[("sA", a1)], inc="sA")
            m3 = c.op("V", lambda e, xb=xb, lt=lt: e.tensor_scalar(
                lt[:], xb[:], lnstat[:, 1:2], lnstat[:, 4:5], ALU.subtract, ALU.mult),
                waits=[("sA", a2)], inc="sV")
            ln_free_v[t] = m3
            for ch in range(32):
                wt = [("sV", m3)] if ch == 0 else []
                if (t, ch - 4) in lncp2:
                    e_, cnt_ = lncp2[(t, ch - 4)]
                    wt.append(("sV" if e_ == "V" else "sA", cnt_))
                elif t >= 1 and ch < 4:
                    e_, cnt_ = lncp2[(t - 1, 28 + ch)]
                    wt.append(("sV" if e_ == "V" else "sA", cnt_))
                tm = c.op("T", lambda e, ch=ch, lt=lt: e.transpose(
                    ps_trD[ch % 4][:], lt[:, ch * 128:(ch + 1) * 128], ident_sb[:]),
                    waits=wt, inc="sT")
                tr_mark2[(t, ch)] = tm
                ceng = "V" if (ch % 2 == 0) else "A"
                dst = ln2_fm[:, ch * TOKW + t * 128: ch * TOKW + (t + 1) * 128]
                cm = c.op(ceng, lambda e, dst=dst, ch=ch: _copy(e, dst, ps_trD[ch % 4][:]),
                          waits=[("sT", tm)], inc="sV" if ceng == "V" else "sA")
                lncp2[(t, ch)] = (ceng, cm)
            wdots = [({"V": "sV", "A": "sA"}[lncp2[(t, ch)][0]], lncp2[(t, ch)][1])
                     for ch in (30, 31)]
            if t >= 1:
                wdots.append(("sV", dt_done[t - 1]))
            lastd = None
            for ch in range(32):
                lastd = c.op("T", lambda e, ch=ch, t=t: e.matmul(
                    ps_dots[:, 0:NCLUST], ln2_fm[:, ch * TOKW + t * 128: ch * TOKW + (t + 1) * 128],
                    cent_T[:, ch * NCLUST:(ch + 1) * NCLUST],
                    start=(ch == 0), stop=(ch == 31)),
                    waits=wdots if ch == 0 else [], inc="sT" if ch == 31 else None)
            d0 = c.op("V", lambda e: e.reduce_max(dstat[:, 0:1], ps_dots[:, 0:NCLUST], axis=AX),
                      waits=[("sT", lastd)], inc="sV")
            d1 = c.op("V", lambda e: e.tensor_scalar(dstat[:, 1:2], dstat[:, 0:1],
                                                     -TEMP_INV, None, ALU.mult),
                      waits=[("sV", d0)], inc="sV")
            d2 = c.op("A", lambda e: e.activation(out=dots_e[:], in_=ps_dots[:, 0:NCLUST], func=AF.Exp,
                                                  bias=dstat[:, 1:2], scale=TEMP_INV),
                      waits=[("sV", d1)], inc="sA")
            d3 = c.op("V", lambda e: e.reduce_sum(dstat[:, 2:3], dots_e[:], axis=AX),
                      waits=[("sA", d2)], inc="sV")
            d4 = c.op("V", lambda e: e.reciprocal(dstat[:, 3:4], dstat[:, 2:3]),
                      waits=[("sV", d3)], inc="sV")
            d5 = c.op("V", lambda e: e.tensor_scalar(dots_n[:], dots_e[:], dstat[:, 3:4],
                                                     None, ALU.mult),
                      waits=[("sV", d4)], inc="sV")
            d6 = c.op("V", lambda e: e.reduce_max(dstat[:, 4:5], dots_n[:], axis=AX),
                      waits=[("sV", d5)], inc="sV")
            c.op("V", lambda e, t=t: e.tensor_single_scalar(
                qs_all[:, t:t + 1], dstat[:, 4:5], THRESH, ALU.is_gt),
                waits=[("sV", d6)], inc="sV")
            td = c.op("T", lambda e: e.transpose(ps_dt[:], dots_n[:], identf_sb[:]),
                      waits=[("sV", d5), ("dP", idf_ready)], inc="sT")
            d8 = c.op("V", lambda e: e.reduce_max(cmax_t[:], ps_dt[:], axis=AX),
                      waits=[("sT", td)], inc="sV")
            d9 = c.op("V", lambda e: e.tensor_tensor(run_cmax[:], run_cmax[:], cmax_t[:],
                                                     ALU.max),
                      waits=[("sV", d8)], inc="sV")
            dt_done[t] = d9
        wt0w = [("dDg", restage_done),
                ("sV", ln_free_v[nT - 1]), ("sA", ln_free_a[nT - 1]),
                ("sV", ln_free_v[nT - 2]), ("sA", ln_free_a[nT - 2])]
        for ch in range(32):
            c.op("Y", lambda e, ch=ch: e.dma_start(
                out=wT_sb[0][:, ch, :],
                in_=w_bf_scr.ap()[0:512, ch * 128:(ch + 1) * 128],
                transpose=True),
                waits=wt0w if ch == 0 else [], inc="dWT0", inc_n=16)
        c.dma("G", cc_in.ap(), run_cmax[:], "dCC", waits=[("sV", dt_done[nT - 1])])
        ccm = c.op("G", lambda e: e.collective_compute(
            "AllReduce", ALU.max, replica_groups=[list(range(N_CORES))],
            ins=[cc_in.ap().opt()], outs=[cc_out.ap().opt()]),
            waits=[("dCC", c.c("dCC"))], inc="sG")
        c.dma("G", gmax_sb[:], cc_out.ap(), "dCC", waits=[("sG", ccm)])
        dcc2 = c.c("dCC")
        e0 = c.op("V", lambda e: e.tensor_single_scalar(cs_f[:], gmax_sb[:], THRESH,
                                                        ALU.is_gt),
                  waits=[("dCC", dcc2)], inc="sV")
        e1 = c.op("V", lambda e: e.tensor_copy(cs_bf[:], cs_f[:]),
                  waits=[("sV", e0)], inc="sV")
        for ocb in range(8):
            tb = c.op("T", lambda e, ocb=ocb: e.matmul(
                ps_dots[0:1, :], cs_bf[:], a1_sb[:, ocb * 512:(ocb + 1) * 512],
                start=True, stop=True),
                waits=[("sV", e1)] if ocb == 0 else [("sV", c.c("sV"))], inc="sT")
            vcp0 = c.op("V", lambda e: e.tensor_copy(rs_row_bf[:], ps_dots[0:1, :]),
                        waits=[("sT", tb)], inc="sV")
            tb2 = c.op("T", lambda e: e.matmul(
                ps_dots[:, :], ones_sb[0:1, :], rs_row_bf[:], start=True, stop=True),
                waits=[("sV", vcp0)], inc="sT")
            vcp1 = c.op("V", lambda e: e.tensor_copy(rs_stage[:], ps_dots[:, :]),
                        waits=[("sT", tb2), ("dDg", c.c("dDg"))], inc="sV")
            c.dma("G", rs_bc_scr.ap()[:, ocb * 512:(ocb + 1) * 512], rs_stage[:],
                  "dDg", waits=[("sV", vcp1)])
        wt1w = [("dDg", c.c("dDg")), ("sV", c.c("sV")), ("sA", c.c("sA")),
                ("sT", c.c("sT"))]
        for ch in range(32):
            c.op("Y", lambda e, ch=ch: e.dma_start(
                out=wT_sb[1][:, ch, :],
                in_=w_bf_scr.ap()[512:1024, ch * 128:(ch + 1) * 128],
                transpose=True),
                waits=wt1w if ch == 0 else [], inc="dWT1", inc_n=16)

        d_last_mm = {}
        dslot_v = {}
        x1b_free = {}
        outst_free = {}
        rs_used = {}

        def load_wT(oc, queue):
            par = oc % 2
            wY = [("dDg", restage_done)]
            if oc >= 2:
                wY.append(("sT", d_last_mm[oc - 2]))
            for ch in range(32):
                c.op(queue, lambda e, oc=oc, ch=ch, par=par: e.dma_start(
                    out=wT_sb[par][:, ch, :],
                    in_=w_bf_scr.ap()[oc * 512:(oc + 1) * 512, ch * 128:(ch + 1) * 128],
                    transpose=True),
                    waits=wY if ch == 0 else [], inc=f"dWT{par}", inc_n=16)

        def load_rs(oc):
            par = oc % 2
            wG = [("dDg", c.c("dDg"))]
            if par in rs_used:
                wG.append(("sV", rs_used[par]))
            if oc >= 2:
                wG.append(("sT", d_last_mm[oc - 2]))
            c.dma("G", rs_bc[par][:], rs_bc_scr.ap()[:, oc * 512:(oc + 1) * 512],
                  f"dWF{par}", waits=wG)
            c.dma("G", biasb_oc[par][:], ffn_b[0:1, oc * 512:(oc + 1) * 512],
                  f"dWF{par}")

        load_rs(0)
        load_rs(1)
        for oc in range(8):
            par = oc % 2
            if oc >= 1 and oc + 1 < 8:
                load_wT(oc + 1, "Y")
            wt_ready = c.c(f"dWT{par}")
            for t in range(nT):
                slot = (oc * nT + t) % 2
                sslot = (oc * nT + t) % 2
                wmm = []
                if t == 0:
                    wmm.append((f"dWT{par}", wt_ready))
                    wmm.append((f"dWF{par}", c.c(f"dWF{par}")))
                if oc == 0:
                    wmm.extend([({"V": "sV", "A": "sA"}[lncp2[(t, ch)][0]],
                                 lncp2[(t, ch)][1]) for ch in (30, 31)])
                if slot in dslot_v:
                    wmm.append(("sV", dslot_v[slot]))
                for ch in range(32):
                    c.op("T", lambda e, ch=ch, t=t, slot=slot, par=par: e.matmul(
                        ps_d[slot][:],
                        ln2_fm[:, ch * TOKW + t * 128: ch * TOKW + (t + 1) * 128],
                        wT_sb[par][:, ch, :],
                        start=(ch == 0), stop=False),
                        waits=wmm if ch == 0 else [], inc=None)
                last = c.op("T", lambda e, par=par, slot=slot: e.matmul(
                    ps_d[slot][:], ones_sb[0:1, 0:128], biasb_oc[par][:],
                    start=False, stop=True), inc="sT")
                d_last_mm[oc] = last
                wx1 = [("sV", x1b_free[sslot])] if sslot in x1b_free else []
                c.dma("Y", x1s_stage[sslot][:],
                      x1_scr.ap()[t * 128:(t + 1) * 128, oc * 512:(oc + 1) * 512],
                      f"dX1B{sslot}", waits=wx1)
                x1_ready = c.c(f"dX1B{sslot}")
                wmsk = [("sT", last)]
                if sslot in outst_free:
                    wmsk.append((f"dOUT{sslot}", outst_free[sslot]))
                v0_ = c.op("V", lambda e, t=t, slot=slot, sslot=sslot, par=par: e.scalar_tensor_tensor(
                    out_stage[sslot][:], ps_d[slot][:], qs_all[:, t:t + 1],
                    rs_bc[par][:], ALU.mult, ALU.mult),
                    waits=wmsk, inc="sV")
                dslot_v[slot] = v0_
                rs_used[par] = v0_
                v1_ = c.op("V", lambda e, sslot=sslot: e.tensor_tensor(
                    out_stage[sslot][:], out_stage[sslot][:], x1s_stage[sslot][:],
                    ALU.add), waits=[("sV", v0_), (f"dX1B{sslot}", x1_ready)], inc="sV")
                x1b_free[sslot] = v1_
                do = c.dma("Y", out[t * 128:(t + 1) * 128, oc * 512:(oc + 1) * 512],
                           out_stage[sslot][:], f"dOUT{sslot}", waits=[("sV", v1_)])
                outst_free[sslot] = do
            if oc + 2 < 8:
                load_rs(oc + 2)
        c.wait("Y", ("dOUT0", c.c("dOUT0")), ("dOUT1", c.c("dOUT1")))
        c.barrier(self_waits={"Y": [("dOUT0", c.c("dOUT0")), ("dOUT1", c.c("dOUT1")),
                                    ("dX1B0", c.c("dX1B0")), ("dX1B1", c.c("dX1B1")),
                                    ("dWT0", c.c("dWT0")), ("dWT1", c.c("dWT1"))],
                              "G": [("dWF0", c.c("dWF0")), ("dWF1", c.c("dWF1"))]})

    with nc.Block() as block:
        @block.tensor
        def _(eng):
            emit(Ctx(nc, "T", engines, sems))

        @block.vector
        def _(eng):
            emit(Ctx(nc, "V", engines, sems))

        @block.scalar
        def _(eng):
            emit(Ctx(nc, "A", engines, sems))

        @block.gpsimd
        def _(eng):
            emit(Ctx(nc, "G", engines, sems))

        @block.sync
        def _(eng):
            emit(Ctx(nc, "Y", engines, sems))

    return nc


def kernel(L0_attn_hidden_states, L0_attn_w_qkv, L0_attn_w_o, L0_ffn_weight,
           L0_ffn_bias, L0_ffn_centroids, L0_ffn_assignments):
    x = np.ascontiguousarray(np.asarray(L0_attn_hidden_states, dtype=np.float32))
    wq = np.ascontiguousarray(np.asarray(L0_attn_w_qkv, dtype=np.float32))
    wo = np.ascontiguousarray(np.asarray(L0_attn_w_o, dtype=np.float32))
    fw = np.ascontiguousarray(np.asarray(L0_ffn_weight, dtype=np.float32))
    fb = np.ascontiguousarray(np.asarray(L0_ffn_bias, dtype=np.float32)).reshape(1, -1)
    ce = np.ascontiguousarray(np.asarray(L0_ffn_centroids, dtype=np.float32))
    asg = np.ascontiguousarray(np.asarray(L0_ffn_assignments, dtype=np.int32)).reshape(1, -1)

    B, S, H_ = x.shape
    assert B == N_CORES and H_ == H

    nc = build_nc(S=S)
    in_maps = [{"x0": x[b], "w_qkv": wq, "w_o": wo, "ffn_w": fw,
                "ffn_b": fb, "cent": ce, "assign": asg} for b in range(B)]
    res = run_bass_kernel_spmd(nc, in_maps, core_ids=list(range(N_CORES)))
    return np.stack([res.results[b]["out"] for b in range(B)], axis=0)


# version 15
# speedup vs baseline: 1.1331x; 1.0800x over previous
import math
from contextlib import ExitStack

import numpy as np
import ml_dtypes

import concourse.bass as bass
import concourse.mybir as mybir
from concourse.bass_utils import run_bass_kernel_spmd

F32 = mybir.dt.float32
BF16 = mybir.dt.bfloat16
I32 = mybir.dt.int32
AF = mybir.ActivationFunctionType
ALU = mybir.AluOpType
AX = mybir.AxisListType.X

H = 4096
QKV = 6144
NH, NKV, D = 32, 8, 128
G_HEADS = 4
NCLUST = 64
TEMP_INV = 10.0
THRESH = 0.01
EPS = 1e-5
SCALE = D ** -0.5
C_SHIFT = 20.0
BIG = 1.0e33

N_CORES = 8

ENGS = ("T", "V", "A", "G", "Y")


def _alibi_slopes(n):
    cp2 = 2 ** math.floor(math.log2(n))
    base = 2.0 ** (-8.0 / cp2)
    slopes = base ** np.arange(1, cp2 + 1, dtype=np.float32)
    if cp2 != n:
        extra_base = 2.0 ** (-4.0 / cp2)
        extra = extra_base ** np.arange(1, 1 + 2 * (n - cp2), 2, dtype=np.float32)
        slopes = np.concatenate([slopes, extra])
    return slopes.astype(np.float32)


SLOPES = _alibi_slopes(NH)


def _sd_all(nq):
    ntile = 4 * nq + 4
    out = np.full((128, ntile * 128), BIG, dtype=np.float32)
    qr = np.arange(128)
    kr = np.arange(128)[:, None]
    for didx in range(ntile):
        delta = didx - 3
        rel = delta * 128 + qr[None, :] - kr
        blk = np.where(rel >= 0, np.sqrt(np.maximum(rel, 0).astype(np.float64)), BIG)
        out[:, didx * 128:(didx + 1) * 128] = blk.astype(np.float32)
    return out



def _copy(e, dst, src):
    if isinstance(e, bass.BassScalarEngine):
        return e.copy(dst, src)
    return e.tensor_copy(dst, src)

class Ctx:

    def __init__(self, nc, me, engines, sems):
        self.nc = nc
        self.me = me
        self.engines = engines
        self.sems = sems
        self.cnt = {k: 0 for k in sems}
        self.bar_k = 0

    def c(self, s):
        return self.cnt[s]

    def op(self, eng, fn, waits=(), inc=None, inc_n=1):
        if self.me == eng:
            e = self.engines[eng]
            for sname, val in waits:
                if val > 0:
                    e.wait_ge(self.sems[sname], val)
            inst = fn(e)
            if inc is not None:
                inst.then_inc(self.sems[inc], inc_n)
        if inc is not None:
            self.cnt[inc] += inc_n
        return self.cnt[inc] if inc is not None else None

    def wait(self, eng, *waits):
        if self.me == eng:
            e = self.engines[eng]
            for sname, val in waits:
                if val > 0:
                    e.wait_ge(self.sems[sname], val)

    def dma(self, eng, out, in_, sem, waits=()):
        return self.op(eng, lambda e: e.dma_start(out=out, in_=in_),
                       waits=waits, inc=sem, inc_n=16)

    def barrier(self, self_waits=None):
        self.bar_k += 1
        target = 5 * self.bar_k
        own = {"T": "sT", "V": "sV", "A": "sA", "G": "sG"}
        for eng in ENGS:
            w = list((self_waits or {}).get(eng, ()))
            if eng in own and self.cnt[own[eng]] > 0:
                w.append((own[eng], self.cnt[own[eng]]))
            self.op(eng, lambda e: e.nop(), waits=w, inc="bar")
        for eng in ENGS:
            self.wait(eng, ("bar", target))


def build_nc(S=2048):
    nT = S // 128
    nQ = S // 512
    TOKW = S
    ND = 4 * nQ + 4

    nc = bass.Bass("TRN2", target_bir_lowering=False)

    x0 = nc.dram_tensor("x0", [S, H], F32, kind="ExternalInput").ap()
    w_qkv = nc.dram_tensor("w_qkv", [H, QKV], F32, kind="ExternalInput").ap()
    w_o = nc.dram_tensor("w_o", [H, H], F32, kind="ExternalInput").ap()
    ffn_w = nc.dram_tensor("ffn_w", [H, H], F32, kind="ExternalInput").ap()
    ffn_b = nc.dram_tensor("ffn_b", [1, H], F32, kind="ExternalInput").ap()
    cent = nc.dram_tensor("cent", [NCLUST, H], F32, kind="ExternalInput").ap()
    assign = nc.dram_tensor("assign", [1, H], I32, kind="ExternalInput").ap()
    out = nc.dram_tensor("out", [S, H], F32, kind="ExternalOutput").ap()

    wqkv_r = w_qkv.rearrange("(c p) n -> p c n", p=128)
    wo_r = w_o.rearrange("(c p) n -> p c n", p=128)

    q_scr = nc.dram_tensor("q_scr", [NH, 128, S], BF16)
    k_scr = nc.dram_tensor("k_scr", [NKV, 128, S], BF16)
    v_scr = nc.dram_tensor("v_scr", [NKV, 128, S], BF16)
    x1_scr = nc.dram_tensor("x1_scr", [S, H], F32)
    w_bf_scr = nc.dram_tensor("w_bf_scr", [H, H], BF16)
    rs_bc_scr = nc.dram_tensor("rs_bc_scr", [128, H], BF16)
    cc_in = nc.dram_tensor("cc_in", [NCLUST, 1], F32)
    cc_out = nc.dram_tensor("cc_out", [NCLUST, 1], F32)

    ident_bf = nc.inline_tensor(np.eye(128, dtype=ml_dtypes.bfloat16), "ident_bf")
    ident_f = nc.inline_tensor(np.eye(128, dtype=np.float32), "ident_f")
    sd_const = nc.inline_tensor(_sd_all(nQ), "sd_const")
    ones_const = nc.inline_tensor(np.ones((1, 128), dtype=ml_dtypes.bfloat16), "ones_const")
    iota_const = nc.inline_tensor(
        np.arange(NCLUST, dtype=np.float32).reshape(NCLUST, 1), "iota_const")
    onescol_const = nc.inline_tensor(np.ones((128, 1), dtype=ml_dtypes.bfloat16), "onescol_const")
    ones128_const = nc.inline_tensor(np.ones((128, 128), dtype=ml_dtypes.bfloat16), "ones128_const")
    onesf_const = nc.inline_tensor(np.ones((1, 128), dtype=np.float32), "onesf_const")

    ident_sb = nc.alloc_sbuf_tensor("ident_sb", [128, 128], BF16)
    ones_sb = nc.alloc_sbuf_tensor("ones_sb", [1, 128], BF16)
    iota_sb = nc.alloc_sbuf_tensor("iota_sb", [NCLUST, 1], F32)
    qs_all = nc.alloc_sbuf_tensor("qs_all", [128, nT], F32)
    lnstat = nc.alloc_sbuf_tensor("lnstat", [128, 8], F32)
    eps_sb = nc.alloc_sbuf_tensor("eps_sb", [128, 1], F32)
    onescol_sb = nc.alloc_sbuf_tensor("onescol_sb", [128, 1], BF16)
    ones128_sb = nc.alloc_sbuf_tensor("ones128_sb", [128, 128], BF16)
    onesf_sb = nc.alloc_sbuf_tensor("onesf_sb", [1, 128], F32)

    esA = ExitStack()
    ln1_fm = esA.enter_context(nc.sbuf_tensor("ln1_fm", [128, 32 * TOKW], BF16))
    xbuf = [esA.enter_context(nc.sbuf_tensor(f"xbuf{i}", [128, H], F32)) for i in range(2)]
    ln1_tm = [esA.enter_context(nc.sbuf_tensor(f"ln1_tm{i}", [128, H], BF16)) for i in range(2)]
    w_sbA = [esA.enter_context(nc.sbuf_tensor(f"w_sbA{i}", [128, 32, 128], BF16)) for i in range(2)]
    qkv_stage = [esA.enter_context(nc.sbuf_tensor(f"qkv_st{i}", [128, 512], BF16)) for i in range(2)]
    vstage = esA.enter_context(nc.sbuf_tensor("vstage", [128, 512], BF16))
    vtr_stage = [esA.enter_context(nc.sbuf_tensor(f"vtr_st{i}", [128, 512], BF16)) for i in range(2)]
    psA = ExitStack()
    ps_tr = [psA.enter_context(nc.psum_tensor(f"ps_tr{i}", [128, 128], BF16)) for i in range(4)]
    ps_qkv = [psA.enter_context(nc.psum_tensor(f"ps_qkv{i}", [128, 512], F32)) for i in range(4)]
    psA.close()
    esA.close()

    esBC = ExitStack()
    attn_fm = esBC.enter_context(nc.sbuf_tensor("attn_fm", [128, 32 * TOKW], BF16))
    esB = ExitStack()
    sd_sb = esB.enter_context(nc.sbuf_tensor("sd_sb", [128, ND * 128], F32))
    k_sb = [esB.enter_context(nc.sbuf_tensor(f"k_sb{i}", [128, S], BF16)) for i in range(2)]
    v_sb = [esB.enter_context(nc.sbuf_tensor(f"v_sb{i}", [128, S], BF16)) for i in range(2)]
    p_all = esB.enter_context(nc.sbuf_tensor("p_all", [128, nT * 512], BF16))
    q_tile = [esB.enter_context(nc.sbuf_tensor(f"q_tile{i}", [128, 512], BF16)) for i in range(2)]
    rec_row = esB.enter_context(nc.sbuf_tensor("rec_row", [1, 512], F32))
    rec_bc = [esB.enter_context(nc.sbuf_tensor(f"rec_bc{i}", [128, 512], F32)) for i in range(2)]
    psB = ExitStack()
    ps_s = [psB.enter_context(nc.psum_tensor(f"ps_s{i}", [128, 512], F32)) for i in range(4)]
    ps_avf = [psB.enter_context(nc.psum_tensor(f"ps_avf{i}", [128, 512], F32)) for i in range(2)]
    ps_dbc = [psB.enter_context(nc.psum_tensor(f"ps_dbc{i}", [128, 512], F32)) for i in range(2)]
    psB.close()
    esB.close()
    esC = ExitStack()
    wo_sb = [esC.enter_context(nc.sbuf_tensor(f"wo_sb{i}", [128, 32, 512], BF16)) for i in range(2)]
    x0_stage = [esC.enter_context(nc.sbuf_tensor(f"x0_st{i}", [128, 512], F32)) for i in range(2)]
    x1_stage = [esC.enter_context(nc.sbuf_tensor(f"x1_st{i}", [128, 512], F32)) for i in range(2)]
    psC = ExitStack()
    ps_x1 = [psC.enter_context(nc.psum_tensor(f"ps_x1{i}", [128, 512], F32)) for i in range(3)]
    psC.close()
    esC.close()
    esBC.close()

    esR = ExitStack()
    cent_T = esR.enter_context(nc.sbuf_tensor("cent_T", [128, 32 * NCLUST], BF16, side="right"))
    a1_sb = esR.enter_context(nc.sbuf_tensor("a1_sb", [NCLUST, H], BF16, side="right"))
    esD0 = ExitStack()
    cent_sb = esD0.enter_context(nc.sbuf_tensor("cent_sb", [NCLUST, H], BF16))
    asg_i = esD0.enter_context(nc.sbuf_tensor("asg_i", [1, H], I32))
    asg_f = esD0.enter_context(nc.sbuf_tensor("asg_f", [1, H], F32))
    asg_bf = esD0.enter_context(nc.sbuf_tensor("asg_bf", [1, H], BF16))
    esD0.close()
    esD = ExitStack()
    ln2_fm = esD.enter_context(nc.sbuf_tensor("ln2_fm", [128, 32 * TOKW], BF16))
    esD1 = ExitStack()
    x1buf = [esD1.enter_context(nc.sbuf_tensor(f"x1buf{i}", [128, H], F32)) for i in range(2)]
    ln2_tm = [esD1.enter_context(nc.sbuf_tensor(f"ln2_tm{i}", [128, H], BF16)) for i in range(2)]
    identf_sb = esD1.enter_context(nc.sbuf_tensor("identf_sb", [128, 128], F32))
    dots_e = esD1.enter_context(nc.sbuf_tensor("dots_e", [128, NCLUST], F32))
    dots_n = esD1.enter_context(nc.sbuf_tensor("dots_n", [128, NCLUST], F32))
    dstat = esD1.enter_context(nc.sbuf_tensor("dstat", [128, 8], F32))
    cmax_t = esD1.enter_context(nc.sbuf_tensor("cmax_t", [NCLUST, 1], F32))
    run_cmax = esD1.enter_context(nc.sbuf_tensor("run_cmax", [NCLUST, 1], F32))
    gmax_sb = esD1.enter_context(nc.sbuf_tensor("gmax_sb", [NCLUST, 1], F32))
    cs_f = esD1.enter_context(nc.sbuf_tensor("cs_f", [NCLUST, 1], F32))
    cs_bf = esD1.enter_context(nc.sbuf_tensor("cs_bf", [NCLUST, 1], BF16))
    rs_row_bf = esD1.enter_context(nc.sbuf_tensor("rs_row_bf", [1, 512], BF16))
    rs_stage = esD1.enter_context(nc.sbuf_tensor("rs_stage", [128, 512], BF16))
    esD1.close()
    esR.close()
    esD3 = ExitStack()
    wT_sb = [esD3.enter_context(nc.sbuf_tensor(f"wT_sb{i}", [128, 32, 512], BF16)) for i in range(2)]
    rs_bc = [esD3.enter_context(nc.sbuf_tensor(f"rs_bc{i}", [128, 512], BF16)) for i in range(2)]
    biasb_oc = [esD3.enter_context(nc.sbuf_tensor(f"biasb_oc{i}", [1, 512], BF16)) for i in range(2)]
    x1s_stage = [esD3.enter_context(nc.sbuf_tensor(f"x1s_st{i}", [128, 512], F32)) for i in range(2)]
    out_stage = [esD3.enter_context(nc.sbuf_tensor(f"out_st{i}", [128, 512], F32)) for i in range(2)]
    esD3.close()
    esD.close()
    psD = ExitStack()
    ps_trD = [psD.enter_context(nc.psum_tensor(f"ps_trD{i}", [128, 128], BF16)) for i in range(4)]
    ps_dots = psD.enter_context(nc.psum_tensor("ps_dots", [128, 512], F32))
    ps_dt = psD.enter_context(nc.psum_tensor("ps_dt", [NCLUST, 128], F32))
    ps_d = [psD.enter_context(nc.psum_tensor(f"ps_d{i}", [128, 512], F32)) for i in range(2)]
    psD.close()

    sem_names = ["bar", "sT", "sV", "sA", "sG", "dP", "dD", "dCC",
                 "dDg", "dXA0", "dXA1", "dW0", "dW1", "dSQ0", "dSQ1", "dSV0", "dSV1",
                 "dKV0", "dKV1", "dQL0", "dQL1", "dWO0", "dWO1", "dX00", "dX01",
                 "dX1S0", "dX1S1", "dX1L0", "dX1L1", "dWF0", "dWF1", "dWT0", "dWT1",
                 "dX1B0", "dX1B1", "dOUT0", "dOUT1"]
    sems = {n: nc.alloc_semaphore(name=n) for n in sem_names}

    engines = {"T": nc.tensor, "V": nc.vector, "A": nc.scalar,
               "G": nc.gpsimd, "Y": nc.sync}

    def emit(c: Ctx):
        c.dma("Y", ident_sb[:], ident_bf.ap(), "dP")
        c.dma("Y", ones_sb[:], ones_const.ap(), "dP")
        c.dma("Y", iota_sb[:], iota_const.ap(), "dP")
        c.dma("Y", onescol_sb[:], onescol_const.ap(), "dP")
        c.dma("Y", ones128_sb[:], ones128_const.ap(), "dP")
        c.dma("Y", onesf_sb[:], onesf_const.ap(), "dP")
        c.op("V", lambda e: e.memset(eps_sb[:], EPS), inc="sV")
        p0_done = {"Y": [("dP", c.c("dP"))]}
        c.barrier(self_waits=p0_done)

        ln_free_v = {}
        ln_free_a = {}
        lncp = {}
        tr_mark = {}
        var_mark = {}
        for t in range(nT):
            w = []
            if t >= 2:
                w = [("sV", ln_free_v[t - 2]), ("sA", ln_free_a[t - 2])]
            xa = c.dma("Y", xbuf[t % 2][:], x0[t * 128:(t + 1) * 128, :],
                       f"dXA{t % 2}", waits=w)
            xb = xbuf[t % 2]
            lt = ln1_tm[t % 2]
            m0 = c.op("V", lambda e, xb=xb: e.reduce_sum(lnstat[:, 0:1], xb[:], axis=AX),
                      waits=[(f"dXA{t % 2}", xa)], inc="sV")
            wsq = [(f"dXA{t % 2}", xa)]
            if t >= 1:
                wsq.append(("sV", var_mark[t - 1]))
            if t >= 2:
                wsq.append(("sT", tr_mark[(t - 2, 31)]))
            a0 = c.op("A", lambda e, xb=xb, lt=lt: e.activation(
                out=lt[:], in_=xb[:], func=AF.Square, accum_out=lnstat[:, 2:3]),
                waits=wsq, inc="sA")
            ln_free_a[t] = a0
            m1 = c.op("V", lambda e: e.tensor_scalar(lnstat[:, 1:2], lnstat[:, 0:1],
                                                     1.0 / H, None, ALU.mult),
                      waits=[("sV", m0)], inc="sV")
            m1b = c.op("V", lambda e: e.tensor_scalar(lnstat[:, 5:6], lnstat[:, 1:2],
                                                      lnstat[:, 1:2], None, ALU.mult),
                      waits=[("sV", m1)], inc="sV")
            m1c = c.op("V", lambda e: e.tensor_scalar(lnstat[:, 6:7], lnstat[:, 2:3],
                                                      1.0 / H, lnstat[:, 5:6],
                                                      ALU.mult, ALU.subtract),
                      waits=[("sV", m1b), ("sA", a0)], inc="sV")
            var_mark[t] = m1c
            a1 = c.op("A", lambda e: e.activation(out=lnstat[:, 3:4], in_=lnstat[:, 6:7],
                                                  func=AF.Ln, bias=eps_sb[:]),
                      waits=[("sV", m1c)], inc="sA")
            a2 = c.op("A", lambda e: e.activation(out=lnstat[:, 4:5], in_=lnstat[:, 3:4],
                                                  func=AF.Exp, scale=-0.5),
                      waits=[("sA", a1)], inc="sA")
            m3 = c.op("V", lambda e, xb=xb, lt=lt: e.tensor_scalar(
                lt[:], xb[:], lnstat[:, 1:2], lnstat[:, 4:5], ALU.subtract, ALU.mult),
                waits=[("sA", a2)], inc="sV")
            ln_free_v[t] = m3
            for ch in range(32):
                wt = [("sV", m3)] if ch == 0 else []
                if (t, ch - 4) in lncp:
                    e_, cnt_ = lncp[(t, ch - 4)]
                    wt.append(("sV" if e_ == "V" else "sA", cnt_))
                elif t >= 1 and ch < 4:
                    e_, cnt_ = lncp[(t - 1, 28 + ch)]
                    wt.append(("sV" if e_ == "V" else "sA", cnt_))
                tm = c.op("T", lambda e, ch=ch, lt=lt: e.transpose(
                    ps_tr[ch % 4][:], lt[:, ch * 128:(ch + 1) * 128], ident_sb[:]),
                    waits=wt, inc="sT")
                tr_mark[(t, ch)] = tm
                ceng = "V" if (ch % 2 == 0) else "A"
                dst = ln1_fm[:, ch * TOKW + t * 128: ch * TOKW + (t + 1) * 128]
                cm = c.op(ceng, lambda e, dst=dst, ch=ch: _copy(e, dst, ps_tr[ch % 4][:]),
                          waits=[("sT", tm)], inc="sV" if ceng == "V" else "sA")
                lncp[(t, ch)] = (ceng, cm)

        NJG = QKV // 128
        grp_last_mm = {}
        slot_free = {}
        stage_free = {}
        vtr_free = {}
        ps_tr_cons = {sub: lncp[(nT - 1, 28 + sub)] for sub in range(4)}
        ln_chunk_ready = {n: [({"V": "sV", "A": "sA"}[lncp[(4 * n + 3, ch)][0]],
                               lncp[(4 * n + 3, ch)][1]) for ch in (30, 31)]
                          for n in range(nQ)}
        for jg in range(NJG):
            wv = []
            if jg >= 2:
                wv.append(("sT", grp_last_mm[jg - 2]))
            wld = c.dma("G", w_sbA[jg % 2][:], wqkv_r[:, :, jg * 128:(jg + 1) * 128],
                        f"dW{jg % 2}", waits=wv)
            if jg < 32:
                c.dma("G", w_bf_scr.ap()[jg * 128:(jg + 1) * 128, :],
                      ffn_w[jg * 128:(jg + 1) * 128, :], "dDg")
            j_abs = jg
            for n in range(nQ):
                slot = n % 4
                wmm = [(f"dW{jg % 2}", wld)]
                if jg < 2:
                    wmm.extend(ln_chunk_ready[n])
                if slot in slot_free:
                    e_, cnt_ = slot_free[slot]
                    wmm.append(({"V": "sV", "A": "sA", "T": "sT"}[e_], cnt_))
                last = None
                for ch in range(32):
                    last = c.op("T", lambda e, ch=ch, n=n, jg=jg, slot=slot: e.matmul(
                        ps_qkv[slot][:],
                        w_sbA[jg % 2][:, ch, :],
                        ln1_fm[:, ch * TOKW + n * 512: ch * TOKW + (n + 1) * 512],
                        start=(ch == 0), stop=(ch == 31)),
                        waits=wmm if ch == 0 else [],
                        inc="sT" if ch == 31 else None)
                grp_last_mm[jg] = last
                if j_abs < NH:
                    st = n % 2
                    wep = [("sT", last)]
                    if st in stage_free:
                        wep.append((f"dSQ{st}", stage_free[st]))
                    acp = c.op("A", lambda e, st=st, slot=slot: e.activation(
                        out=qkv_stage[st][:], in_=ps_qkv[slot][:], func=AF.Copy,
                        scale=SCALE),
                        waits=wep, inc="sA")
                    slot_free[slot] = ("A", acp)
                    dsp = c.dma("Y", q_scr.ap()[j_abs, :, n * 512:(n + 1) * 512],
                                qkv_stage[st][:], f"dSQ{st}", waits=[("sA", acp)])
                    stage_free[st] = dsp
                elif j_abs < NH + NKV:
                    hk = j_abs - NH
                    st = n % 2
                    wep = [("sT", last)]
                    if st in stage_free:
                        wep.append((f"dSQ{st}", stage_free[st]))
                    acp = c.op("A", lambda e, st=st, slot=slot: e.activation(
                        out=qkv_stage[st][:], in_=ps_qkv[slot][:], func=AF.Copy),
                        waits=wep, inc="sA")
                    slot_free[slot] = ("A", acp)
                    dsp = c.dma("Y", k_scr.ap()[hk, :, n * 512:(n + 1) * 512],
                                qkv_stage[st][:], f"dSQ{st}", waits=[("sA", acp)])
                    stage_free[st] = dsp
                else:
                    hk = j_abs - NH - NKV
                    wep = [("sT", last)]
                    if "vstage" in slot_free:
                        wep.append(("sT", slot_free["vstage"][1]))
                    vcp = c.op("A", lambda e, slot=slot: e.activation(
                        out=vstage[:], in_=ps_qkv[slot][:], func=AF.Copy),
                        waits=wep, inc="sA")
                    slot_free[slot] = ("A", vcp)
                    vslot = n % 2
                    trs = []
                    for sub in range(4):
                        wtr = [("sA", vcp)] if sub == 0 else []
                        if sub in ps_tr_cons:
                            e_, cnt_ = ps_tr_cons[sub]
                            wtr = wtr + [("sV" if e_ == "V" else "sA", cnt_)]
                        tm = c.op("T", lambda e, sub=sub: e.transpose(
                            ps_tr[sub % 4][:], vstage[:, sub * 128:(sub + 1) * 128],
                            ident_sb[:]), waits=wtr, inc="sT")
                        trs.append(tm)
                    slot_free["vstage"] = ("T", trs[-1])
                    wcp_extra = [(f"dSV{vslot}", vtr_free[vslot])] if vslot in vtr_free else []
                    vcps = []
                    for sub in range(4):
                        ceng = "V" if sub % 2 == 0 else "A"
                        cm = c.op(ceng, lambda e, sub=sub, vslot=vslot: _copy(e, vtr_stage[vslot][:, sub * 128:(sub + 1) * 128], ps_tr[sub % 4][:]),
                            waits=[("sT", trs[sub])] + (wcp_extra if sub < 2 else []),
                            inc="sV" if ceng == "V" else "sA")
                        vcps.append((ceng, cm))
                        ps_tr_cons[sub] = (ceng, cm)
                    wd = [({"V": "sV", "A": "sA"}[e_], cnt_) for e_, cnt_ in vcps[-2:]]
                    dsp = c.dma("Y", v_scr.ap()[hk, :, n * 512:(n + 1) * 512],
                                vtr_stage[vslot][:], f"dSV{vslot}", waits=wd)
                    vtr_free[vslot] = dsp
        c.barrier(self_waits={"G": [("dW0", c.c("dW0")), ("dW1", c.c("dW1"))],
                              "Y": [("dSQ0", c.c("dSQ0")), ("dSQ1", c.c("dSQ1")),
                                    ("dSV0", c.c("dSV0")), ("dSV1", c.c("dSV1"))]})

        c.dma("Y", sd_sb[:], sd_const.ap(), "dP")
        sd_ready = c.c("dP")

        kv_last_use = {}
        q_last = {}
        avf_free = {}
        den_free = {}
        prev_exp = {}
        pall_av_last = None
        pend_ep = [None]

        def flush_ep1():
            ep = pend_ep[0]
            if ep is None:
                return
            hp, np_, par, avf_stop, den_stop = ep
            wln = [("sT", den_stop)]
            if par in avf_free:
                wln.append(("sV", avf_free[par]))
            ra = c.op("A", lambda e, par=par: e.activation(
                out=rec_bc[par][:], in_=ps_dbc[par][:, :], func=AF.Ln),
                waits=wln, inc="sA")
            den_free[par] = ("sA", ra)
            r0 = c.op("A", lambda e, par=par: e.activation(
                out=rec_bc[par][:], in_=rec_bc[par][:], func=AF.Exp, scale=-1.0),
                waits=[("sA", ra)], inc="sA")
            pend_ep[0] = (hp, np_, par, avf_stop, r0)

        def flush_ep2():
            ep = pend_ep[0]
            if ep is None:
                return
            pend_ep[0] = None
            hp, np_, par, avf_stop, r0 = ep
            dst = attn_fm[:, hp * TOKW + np_ * 512: hp * TOKW + (np_ + 1) * 512]
            tt = c.op("V", lambda e, dst=dst, par=par: e.tensor_tensor(
                dst, ps_avf[par][:], rec_bc[par][:], ALU.mult),
                waits=[("sT", avf_stop), ("sA", r0)], inc="sV")
            avf_free[par] = tt

        chunks = [(hk, g, n) for hk in range(NKV) for g in range(G_HEADS)
                  for n in range(nQ)]
        q_rdy = {}

        def q_load(ci):
            hk, g, n = chunks[ci]
            h = hk * G_HEADS + g
            qslot = ci % 2
            wq = []
            if qslot in q_last:
                wq.append(("sT", q_last[qslot]))
            c.dma("Y", q_tile[qslot][:], q_scr.ap()[h, :, n * 512:(n + 1) * 512],
                  f"dQL{qslot}", waits=wq)
            q_rdy[ci] = c.c(f"dQL{qslot}")

        kv_rdy = {}
        for ci, (hk, g, n) in enumerate(chunks):
            if g == 0 and n == 0:
                wkv = []
                if hk >= 2:
                    wkv.append(("sT", kv_last_use[hk % 2]))
                c.dma("Y", k_sb[hk % 2][:], k_scr.ap()[hk, :, :], f"dKV{hk % 2}",
                      waits=wkv)
                c.dma("Y", v_sb[hk % 2][:], v_scr.ap()[hk, :, :], f"dKV{hk % 2}")
                kv_rdy[hk] = c.c(f"dKV{hk % 2}")
            kv_sem = f"dKV{hk % 2}"
            kv_ready = kv_rdy[hk]
            if ci == 0:
                q_load(0)
            h = hk * G_HEADS + g
            slope = -float(SLOPES[h])
            qslot = ci % 2
            q_sem = f"dQL{qslot}"
            q_ready = q_rdy[ci]
            par = ci % 2
            nkt = 4 * n + 4

            def score(kt):
                wsc = [(kv_sem, kv_ready), (q_sem, q_ready)]
                if (kt % 4) in prev_exp:
                    wsc.append(("sA", prev_exp[kt % 4]))
                sm = c.op("T", lambda e, kt=kt, hk=hk, qslot=qslot: e.matmul(
                    ps_s[kt % 4][:], k_sb[hk % 2][:, kt * 128:(kt + 1) * 128],
                    q_tile[qslot][:], start=True, stop=True),
                    waits=wsc, inc="sT")
                q_last[qslot] = sm
                kv_last_use[hk % 2] = sm
                return sm

            sms = {}
            for kt0 in range(min(3, nkt)):
                sms[kt0] = score(kt0)
            for kt in range(nkt):
                if kt + 3 < nkt:
                    sms[kt + 3] = score(kt + 3)
                didx = 4 * n - kt + 3
                wst = [("sT", sms[kt])]
                if kt == 0:
                    wst.append(("dP", sd_ready))
                tt = c.op("V", lambda e, kt=kt, didx=didx, slope=slope: e.scalar_tensor_tensor(
                    ps_s[kt % 4][:], sd_sb[:, didx * 128: didx * 128 + 512],
                    slope, ps_s[kt % 4][:], ALU.mult, ALU.add),
                    waits=wst, inc="sV")
                wex = [("sV", tt)]
                if kt == 0 and pall_av_last is not None:
                    wex.append(("sT", pall_av_last))
                em = c.op("A", lambda e, kt=kt: e.activation(
                    out=p_all[:, kt * 512:(kt + 1) * 512], in_=ps_s[kt % 4][:],
                    func=AF.Exp), waits=wex, inc="sA")
                prev_exp[kt % 4] = em
                if kt == 0 and ci + 1 < len(chunks):
                    q_load(ci + 1)
                if kt == 1:
                    flush_ep1()
                if kt == 3:
                    flush_ep2()
                wav = [("sA", em)]
                if kt == 0 and par in avf_free:
                    wav.append(("sV", avf_free[par]))
                c.op("T", lambda e, kt=kt, hk=hk, par=par, nkt=nkt: e.matmul(
                    ps_avf[par][:], v_sb[hk % 2][:, kt * 128:(kt + 1) * 128],
                    p_all[:, kt * 512:(kt + 1) * 512],
                    start=(kt == 0), stop=(kt == nkt - 1)),
                    waits=wav, inc="sT")
                avf_stop = c.c("sT")
                wdn = []
                if kt == 0 and par in den_free:
                    wdn.append(den_free[par])
                c.op("T", lambda e, kt=kt, par=par, nkt=nkt: e.matmul(
                    ps_dbc[par][:, :], ones128_sb[:], p_all[:, kt * 512:(kt + 1) * 512],
                    start=(kt == 0), stop=(kt == nkt - 1)),
                    waits=wdn, inc="sT")
                den_stop = c.c("sT")
                kv_last_use[hk % 2] = c.c("sT")
                if kt == nkt - 1:
                    pall_av_last = c.c("sT")
                    pend_ep[0] = (h, n, par, avf_stop, den_stop)
            if nkt == 4:
                pass
        flush_ep1()
        flush_ep2()
        c.barrier(self_waits={"Y": [("dKV0", c.c("dKV0")), ("dKV1", c.c("dKV1")),
                                    ("dQL0", c.c("dQL0")), ("dQL1", c.c("dQL1"))]})

        wo_last_mm = {}
        cslot_tt = {}
        x0s_free = {}
        x1s_free = {}
        x0_rdy = {}

        def c_x0(oc, t):
            sslot = (oc * nT + t) % 2
            wx0 = [("sV", x0s_free[sslot])] if sslot in x0s_free else []
            c.dma("Y", x0_stage[sslot][:],
                  x0[t * 128:(t + 1) * 128, oc * 512:(oc + 1) * 512], f"dX0{sslot}",
                  waits=wx0)
            x0_rdy[(oc, t)] = c.c(f"dX0{sslot}")

        for oc in range(8):
            wv = []
            if oc >= 2:
                wv.append(("sT", wo_last_mm[oc - 2]))
            wold = c.dma("G", wo_sb[oc % 2][:], wo_r[:, :, oc * 512:(oc + 1) * 512],
                         f"dWO{oc % 2}", waits=wv)
            for t in range(nT):
                slot = (oc * nT + t) % 3
                sslot = (oc * nT + t) % 2
                if oc == 0 and t == 0:
                    c_x0(0, 0)
                wmm = [(f"dWO{oc % 2}", wold)]
                if slot in cslot_tt:
                    wmm.append(("sV", cslot_tt[slot]))
                last = None
                for ch in range(32):
                    last = c.op("T", lambda e, ch=ch, oc=oc, t=t, slot=slot: e.matmul(
                        ps_x1[slot][:],
                        attn_fm[:, ch * TOKW + t * 128: ch * TOKW + (t + 1) * 128],
                        wo_sb[oc % 2][:, ch, :],
                        start=(ch == 0), stop=(ch == 31)),
                        waits=wmm if ch == 0 else [],
                        inc="sT" if ch == 31 else None)
                wo_last_mm[oc] = last
                if t + 1 < nT:
                    c_x0(oc, t + 1)
                elif oc + 1 < 8:
                    c_x0(oc + 1, 0)
                wtt = [("sT", last), (f"dX0{sslot}", x0_rdy[(oc, t)])]
                if sslot in x1s_free:
                    wtt.append((f"dX1S{sslot}", x1s_free[sslot]))
                vt = c.op("V", lambda e, slot=slot, sslot=sslot: e.tensor_tensor(
                    x1_stage[sslot][:], ps_x1[slot][:], x0_stage[sslot][:], ALU.add),
                    waits=wtt, inc="sV")
                cslot_tt[slot] = vt
                x0s_free[sslot] = vt
                d1 = c.dma("Y", x1_scr.ap()[t * 128:(t + 1) * 128, oc * 512:(oc + 1) * 512],
                           x1_stage[sslot][:], f"dX1S{sslot}", waits=[("sV", vt)])
                x1s_free[sslot] = d1
        c.barrier(self_waits={"G": [("dWO0", c.c("dWO0")), ("dWO1", c.c("dWO1")),
                                    ("dDg", c.c("dDg"))],
                              "Y": [("dX1S0", c.c("dX1S0")), ("dX1S1", c.c("dX1S1")),
                                    ("dX00", c.c("dX00")), ("dX01", c.c("dX01"))]})

        c.dma("G", cent_sb[:], cent, "dDg")
        c.dma("Y", asg_i[:], assign, "dD")
        dD0 = c.c("dD")
        dDg0 = c.c("dDg")
        v0 = c.op("V", lambda e: e.tensor_copy(asg_f[:], asg_i[:]),
                  waits=[("dD", dD0)], inc="sV")
        v1 = c.op("V", lambda e: e.tensor_copy(asg_bf[:], asg_f[:]),
                  waits=[("sV", v0)], inc="sV")
        for ocb in range(8):
            tb = c.op("T", lambda e, ocb=ocb: e.matmul(
                ps_d[0][:NCLUST, :], ones_sb[0:1, 0:NCLUST],
                asg_bf[0:1, ocb * 512:(ocb + 1) * 512], start=True, stop=True),
                waits=[("sV", v1)] if ocb == 0 else [("sV", c.c("sV"))], inc="sT")
            c.op("V", lambda e, ocb=ocb: e.tensor_scalar(
                a1_sb[:, ocb * 512:(ocb + 1) * 512], ps_d[0][:NCLUST, :],
                iota_sb[:], None, ALU.is_equal),
                waits=[("sT", tb)], inc="sV")
        ctp = {}
        for ch in range(32):
            wt = [("dDg", dDg0)] if ch == 0 else []
            if ch >= 4:
                e_, cnt_ = ctp[ch - 4]
                wt.append(("sA" if e_ == "A" else "sV", cnt_))
            tm = c.op("T", lambda e, ch=ch: e.transpose(
                ps_trD[ch % 4][:, 0:NCLUST],
                cent_sb[0:NCLUST, ch * 128:(ch + 1) * 128],
                ident_sb[0:NCLUST, 0:NCLUST]),
                waits=wt, inc="sT")
            ceng = "A" if ch % 2 == 0 else "V"
            cm = c.op(ceng, lambda e, ch=ch: _copy(e, cent_T[:, ch * NCLUST:(ch + 1) * NCLUST], ps_trD[ch % 4][:, 0:NCLUST]),
                waits=[("sT", tm)], inc="sA" if ceng == "A" else "sV")
            ctp[ch] = (ceng, cm)
        c.barrier(self_waits={"Y": [("dD", c.c("dD"))], "G": [("dDg", c.c("dDg"))]})

        c.dma("Y", identf_sb[:], ident_f.ap(), "dP")
        idf_ready = c.c("dP")
        restage_done = dDg0
        c.op("V", lambda e: e.memset(run_cmax[:], -1e30), inc="sV")
        ln_free_v.clear()
        ln_free_a.clear()
        var_mark.clear()
        lncp2 = {}
        tr_mark2 = {}
        dt_done = {}
        for t in range(nT):
            w = []
            if t >= 2:
                w = [("sV", ln_free_v[t - 2]), ("sA", ln_free_a[t - 2])]
            x1r = c.dma("Y", x1buf[t % 2][:], x1_scr.ap()[t * 128:(t + 1) * 128, :],
                        f"dX1L{t % 2}", waits=w)
            xb = x1buf[t % 2]
            lt = ln2_tm[t % 2]
            m0 = c.op("V", lambda e, xb=xb: e.reduce_sum(lnstat[:, 0:1], xb[:], axis=AX),
                      waits=[(f"dX1L{t % 2}", x1r)], inc="sV")
            wsq = [(f"dX1L{t % 2}", x1r)]
            if t >= 1:
                wsq.append(("sV", var_mark[t - 1]))
            if t >= 2:
                wsq.append(("sT", tr_mark2[(t - 2, 31)]))
            a0 = c.op("A", lambda e, xb=xb, lt=lt: e.activation(
                out=lt[:], in_=xb[:], func=AF.Square, accum_out=lnstat[:, 2:3]),
                waits=wsq, inc="sA")
            ln_free_a[t] = a0
            m1 = c.op("V", lambda e: e.tensor_scalar(lnstat[:, 1:2], lnstat[:, 0:1],
                                                     1.0 / H, None, ALU.mult),
                      waits=[("sV", m0)], inc="sV")
            m1b = c.op("V", lambda e: e.tensor_scalar(lnstat[:, 5:6], lnstat[:, 1:2],
                                                      lnstat[:, 1:2], None, ALU.mult),
                      waits=[("sV", m1)], inc="sV")
            m1c = c.op("V", lambda e: e.tensor_scalar(lnstat[:, 6:7], lnstat[:, 2:3],
                                                      1.0 / H, lnstat[:, 5:6],
                                                      ALU.mult, ALU.subtract),
                      waits=[("sV", m1b), ("sA", a0)], inc="sV")
            var_mark[t] = m1c
            a1 = c.op("A", lambda e: e.activation(out=lnstat[:, 3:4], in_=lnstat[:, 6:7],
                                                  func=AF.Ln, bias=eps_sb[:]),
                      waits=[("sV", m1c)], inc="sA")
            a2 = c.op("A", lambda e: e.activation(out=lnstat[:, 4:5], in_=lnstat[:, 3:4],
                                                  func=AF.Exp, scale=-0.5),
                      waits=[("sA", a1)], inc="sA")
            m3 = c.op("V", lambda e, xb=xb, lt=lt: e.tensor_scalar(
                lt[:], xb[:], lnstat[:, 1:2], lnstat[:, 4:5], ALU.subtract, ALU.mult),
                waits=[("sA", a2)], inc="sV")
            ln_free_v[t] = m3
            for ch in range(32):
                wt = [("sV", m3)] if ch == 0 else []
                if (t, ch - 4) in lncp2:
                    e_, cnt_ = lncp2[(t, ch - 4)]
                    wt.append(("sV" if e_ == "V" else "sA", cnt_))
                elif t >= 1 and ch < 4:
                    e_, cnt_ = lncp2[(t - 1, 28 + ch)]
                    wt.append(("sV" if e_ == "V" else "sA", cnt_))
                tm = c.op("T", lambda e, ch=ch, lt=lt: e.transpose(
                    ps_trD[ch % 4][:], lt[:, ch * 128:(ch + 1) * 128], ident_sb[:]),
                    waits=wt, inc="sT")
                tr_mark2[(t, ch)] = tm
                ceng = "V" if (ch % 2 == 0) else "A"
                dst = ln2_fm[:, ch * TOKW + t * 128: ch * TOKW + (t + 1) * 128]
                cm = c.op(ceng, lambda e, dst=dst, ch=ch: _copy(e, dst, ps_trD[ch % 4][:]),
                          waits=[("sT", tm)], inc="sV" if ceng == "V" else "sA")
                lncp2[(t, ch)] = (ceng, cm)
            wdots = [({"V": "sV", "A": "sA"}[lncp2[(t, ch)][0]], lncp2[(t, ch)][1])
                     for ch in (30, 31)]
            if t >= 1:
                wdots.append(("sV", dt_done[t - 1]))
            lastd = None
            for ch in range(32):
                lastd = c.op("T", lambda e, ch=ch, t=t: e.matmul(
                    ps_dots[:, 0:NCLUST], ln2_fm[:, ch * TOKW + t * 128: ch * TOKW + (t + 1) * 128],
                    cent_T[:, ch * NCLUST:(ch + 1) * NCLUST],
                    start=(ch == 0), stop=(ch == 31)),
                    waits=wdots if ch == 0 else [], inc="sT" if ch == 31 else None)
            d0 = c.op("V", lambda e: e.reduce_max(dstat[:, 0:1], ps_dots[:, 0:NCLUST], axis=AX),
                      waits=[("sT", lastd)], inc="sV")
            d1 = c.op("V", lambda e: e.tensor_scalar(dstat[:, 1:2], dstat[:, 0:1],
                                                     -TEMP_INV, None, ALU.mult),
                      waits=[("sV", d0)], inc="sV")
            d2 = c.op("A", lambda e: e.activation(out=dots_e[:], in_=ps_dots[:, 0:NCLUST], func=AF.Exp,
                                                  bias=dstat[:, 1:2], scale=TEMP_INV),
                      waits=[("sV", d1)], inc="sA")
            d3 = c.op("V", lambda e: e.reduce_sum(dstat[:, 2:3], dots_e[:], axis=AX),
                      waits=[("sA", d2)], inc="sV")
            d4 = c.op("V", lambda e: e.reciprocal(dstat[:, 3:4], dstat[:, 2:3]),
                      waits=[("sV", d3)], inc="sV")
            d5 = c.op("V", lambda e: e.tensor_scalar(dots_n[:], dots_e[:], dstat[:, 3:4],
                                                     None, ALU.mult),
                      waits=[("sV", d4)], inc="sV")
            d6 = c.op("V", lambda e: e.reduce_max(dstat[:, 4:5], dots_n[:], axis=AX),
                      waits=[("sV", d5)], inc="sV")
            c.op("V", lambda e, t=t: e.tensor_single_scalar(
                qs_all[:, t:t + 1], dstat[:, 4:5], THRESH, ALU.is_gt),
                waits=[("sV", d6)], inc="sV")
            td = c.op("T", lambda e: e.transpose(ps_dt[:], dots_n[:], identf_sb[:]),
                      waits=[("sV", d5), ("dP", idf_ready)], inc="sT")
            d8 = c.op("V", lambda e: e.reduce_max(cmax_t[:], ps_dt[:], axis=AX),
                      waits=[("sT", td)], inc="sV")
            d9 = c.op("V", lambda e: e.tensor_tensor(run_cmax[:], run_cmax[:], cmax_t[:],
                                                     ALU.max),
                      waits=[("sV", d8)], inc="sV")
            dt_done[t] = d9
        wt0w = [("dDg", restage_done),
                ("sV", ln_free_v[nT - 1]), ("sA", ln_free_a[nT - 1]),
                ("sV", ln_free_v[nT - 2]), ("sA", ln_free_a[nT - 2])]
        for ch in range(32):
            c.op("Y", lambda e, ch=ch: e.dma_start(
                out=wT_sb[0][:, ch, :],
                in_=w_bf_scr.ap()[0:512, ch * 128:(ch + 1) * 128],
                transpose=True),
                waits=wt0w if ch == 0 else [], inc="dWT0", inc_n=16)
        c.dma("G", cc_in.ap(), run_cmax[:], "dCC", waits=[("sV", dt_done[nT - 1])])
        ccm = c.op("G", lambda e: e.collective_compute(
            "AllReduce", ALU.max, replica_groups=[list(range(N_CORES))],
            ins=[cc_in.ap().opt()], outs=[cc_out.ap().opt()]),
            waits=[("dCC", c.c("dCC"))], inc="sG")
        c.dma("G", gmax_sb[:], cc_out.ap(), "dCC", waits=[("sG", ccm)])
        dcc2 = c.c("dCC")
        e0 = c.op("V", lambda e: e.tensor_single_scalar(cs_f[:], gmax_sb[:], THRESH,
                                                        ALU.is_gt),
                  waits=[("dCC", dcc2)], inc="sV")
        e1 = c.op("V", lambda e: e.tensor_copy(cs_bf[:], cs_f[:]),
                  waits=[("sV", e0)], inc="sV")
        for ocb in range(8):
            tb = c.op("T", lambda e, ocb=ocb: e.matmul(
                ps_dots[0:1, :], cs_bf[:], a1_sb[:, ocb * 512:(ocb + 1) * 512],
                start=True, stop=True),
                waits=[("sV", e1)] if ocb == 0 else [("sV", c.c("sV"))], inc="sT")
            vcp0 = c.op("V", lambda e: e.tensor_copy(rs_row_bf[:], ps_dots[0:1, :]),
                        waits=[("sT", tb)], inc="sV")
            tb2 = c.op("T", lambda e: e.matmul(
                ps_dots[:, :], ones_sb[0:1, :], rs_row_bf[:], start=True, stop=True),
                waits=[("sV", vcp0)], inc="sT")
            vcp1 = c.op("V", lambda e: e.tensor_copy(rs_stage[:], ps_dots[:, :]),
                        waits=[("sT", tb2), ("dDg", c.c("dDg"))], inc="sV")
            c.dma("G", rs_bc_scr.ap()[:, ocb * 512:(ocb + 1) * 512], rs_stage[:],
                  "dDg", waits=[("sV", vcp1)])
        wt1w = [("dDg", c.c("dDg")), ("sV", c.c("sV")), ("sA", c.c("sA")),
                ("sT", c.c("sT"))]
        for ch in range(32):
            c.op("Y", lambda e, ch=ch: e.dma_start(
                out=wT_sb[1][:, ch, :],
                in_=w_bf_scr.ap()[512:1024, ch * 128:(ch + 1) * 128],
                transpose=True),
                waits=wt1w if ch == 0 else [], inc="dWT1", inc_n=16)

        d_last_mm = {}
        dslot_v = {}
        x1b_free = {}
        outst_free = {}
        rs_used = {}

        def load_wT(oc, queue):
            par = oc % 2
            wY = [("dDg", restage_done)]
            if oc >= 2:
                wY.append(("sT", d_last_mm[oc - 2]))
            for ch in range(32):
                c.op(queue, lambda e, oc=oc, ch=ch, par=par: e.dma_start(
                    out=wT_sb[par][:, ch, :],
                    in_=w_bf_scr.ap()[oc * 512:(oc + 1) * 512, ch * 128:(ch + 1) * 128],
                    transpose=True),
                    waits=wY if ch == 0 else [], inc=f"dWT{par}", inc_n=16)

        def load_rs(oc):
            par = oc % 2
            wG = [("dDg", c.c("dDg"))]
            if par in rs_used:
                wG.append(("sV", rs_used[par]))
            if oc >= 2:
                wG.append(("sT", d_last_mm[oc - 2]))
            c.dma("G", rs_bc[par][:], rs_bc_scr.ap()[:, oc * 512:(oc + 1) * 512],
                  f"dWF{par}", waits=wG)
            c.dma("G", biasb_oc[par][:], ffn_b[0:1, oc * 512:(oc + 1) * 512],
                  f"dWF{par}")

        load_rs(0)
        load_rs(1)
        for oc in range(8):
            par = oc % 2
            if oc >= 1 and oc + 1 < 8:
                load_wT(oc + 1, "Y")
            wt_ready = c.c(f"dWT{par}")
            for t in range(nT):
                slot = (oc * nT + t) % 2
                sslot = (oc * nT + t) % 2
                wmm = []
                if t == 0:
                    wmm.append((f"dWT{par}", wt_ready))
                    wmm.append((f"dWF{par}", c.c(f"dWF{par}")))
                if oc == 0:
                    wmm.extend([({"V": "sV", "A": "sA"}[lncp2[(t, ch)][0]],
                                 lncp2[(t, ch)][1]) for ch in (30, 31)])
                if slot in dslot_v:
                    wmm.append(("sV", dslot_v[slot]))
                for ch in range(32):
                    c.op("T", lambda e, ch=ch, t=t, slot=slot, par=par: e.matmul(
                        ps_d[slot][:],
                        ln2_fm[:, ch * TOKW + t * 128: ch * TOKW + (t + 1) * 128],
                        wT_sb[par][:, ch, :],
                        start=(ch == 0), stop=False),
                        waits=wmm if ch == 0 else [], inc=None)
                last = c.op("T", lambda e, par=par, slot=slot: e.matmul(
                    ps_d[slot][:], ones_sb[0:1, 0:128], biasb_oc[par][:],
                    start=False, stop=True), inc="sT")
                d_last_mm[oc] = last
                wx1 = [("sV", x1b_free[sslot])] if sslot in x1b_free else []
                c.dma("Y", x1s_stage[sslot][:],
                      x1_scr.ap()[t * 128:(t + 1) * 128, oc * 512:(oc + 1) * 512],
                      f"dX1B{sslot}", waits=wx1)
                x1_ready = c.c(f"dX1B{sslot}")
                wmsk = [("sT", last)]
                if sslot in outst_free:
                    wmsk.append((f"dOUT{sslot}", outst_free[sslot]))
                v0_ = c.op("V", lambda e, t=t, slot=slot, sslot=sslot, par=par: e.scalar_tensor_tensor(
                    out_stage[sslot][:], ps_d[slot][:], qs_all[:, t:t + 1],
                    rs_bc[par][:], ALU.mult, ALU.mult),
                    waits=wmsk, inc="sV")
                dslot_v[slot] = v0_
                rs_used[par] = v0_
                v1_ = c.op("V", lambda e, sslot=sslot: e.tensor_tensor(
                    out_stage[sslot][:], out_stage[sslot][:], x1s_stage[sslot][:],
                    ALU.add), waits=[("sV", v0_), (f"dX1B{sslot}", x1_ready)], inc="sV")
                x1b_free[sslot] = v1_
                do = c.dma("Y", out[t * 128:(t + 1) * 128, oc * 512:(oc + 1) * 512],
                           out_stage[sslot][:], f"dOUT{sslot}", waits=[("sV", v1_)])
                outst_free[sslot] = do
            if oc + 2 < 8:
                load_rs(oc + 2)
        c.wait("Y", ("dOUT0", c.c("dOUT0")), ("dOUT1", c.c("dOUT1")))
        c.barrier(self_waits={"Y": [("dOUT0", c.c("dOUT0")), ("dOUT1", c.c("dOUT1")),
                                    ("dX1B0", c.c("dX1B0")), ("dX1B1", c.c("dX1B1")),
                                    ("dWT0", c.c("dWT0")), ("dWT1", c.c("dWT1"))],
                              "G": [("dWF0", c.c("dWF0")), ("dWF1", c.c("dWF1"))]})

    with nc.Block() as block:
        @block.tensor
        def _(eng):
            emit(Ctx(nc, "T", engines, sems))

        @block.vector
        def _(eng):
            emit(Ctx(nc, "V", engines, sems))

        @block.scalar
        def _(eng):
            emit(Ctx(nc, "A", engines, sems))

        @block.gpsimd
        def _(eng):
            emit(Ctx(nc, "G", engines, sems))

        @block.sync
        def _(eng):
            emit(Ctx(nc, "Y", engines, sems))

    return nc


def kernel(L0_attn_hidden_states, L0_attn_w_qkv, L0_attn_w_o, L0_ffn_weight,
           L0_ffn_bias, L0_ffn_centroids, L0_ffn_assignments):
    x = np.ascontiguousarray(np.asarray(L0_attn_hidden_states, dtype=np.float32))
    wq = np.ascontiguousarray(np.asarray(L0_attn_w_qkv, dtype=np.float32))
    wo = np.ascontiguousarray(np.asarray(L0_attn_w_o, dtype=np.float32))
    fw = np.ascontiguousarray(np.asarray(L0_ffn_weight, dtype=np.float32))
    fb = np.ascontiguousarray(np.asarray(L0_ffn_bias, dtype=np.float32)).reshape(1, -1)
    ce = np.ascontiguousarray(np.asarray(L0_ffn_centroids, dtype=np.float32))
    asg = np.ascontiguousarray(np.asarray(L0_ffn_assignments, dtype=np.int32)).reshape(1, -1)

    B, S, H_ = x.shape
    assert B == N_CORES and H_ == H

    nc = build_nc(S=S)
    in_maps = [{"x0": x[b], "w_qkv": wq, "w_o": wo, "ffn_w": fw,
                "ffn_b": fb, "cent": ce, "assign": asg} for b in range(B)]
    res = run_bass_kernel_spmd(nc, in_maps, core_ids=list(range(N_CORES)))
    return np.stack([res.results[b]["out"] for b in range(B)], axis=0)


# version 16
# speedup vs baseline: 1.1583x; 1.0223x over previous
import math
from contextlib import ExitStack

import numpy as np
import ml_dtypes

import concourse.bass as bass
import concourse.mybir as mybir
from concourse.bass_utils import run_bass_kernel_spmd

F32 = mybir.dt.float32
BF16 = mybir.dt.bfloat16
I32 = mybir.dt.int32
AF = mybir.ActivationFunctionType
ALU = mybir.AluOpType
AX = mybir.AxisListType.X

H = 4096
QKV = 6144
NH, NKV, D = 32, 8, 128
G_HEADS = 4
NCLUST = 64
TEMP_INV = 10.0
THRESH = 0.01
EPS = 1e-5
SCALE = D ** -0.5
C_SHIFT = 20.0
BIG = 1.0e33

N_CORES = 8

ENGS = ("T", "V", "A", "G", "Y")


def _alibi_slopes(n):
    cp2 = 2 ** math.floor(math.log2(n))
    base = 2.0 ** (-8.0 / cp2)
    slopes = base ** np.arange(1, cp2 + 1, dtype=np.float32)
    if cp2 != n:
        extra_base = 2.0 ** (-4.0 / cp2)
        extra = extra_base ** np.arange(1, 1 + 2 * (n - cp2), 2, dtype=np.float32)
        slopes = np.concatenate([slopes, extra])
    return slopes.astype(np.float32)


SLOPES = _alibi_slopes(NH)


def _sd_all(nq):
    ntile = 4 * nq + 4
    out = np.full((128, ntile * 128), BIG, dtype=np.float32)
    qr = np.arange(128)
    kr = np.arange(128)[:, None]
    for didx in range(ntile):
        delta = didx - 3
        rel = delta * 128 + qr[None, :] - kr
        blk = np.where(rel >= 0, np.sqrt(np.maximum(rel, 0).astype(np.float64)), BIG)
        out[:, didx * 128:(didx + 1) * 128] = blk.astype(np.float32)
    return out



def _copy(e, dst, src):
    if isinstance(e, bass.BassScalarEngine):
        return e.copy(dst, src)
    return e.tensor_copy(dst, src)

class Ctx:

    def __init__(self, nc, me, engines, sems):
        self.nc = nc
        self.me = me
        self.engines = engines
        self.sems = sems
        self.cnt = {k: 0 for k in sems}
        self.bar_k = 0

    def c(self, s):
        return self.cnt[s]

    def op(self, eng, fn, waits=(), inc=None, inc_n=1):
        if self.me == eng:
            e = self.engines[eng]
            for sname, val in waits:
                if val > 0:
                    e.wait_ge(self.sems[sname], val)
            inst = fn(e)
            if inc is not None:
                inst.then_inc(self.sems[inc], inc_n)
        if inc is not None:
            self.cnt[inc] += inc_n
        return self.cnt[inc] if inc is not None else None

    def wait(self, eng, *waits):
        if self.me == eng:
            e = self.engines[eng]
            for sname, val in waits:
                if val > 0:
                    e.wait_ge(self.sems[sname], val)

    def dma(self, eng, out, in_, sem, waits=()):
        return self.op(eng, lambda e: e.dma_start(out=out, in_=in_),
                       waits=waits, inc=sem, inc_n=16)

    def barrier(self, self_waits=None):
        self.bar_k += 1
        target = 5 * self.bar_k
        own = {"T": "sT", "V": "sV", "A": "sA", "G": "sG"}
        for eng in ENGS:
            w = list((self_waits or {}).get(eng, ()))
            if eng in own and self.cnt[own[eng]] > 0:
                w.append((own[eng], self.cnt[own[eng]]))
            self.op(eng, lambda e: e.nop(), waits=w, inc="bar")
        for eng in ENGS:
            self.wait(eng, ("bar", target))


def build_nc(S=2048):
    nT = S // 128
    nQ = S // 512
    TOKW = S
    ND = 4 * nQ + 4

    nc = bass.Bass("TRN2", target_bir_lowering=False)

    x0 = nc.dram_tensor("x0", [S, H], F32, kind="ExternalInput").ap()
    w_qkv = nc.dram_tensor("w_qkv", [H, QKV], F32, kind="ExternalInput").ap()
    w_o = nc.dram_tensor("w_o", [H, H], F32, kind="ExternalInput").ap()
    ffn_w = nc.dram_tensor("ffn_w", [H, H], F32, kind="ExternalInput").ap()
    ffn_b = nc.dram_tensor("ffn_b", [1, H], F32, kind="ExternalInput").ap()
    cent = nc.dram_tensor("cent", [NCLUST, H], F32, kind="ExternalInput").ap()
    assign = nc.dram_tensor("assign", [1, H], I32, kind="ExternalInput").ap()
    out = nc.dram_tensor("out", [S, H], F32, kind="ExternalOutput").ap()

    wqkv_r = w_qkv.rearrange("(c p) n -> p c n", p=128)
    wo_r = w_o.rearrange("(c p) n -> p c n", p=128)

    q_scr = nc.dram_tensor("q_scr", [NH, 128, S], BF16)
    k_scr = nc.dram_tensor("k_scr", [NKV, 128, S], BF16)
    v_scr = nc.dram_tensor("v_scr", [NKV, 128, S], BF16)
    x1_scr = nc.dram_tensor("x1_scr", [S, H], F32)
    w_bf_scr = nc.dram_tensor("w_bf_scr", [H, H], BF16)
    rs_bc_scr = nc.dram_tensor("rs_bc_scr", [128, H], BF16)
    cc_in = nc.dram_tensor("cc_in", [NCLUST, 1], F32)
    cc_out = nc.dram_tensor("cc_out", [NCLUST, 1], F32)

    ident_bf = nc.inline_tensor(np.eye(128, dtype=ml_dtypes.bfloat16), "ident_bf")
    ident_f = nc.inline_tensor(np.eye(128, dtype=np.float32), "ident_f")
    sd_const = nc.inline_tensor(_sd_all(nQ), "sd_const")
    ones_const = nc.inline_tensor(np.ones((1, 128), dtype=ml_dtypes.bfloat16), "ones_const")
    iota_const = nc.inline_tensor(
        np.arange(NCLUST, dtype=np.float32).reshape(NCLUST, 1), "iota_const")
    onescol_const = nc.inline_tensor(np.ones((128, 1), dtype=ml_dtypes.bfloat16), "onescol_const")
    ones128_const = nc.inline_tensor(np.ones((128, 128), dtype=ml_dtypes.bfloat16), "ones128_const")
    onesf_const = nc.inline_tensor(np.ones((1, 128), dtype=np.float32), "onesf_const")

    ident_sb = nc.alloc_sbuf_tensor("ident_sb", [128, 128], BF16)
    ones_sb = nc.alloc_sbuf_tensor("ones_sb", [1, 128], BF16)
    iota_sb = nc.alloc_sbuf_tensor("iota_sb", [NCLUST, 1], F32)
    qs_all = nc.alloc_sbuf_tensor("qs_all", [128, nT], F32)
    lnstat = nc.alloc_sbuf_tensor("lnstat", [128, 8], F32)
    eps_sb = nc.alloc_sbuf_tensor("eps_sb", [128, 1], F32)
    onescol_sb = nc.alloc_sbuf_tensor("onescol_sb", [128, 1], BF16)
    ones128_sb = nc.alloc_sbuf_tensor("ones128_sb", [128, 128], BF16)
    onesf_sb = nc.alloc_sbuf_tensor("onesf_sb", [1, 128], F32)

    esA = ExitStack()
    ln1_fm = esA.enter_context(nc.sbuf_tensor("ln1_fm", [128, 32 * TOKW], BF16))
    xbuf = [esA.enter_context(nc.sbuf_tensor(f"xbuf{i}", [128, H], F32)) for i in range(2)]
    ln1_tm = [esA.enter_context(nc.sbuf_tensor(f"ln1_tm{i}", [128, H], BF16)) for i in range(2)]
    w_sbA = [esA.enter_context(nc.sbuf_tensor(f"w_sbA{i}", [128, 32, 128], BF16)) for i in range(2)]
    qkv_stage = [esA.enter_context(nc.sbuf_tensor(f"qkv_st{i}", [128, 512], BF16)) for i in range(2)]
    vstage = esA.enter_context(nc.sbuf_tensor("vstage", [128, 512], BF16))
    vtr_stage = [esA.enter_context(nc.sbuf_tensor(f"vtr_st{i}", [128, 512], BF16)) for i in range(2)]
    psA = ExitStack()
    ps_tr = [psA.enter_context(nc.psum_tensor(f"ps_tr{i}", [128, 128], BF16)) for i in range(4)]
    ps_qkv = [psA.enter_context(nc.psum_tensor(f"ps_qkv{i}", [128, 512], F32)) for i in range(4)]
    psA.close()
    esA.close()

    esBC = ExitStack()
    attn_fm = esBC.enter_context(nc.sbuf_tensor("attn_fm", [128, 32 * TOKW], BF16))
    esB = ExitStack()
    sd_sb = esB.enter_context(nc.sbuf_tensor("sd_sb", [128, ND * 128], F32))
    k_sb = [esB.enter_context(nc.sbuf_tensor(f"k_sb{i}", [128, S], BF16)) for i in range(2)]
    v_sb = [esB.enter_context(nc.sbuf_tensor(f"v_sb{i}", [128, S], BF16)) for i in range(2)]
    p_all = esB.enter_context(nc.sbuf_tensor("p_all", [128, nT * 512], BF16))
    q_tile = [esB.enter_context(nc.sbuf_tensor(f"q_tile{i}", [128, 512], BF16)) for i in range(2)]
    rec_row = esB.enter_context(nc.sbuf_tensor("rec_row", [1, 512], F32))
    rec_bc = [esB.enter_context(nc.sbuf_tensor(f"rec_bc{i}", [128, 512], F32)) for i in range(2)]
    psB = ExitStack()
    ps_s = [psB.enter_context(nc.psum_tensor(f"ps_s{i}", [128, 512], F32)) for i in range(4)]
    ps_avf = [psB.enter_context(nc.psum_tensor(f"ps_avf{i}", [128, 512], F32)) for i in range(2)]
    ps_dbc = [psB.enter_context(nc.psum_tensor(f"ps_dbc{i}", [128, 512], F32)) for i in range(2)]
    psB.close()
    esB.close()
    esC = ExitStack()
    wo_sb = [esC.enter_context(nc.sbuf_tensor(f"wo_sb{i}", [128, 32, 512], BF16)) for i in range(2)]
    x0_stage = [esC.enter_context(nc.sbuf_tensor(f"x0_st{i}", [128, 512], F32)) for i in range(2)]
    x1_stage = [esC.enter_context(nc.sbuf_tensor(f"x1_st{i}", [128, 512], F32)) for i in range(2)]
    psC = ExitStack()
    ps_x1 = [psC.enter_context(nc.psum_tensor(f"ps_x1{i}", [128, 512], F32)) for i in range(3)]
    psC.close()
    esC.close()
    esBC.close()

    esR = ExitStack()
    cent_T = esR.enter_context(nc.sbuf_tensor("cent_T", [128, 32 * NCLUST], BF16, side="right"))
    a1_sb = esR.enter_context(nc.sbuf_tensor("a1_sb", [NCLUST, H], BF16, side="right"))
    esD0 = ExitStack()
    cent_sb = esD0.enter_context(nc.sbuf_tensor("cent_sb", [NCLUST, H], BF16))
    asg_i = esD0.enter_context(nc.sbuf_tensor("asg_i", [1, H], I32))
    asg_f = esD0.enter_context(nc.sbuf_tensor("asg_f", [1, H], F32))
    asg_bf = esD0.enter_context(nc.sbuf_tensor("asg_bf", [1, H], BF16))
    esD0.close()
    esD = ExitStack()
    ln2_fm = esD.enter_context(nc.sbuf_tensor("ln2_fm", [128, 32 * TOKW], BF16))
    esD1 = ExitStack()
    x1buf = [esD1.enter_context(nc.sbuf_tensor(f"x1buf{i}", [128, H], F32)) for i in range(2)]
    ln2_tm = [esD1.enter_context(nc.sbuf_tensor(f"ln2_tm{i}", [128, H], BF16)) for i in range(2)]
    identf_sb = esD1.enter_context(nc.sbuf_tensor("identf_sb", [128, 128], F32))
    dots_e = esD1.enter_context(nc.sbuf_tensor("dots_e", [128, NCLUST], F32))
    dots_n = esD1.enter_context(nc.sbuf_tensor("dots_n", [128, NCLUST], F32))
    dstat = esD1.enter_context(nc.sbuf_tensor("dstat", [128, 8], F32))
    cmax_t = esD1.enter_context(nc.sbuf_tensor("cmax_t", [NCLUST, 1], F32))
    run_cmax = esD1.enter_context(nc.sbuf_tensor("run_cmax", [NCLUST, 1], F32))
    gmax_sb = esD1.enter_context(nc.sbuf_tensor("gmax_sb", [NCLUST, 1], F32))
    cs_f = esD1.enter_context(nc.sbuf_tensor("cs_f", [NCLUST, 1], F32))
    cs_bf = esD1.enter_context(nc.sbuf_tensor("cs_bf", [NCLUST, 1], BF16))
    rs_row_bf = esD1.enter_context(nc.sbuf_tensor("rs_row_bf", [1, 512], BF16))
    rs_stage = esD1.enter_context(nc.sbuf_tensor("rs_stage", [128, 512], BF16))
    esD1.close()
    esR.close()
    esD3 = ExitStack()
    wT_sb = [esD3.enter_context(nc.sbuf_tensor(f"wT_sb{i}", [128, 32, 512], BF16)) for i in range(2)]
    rs_bc = [esD3.enter_context(nc.sbuf_tensor(f"rs_bc{i}", [128, 512], BF16)) for i in range(2)]
    biasb_oc = [esD3.enter_context(nc.sbuf_tensor(f"biasb_oc{i}", [1, 512], BF16)) for i in range(2)]
    x1s_stage = [esD3.enter_context(nc.sbuf_tensor(f"x1s_st{i}", [128, 512], F32)) for i in range(2)]
    out_stage = [esD3.enter_context(nc.sbuf_tensor(f"out_st{i}", [128, 512], F32)) for i in range(2)]
    esD3.close()
    esD.close()
    psD = ExitStack()
    ps_trD = [psD.enter_context(nc.psum_tensor(f"ps_trD{i}", [128, 128], BF16)) for i in range(4)]
    ps_dots = psD.enter_context(nc.psum_tensor("ps_dots", [128, 512], F32))
    ps_dt = psD.enter_context(nc.psum_tensor("ps_dt", [NCLUST, 128], F32))
    ps_d = [psD.enter_context(nc.psum_tensor(f"ps_d{i}", [128, 512], F32)) for i in range(2)]
    psD.close()

    sem_names = ["bar", "sT", "sV", "sA", "sG", "dP", "dD", "dCC",
                 "dDg", "dXA0", "dXA1", "dW0", "dW1", "dSQ0", "dSQ1", "dSV0", "dSV1",
                 "dKV0", "dKV1", "dQL0", "dQL1", "dWO0", "dWO1", "dX00", "dX01",
                 "dX1S0", "dX1S1", "dX1L0", "dX1L1", "dWF0", "dWF1", "dWT0", "dWT1",
                 "dWB0", "dWB1",
                 "dX1B0", "dX1B1", "dOUT0", "dOUT1"]
    sems = {n: nc.alloc_semaphore(name=n) for n in sem_names}

    engines = {"T": nc.tensor, "V": nc.vector, "A": nc.scalar,
               "G": nc.gpsimd, "Y": nc.sync}

    def emit(c: Ctx):
        c.dma("Y", ident_sb[:], ident_bf.ap(), "dP")
        c.dma("Y", ones_sb[:], ones_const.ap(), "dP")
        c.dma("Y", iota_sb[:], iota_const.ap(), "dP")
        c.dma("Y", onescol_sb[:], onescol_const.ap(), "dP")
        c.dma("Y", ones128_sb[:], ones128_const.ap(), "dP")
        c.dma("Y", onesf_sb[:], onesf_const.ap(), "dP")
        c.op("V", lambda e: e.memset(eps_sb[:], EPS), inc="sV")
        p0_done = {"Y": [("dP", c.c("dP"))]}
        c.barrier(self_waits=p0_done)

        ln_free_v = {}
        ln_free_a = {}
        lncp = {}
        tr_mark = {}
        var_mark = {}
        for t in range(nT):
            w = []
            if t >= 2:
                w = [("sV", ln_free_v[t - 2]), ("sA", ln_free_a[t - 2])]
            xa = c.dma("Y", xbuf[t % 2][:], x0[t * 128:(t + 1) * 128, :],
                       f"dXA{t % 2}", waits=w)
            xb = xbuf[t % 2]
            lt = ln1_tm[t % 2]
            m0 = c.op("V", lambda e, xb=xb: e.reduce_sum(lnstat[:, 0:1], xb[:], axis=AX),
                      waits=[(f"dXA{t % 2}", xa)], inc="sV")
            wsq = [(f"dXA{t % 2}", xa)]
            if t >= 1:
                wsq.append(("sV", var_mark[t - 1]))
            if t >= 2:
                wsq.append(("sT", tr_mark[(t - 2, 31)]))
            a0 = c.op("A", lambda e, xb=xb, lt=lt: e.activation(
                out=lt[:], in_=xb[:], func=AF.Square, accum_out=lnstat[:, 2:3]),
                waits=wsq, inc="sA")
            ln_free_a[t] = a0
            m1 = c.op("V", lambda e: e.tensor_scalar(lnstat[:, 1:2], lnstat[:, 0:1],
                                                     1.0 / H, None, ALU.mult),
                      waits=[("sV", m0)], inc="sV")
            m1b = c.op("V", lambda e: e.tensor_scalar(lnstat[:, 5:6], lnstat[:, 1:2],
                                                      lnstat[:, 1:2], None, ALU.mult),
                      waits=[("sV", m1)], inc="sV")
            m1c = c.op("V", lambda e: e.tensor_scalar(lnstat[:, 6:7], lnstat[:, 2:3],
                                                      1.0 / H, lnstat[:, 5:6],
                                                      ALU.mult, ALU.subtract),
                      waits=[("sV", m1b), ("sA", a0)], inc="sV")
            var_mark[t] = m1c
            a1 = c.op("A", lambda e: e.activation(out=lnstat[:, 3:4], in_=lnstat[:, 6:7],
                                                  func=AF.Ln, bias=eps_sb[:]),
                      waits=[("sV", m1c)], inc="sA")
            a2 = c.op("A", lambda e: e.activation(out=lnstat[:, 4:5], in_=lnstat[:, 3:4],
                                                  func=AF.Exp, scale=-0.5),
                      waits=[("sA", a1)], inc="sA")
            m3 = c.op("V", lambda e, xb=xb, lt=lt: e.tensor_scalar(
                lt[:], xb[:], lnstat[:, 1:2], lnstat[:, 4:5], ALU.subtract, ALU.mult),
                waits=[("sA", a2)], inc="sV")
            ln_free_v[t] = m3
            for ch in range(32):
                wt = [("sV", m3)] if ch == 0 else []
                if (t, ch - 4) in lncp:
                    e_, cnt_ = lncp[(t, ch - 4)]
                    wt.append(("sV" if e_ == "V" else "sA", cnt_))
                elif t >= 1 and ch < 4:
                    e_, cnt_ = lncp[(t - 1, 28 + ch)]
                    wt.append(("sV" if e_ == "V" else "sA", cnt_))
                tm = c.op("T", lambda e, ch=ch, lt=lt: e.transpose(
                    ps_tr[ch % 4][:], lt[:, ch * 128:(ch + 1) * 128], ident_sb[:]),
                    waits=wt, inc="sT")
                tr_mark[(t, ch)] = tm
                ceng = "V" if (ch % 2 == 0) else "A"
                dst = ln1_fm[:, ch * TOKW + t * 128: ch * TOKW + (t + 1) * 128]
                cm = c.op(ceng, lambda e, dst=dst, ch=ch: _copy(e, dst, ps_tr[ch % 4][:]),
                          waits=[("sT", tm)], inc="sV" if ceng == "V" else "sA")
                lncp[(t, ch)] = (ceng, cm)

        NJG = QKV // 128
        grp_last_mm = {}
        slot_free = {}
        stage_free = {}
        vtr_free = {}
        ps_tr_cons = {sub: lncp[(nT - 1, 28 + sub)] for sub in range(4)}
        ln_chunk_ready = {n: [({"V": "sV", "A": "sA"}[lncp[(4 * n + 3, ch)][0]],
                               lncp[(4 * n + 3, ch)][1]) for ch in (30, 31)]
                          for n in range(nQ)}
        for jg in range(NJG):
            wv = []
            if jg >= 2:
                wv.append(("sT", grp_last_mm[jg - 2]))
            wld = c.dma("G", w_sbA[jg % 2][:], wqkv_r[:, :, jg * 128:(jg + 1) * 128],
                        f"dW{jg % 2}", waits=wv)
            if jg < 32:
                c.dma("G", w_bf_scr.ap()[jg * 128:(jg + 1) * 128, :],
                      ffn_w[jg * 128:(jg + 1) * 128, :], "dDg")
            j_abs = jg
            for n in range(nQ):
                slot = n % 4
                wmm = [(f"dW{jg % 2}", wld)]
                if jg < 2:
                    wmm.extend(ln_chunk_ready[n])
                if slot in slot_free:
                    e_, cnt_ = slot_free[slot]
                    wmm.append(({"V": "sV", "A": "sA", "T": "sT"}[e_], cnt_))
                last = None
                for ch in range(32):
                    last = c.op("T", lambda e, ch=ch, n=n, jg=jg, slot=slot: e.matmul(
                        ps_qkv[slot][:],
                        w_sbA[jg % 2][:, ch, :],
                        ln1_fm[:, ch * TOKW + n * 512: ch * TOKW + (n + 1) * 512],
                        start=(ch == 0), stop=(ch == 31)),
                        waits=wmm if ch == 0 else [],
                        inc="sT" if ch == 31 else None)
                grp_last_mm[jg] = last
                if j_abs < NH:
                    st = n % 2
                    wep = [("sT", last)]
                    if st in stage_free:
                        wep.append((f"dSQ{st}", stage_free[st]))
                    acp = c.op("A", lambda e, st=st, slot=slot: e.activation(
                        out=qkv_stage[st][:], in_=ps_qkv[slot][:], func=AF.Copy,
                        scale=SCALE),
                        waits=wep, inc="sA")
                    slot_free[slot] = ("A", acp)
                    dsp = c.dma("Y", q_scr.ap()[j_abs, :, n * 512:(n + 1) * 512],
                                qkv_stage[st][:], f"dSQ{st}", waits=[("sA", acp)])
                    stage_free[st] = dsp
                elif j_abs < NH + NKV:
                    hk = j_abs - NH
                    st = n % 2
                    wep = [("sT", last)]
                    if st in stage_free:
                        wep.append((f"dSQ{st}", stage_free[st]))
                    acp = c.op("A", lambda e, st=st, slot=slot: e.activation(
                        out=qkv_stage[st][:], in_=ps_qkv[slot][:], func=AF.Copy),
                        waits=wep, inc="sA")
                    slot_free[slot] = ("A", acp)
                    dsp = c.dma("Y", k_scr.ap()[hk, :, n * 512:(n + 1) * 512],
                                qkv_stage[st][:], f"dSQ{st}", waits=[("sA", acp)])
                    stage_free[st] = dsp
                else:
                    hk = j_abs - NH - NKV
                    wep = [("sT", last)]
                    if "vstage" in slot_free:
                        wep.append(("sT", slot_free["vstage"][1]))
                    vcp = c.op("A", lambda e, slot=slot: e.activation(
                        out=vstage[:], in_=ps_qkv[slot][:], func=AF.Copy),
                        waits=wep, inc="sA")
                    slot_free[slot] = ("A", vcp)
                    vslot = n % 2
                    trs = []
                    for sub in range(4):
                        wtr = [("sA", vcp)] if sub == 0 else []
                        if sub in ps_tr_cons:
                            e_, cnt_ = ps_tr_cons[sub]
                            wtr = wtr + [("sV" if e_ == "V" else "sA", cnt_)]
                        tm = c.op("T", lambda e, sub=sub: e.transpose(
                            ps_tr[sub % 4][:], vstage[:, sub * 128:(sub + 1) * 128],
                            ident_sb[:]), waits=wtr, inc="sT")
                        trs.append(tm)
                    slot_free["vstage"] = ("T", trs[-1])
                    wcp_extra = [(f"dSV{vslot}", vtr_free[vslot])] if vslot in vtr_free else []
                    vcps = []
                    for sub in range(4):
                        ceng = "V" if sub % 2 == 0 else "A"
                        cm = c.op(ceng, lambda e, sub=sub, vslot=vslot: _copy(e, vtr_stage[vslot][:, sub * 128:(sub + 1) * 128], ps_tr[sub % 4][:]),
                            waits=[("sT", trs[sub])] + (wcp_extra if sub < 2 else []),
                            inc="sV" if ceng == "V" else "sA")
                        vcps.append((ceng, cm))
                        ps_tr_cons[sub] = (ceng, cm)
                    wd = [({"V": "sV", "A": "sA"}[e_], cnt_) for e_, cnt_ in vcps[-2:]]
                    dsp = c.dma("Y", v_scr.ap()[hk, :, n * 512:(n + 1) * 512],
                                vtr_stage[vslot][:], f"dSV{vslot}", waits=wd)
                    vtr_free[vslot] = dsp
        c.barrier(self_waits={"G": [("dW0", c.c("dW0")), ("dW1", c.c("dW1"))],
                              "Y": [("dSQ0", c.c("dSQ0")), ("dSQ1", c.c("dSQ1")),
                                    ("dSV0", c.c("dSV0")), ("dSV1", c.c("dSV1"))]})

        c.dma("Y", sd_sb[:], sd_const.ap(), "dP")
        sd_ready = c.c("dP")

        kv_last_use = {}
        q_last = {}
        avf_free = {}
        den_free = {}
        prev_exp = {}
        pall_av_last = None
        pend_ep = [None]

        def flush_ep1():
            ep = pend_ep[0]
            if ep is None:
                return
            hp, np_, par, avf_stop, den_stop = ep
            wln = [("sT", den_stop)]
            if par in avf_free:
                wln.append(("sV", avf_free[par]))
            ra = c.op("A", lambda e, par=par: e.activation(
                out=rec_bc[par][:], in_=ps_dbc[par][:, :], func=AF.Ln),
                waits=wln, inc="sA")
            den_free[par] = ("sA", ra)
            r0 = c.op("A", lambda e, par=par: e.activation(
                out=rec_bc[par][:], in_=rec_bc[par][:], func=AF.Exp, scale=-1.0),
                waits=[("sA", ra)], inc="sA")
            pend_ep[0] = (hp, np_, par, avf_stop, r0)

        def flush_ep2():
            ep = pend_ep[0]
            if ep is None:
                return
            pend_ep[0] = None
            hp, np_, par, avf_stop, r0 = ep
            dst = attn_fm[:, hp * TOKW + np_ * 512: hp * TOKW + (np_ + 1) * 512]
            tt = c.op("V", lambda e, dst=dst, par=par: e.tensor_tensor(
                dst, ps_avf[par][:], rec_bc[par][:], ALU.mult),
                waits=[("sT", avf_stop), ("sA", r0)], inc="sV")
            avf_free[par] = tt

        chunks = [(hk, g, n) for hk in range(NKV) for g in range(G_HEADS)
                  for n in range(nQ)]
        q_rdy = {}

        def q_load(ci):
            hk, g, n = chunks[ci]
            h = hk * G_HEADS + g
            qslot = ci % 2
            wq = []
            if qslot in q_last:
                wq.append(("sT", q_last[qslot]))
            c.dma("Y", q_tile[qslot][:], q_scr.ap()[h, :, n * 512:(n + 1) * 512],
                  f"dQL{qslot}", waits=wq)
            q_rdy[ci] = c.c(f"dQL{qslot}")

        kv_rdy = {}
        for ci, (hk, g, n) in enumerate(chunks):
            if g == 0 and n == 0:
                wkv = []
                if hk >= 2:
                    wkv.append(("sT", kv_last_use[hk % 2]))
                c.dma("Y", k_sb[hk % 2][:], k_scr.ap()[hk, :, :], f"dKV{hk % 2}",
                      waits=wkv)
                c.dma("Y", v_sb[hk % 2][:], v_scr.ap()[hk, :, :], f"dKV{hk % 2}")
                kv_rdy[hk] = c.c(f"dKV{hk % 2}")
            kv_sem = f"dKV{hk % 2}"
            kv_ready = kv_rdy[hk]
            if ci == 0:
                q_load(0)
            h = hk * G_HEADS + g
            slope = -float(SLOPES[h])
            qslot = ci % 2
            q_sem = f"dQL{qslot}"
            q_ready = q_rdy[ci]
            par = ci % 2
            nkt = 4 * n + 4

            def score(kt):
                wsc = [(kv_sem, kv_ready), (q_sem, q_ready)]
                if (kt % 4) in prev_exp:
                    wsc.append(("sA", prev_exp[kt % 4]))
                sm = c.op("T", lambda e, kt=kt, hk=hk, qslot=qslot: e.matmul(
                    ps_s[kt % 4][:], k_sb[hk % 2][:, kt * 128:(kt + 1) * 128],
                    q_tile[qslot][:], start=True, stop=True),
                    waits=wsc, inc="sT")
                q_last[qslot] = sm
                kv_last_use[hk % 2] = sm
                return sm

            sms = {}
            for kt0 in range(min(3, nkt)):
                sms[kt0] = score(kt0)
            for kt in range(nkt):
                if kt + 3 < nkt:
                    sms[kt + 3] = score(kt + 3)
                didx = 4 * n - kt + 3
                wst = [("sT", sms[kt])]
                if kt == 0:
                    wst.append(("dP", sd_ready))
                tt = c.op("V", lambda e, kt=kt, didx=didx, slope=slope: e.scalar_tensor_tensor(
                    ps_s[kt % 4][:], sd_sb[:, didx * 128: didx * 128 + 512],
                    slope, ps_s[kt % 4][:], ALU.mult, ALU.add),
                    waits=wst, inc="sV")
                wex = [("sV", tt)]
                if kt == 0 and pall_av_last is not None:
                    wex.append(("sT", pall_av_last))
                em = c.op("A", lambda e, kt=kt: e.activation(
                    out=p_all[:, kt * 512:(kt + 1) * 512], in_=ps_s[kt % 4][:],
                    func=AF.Exp), waits=wex, inc="sA")
                prev_exp[kt % 4] = em
                if kt == 0 and ci + 1 < len(chunks):
                    q_load(ci + 1)
                if kt == 1:
                    flush_ep1()
                if kt == 3:
                    flush_ep2()
                wav = [("sA", em)]
                if kt == 0 and par in avf_free:
                    wav.append(("sV", avf_free[par]))
                c.op("T", lambda e, kt=kt, hk=hk, par=par, nkt=nkt: e.matmul(
                    ps_avf[par][:], v_sb[hk % 2][:, kt * 128:(kt + 1) * 128],
                    p_all[:, kt * 512:(kt + 1) * 512],
                    start=(kt == 0), stop=(kt == nkt - 1)),
                    waits=wav, inc="sT")
                avf_stop = c.c("sT")
                wdn = []
                if kt == 0 and par in den_free:
                    wdn.append(den_free[par])
                c.op("T", lambda e, kt=kt, par=par, nkt=nkt: e.matmul(
                    ps_dbc[par][:, :], ones128_sb[:], p_all[:, kt * 512:(kt + 1) * 512],
                    start=(kt == 0), stop=(kt == nkt - 1)),
                    waits=wdn, inc="sT")
                den_stop = c.c("sT")
                kv_last_use[hk % 2] = c.c("sT")
                if kt == nkt - 1:
                    pall_av_last = c.c("sT")
                    pend_ep[0] = (h, n, par, avf_stop, den_stop)
            if nkt == 4:
                pass
        flush_ep1()
        flush_ep2()
        c.barrier(self_waits={"Y": [("dKV0", c.c("dKV0")), ("dKV1", c.c("dKV1")),
                                    ("dQL0", c.c("dQL0")), ("dQL1", c.c("dQL1"))]})

        wo_last_mm = {}
        cslot_tt = {}
        x0s_free = {}
        x1s_free = {}
        x0_rdy = {}

        def c_x0(oc, t):
            sslot = (oc * nT + t) % 2
            wx0 = [("sV", x0s_free[sslot])] if sslot in x0s_free else []
            c.dma("Y", x0_stage[sslot][:],
                  x0[t * 128:(t + 1) * 128, oc * 512:(oc + 1) * 512], f"dX0{sslot}",
                  waits=wx0)
            x0_rdy[(oc, t)] = c.c(f"dX0{sslot}")

        for oc in range(8):
            wv = []
            if oc >= 2:
                wv.append(("sT", wo_last_mm[oc - 2]))
            wold = c.dma("G", wo_sb[oc % 2][:], wo_r[:, :, oc * 512:(oc + 1) * 512],
                         f"dWO{oc % 2}", waits=wv)
            for t in range(nT):
                slot = (oc * nT + t) % 3
                sslot = (oc * nT + t) % 2
                if oc == 0 and t == 0:
                    c_x0(0, 0)
                wmm = [(f"dWO{oc % 2}", wold)]
                if slot in cslot_tt:
                    wmm.append(("sV", cslot_tt[slot]))
                last = None
                for ch in range(32):
                    last = c.op("T", lambda e, ch=ch, oc=oc, t=t, slot=slot: e.matmul(
                        ps_x1[slot][:],
                        attn_fm[:, ch * TOKW + t * 128: ch * TOKW + (t + 1) * 128],
                        wo_sb[oc % 2][:, ch, :],
                        start=(ch == 0), stop=(ch == 31)),
                        waits=wmm if ch == 0 else [],
                        inc="sT" if ch == 31 else None)
                wo_last_mm[oc] = last
                if t + 1 < nT:
                    c_x0(oc, t + 1)
                elif oc + 1 < 8:
                    c_x0(oc + 1, 0)
                wtt = [("sT", last), (f"dX0{sslot}", x0_rdy[(oc, t)])]
                if sslot in x1s_free:
                    wtt.append((f"dX1S{sslot}", x1s_free[sslot]))
                vt = c.op("V", lambda e, slot=slot, sslot=sslot: e.tensor_tensor(
                    x1_stage[sslot][:], ps_x1[slot][:], x0_stage[sslot][:], ALU.add),
                    waits=wtt, inc="sV")
                cslot_tt[slot] = vt
                x0s_free[sslot] = vt
                d1 = c.dma("Y", x1_scr.ap()[t * 128:(t + 1) * 128, oc * 512:(oc + 1) * 512],
                           x1_stage[sslot][:], f"dX1S{sslot}", waits=[("sV", vt)])
                x1s_free[sslot] = d1
        c.barrier(self_waits={"G": [("dWO0", c.c("dWO0")), ("dWO1", c.c("dWO1")),
                                    ("dDg", c.c("dDg"))],
                              "Y": [("dX1S0", c.c("dX1S0")), ("dX1S1", c.c("dX1S1")),
                                    ("dX00", c.c("dX00")), ("dX01", c.c("dX01"))]})

        c.dma("G", cent_sb[:], cent, "dDg")
        c.dma("Y", asg_i[:], assign, "dD")
        dD0 = c.c("dD")
        dDg0 = c.c("dDg")
        v0 = c.op("V", lambda e: e.tensor_copy(asg_f[:], asg_i[:]),
                  waits=[("dD", dD0)], inc="sV")
        v1 = c.op("V", lambda e: e.tensor_copy(asg_bf[:], asg_f[:]),
                  waits=[("sV", v0)], inc="sV")
        for ocb in range(8):
            tb = c.op("T", lambda e, ocb=ocb: e.matmul(
                ps_d[0][:NCLUST, :], ones_sb[0:1, 0:NCLUST],
                asg_bf[0:1, ocb * 512:(ocb + 1) * 512], start=True, stop=True),
                waits=[("sV", v1)] if ocb == 0 else [("sV", c.c("sV"))], inc="sT")
            c.op("V", lambda e, ocb=ocb: e.tensor_scalar(
                a1_sb[:, ocb * 512:(ocb + 1) * 512], ps_d[0][:NCLUST, :],
                iota_sb[:], None, ALU.is_equal),
                waits=[("sT", tb)], inc="sV")
        ctp = {}
        for ch in range(32):
            wt = [("dDg", dDg0)] if ch == 0 else []
            if ch >= 4:
                e_, cnt_ = ctp[ch - 4]
                wt.append(("sA" if e_ == "A" else "sV", cnt_))
            tm = c.op("T", lambda e, ch=ch: e.transpose(
                ps_trD[ch % 4][:, 0:NCLUST],
                cent_sb[0:NCLUST, ch * 128:(ch + 1) * 128],
                ident_sb[0:NCLUST, 0:NCLUST]),
                waits=wt, inc="sT")
            ceng = "A" if ch % 2 == 0 else "V"
            cm = c.op(ceng, lambda e, ch=ch: _copy(e, cent_T[:, ch * NCLUST:(ch + 1) * NCLUST], ps_trD[ch % 4][:, 0:NCLUST]),
                waits=[("sT", tm)], inc="sA" if ceng == "A" else "sV")
            ctp[ch] = (ceng, cm)
        c.barrier(self_waits={"Y": [("dD", c.c("dD"))], "G": [("dDg", c.c("dDg"))]})

        c.dma("Y", identf_sb[:], ident_f.ap(), "dP")
        idf_ready = c.c("dP")
        restage_done = dDg0
        c.op("V", lambda e: e.memset(run_cmax[:], -1e30), inc="sV")
        ln_free_v.clear()
        ln_free_a.clear()
        var_mark.clear()
        lncp2 = {}
        tr_mark2 = {}
        dt_done = {}
        for t in range(nT):
            w = []
            if t >= 2:
                w = [("sV", ln_free_v[t - 2]), ("sA", ln_free_a[t - 2])]
            x1r = c.dma("Y", x1buf[t % 2][:], x1_scr.ap()[t * 128:(t + 1) * 128, :],
                        f"dX1L{t % 2}", waits=w)
            xb = x1buf[t % 2]
            lt = ln2_tm[t % 2]
            m0 = c.op("V", lambda e, xb=xb: e.reduce_sum(lnstat[:, 0:1], xb[:], axis=AX),
                      waits=[(f"dX1L{t % 2}", x1r)], inc="sV")
            wsq = [(f"dX1L{t % 2}", x1r)]
            if t >= 1:
                wsq.append(("sV", var_mark[t - 1]))
            if t >= 2:
                wsq.append(("sT", tr_mark2[(t - 2, 31)]))
            a0 = c.op("A", lambda e, xb=xb, lt=lt: e.activation(
                out=lt[:], in_=xb[:], func=AF.Square, accum_out=lnstat[:, 2:3]),
                waits=wsq, inc="sA")
            ln_free_a[t] = a0
            m1 = c.op("V", lambda e: e.tensor_scalar(lnstat[:, 1:2], lnstat[:, 0:1],
                                                     1.0 / H, None, ALU.mult),
                      waits=[("sV", m0)], inc="sV")
            m1b = c.op("V", lambda e: e.tensor_scalar(lnstat[:, 5:6], lnstat[:, 1:2],
                                                      lnstat[:, 1:2], None, ALU.mult),
                      waits=[("sV", m1)], inc="sV")
            m1c = c.op("V", lambda e: e.tensor_scalar(lnstat[:, 6:7], lnstat[:, 2:3],
                                                      1.0 / H, lnstat[:, 5:6],
                                                      ALU.mult, ALU.subtract),
                      waits=[("sV", m1b), ("sA", a0)], inc="sV")
            var_mark[t] = m1c
            a1 = c.op("A", lambda e: e.activation(out=lnstat[:, 3:4], in_=lnstat[:, 6:7],
                                                  func=AF.Ln, bias=eps_sb[:]),
                      waits=[("sV", m1c)], inc="sA")
            a2 = c.op("A", lambda e: e.activation(out=lnstat[:, 4:5], in_=lnstat[:, 3:4],
                                                  func=AF.Exp, scale=-0.5),
                      waits=[("sA", a1)], inc="sA")
            m3 = c.op("V", lambda e, xb=xb, lt=lt: e.tensor_scalar(
                lt[:], xb[:], lnstat[:, 1:2], lnstat[:, 4:5], ALU.subtract, ALU.mult),
                waits=[("sA", a2)], inc="sV")
            ln_free_v[t] = m3
            for ch in range(32):
                wt = [("sV", m3)] if ch == 0 else []
                if (t, ch - 4) in lncp2:
                    e_, cnt_ = lncp2[(t, ch - 4)]
                    wt.append(("sV" if e_ == "V" else "sA", cnt_))
                elif t >= 1 and ch < 4:
                    e_, cnt_ = lncp2[(t - 1, 28 + ch)]
                    wt.append(("sV" if e_ == "V" else "sA", cnt_))
                tm = c.op("T", lambda e, ch=ch, lt=lt: e.transpose(
                    ps_trD[ch % 4][:], lt[:, ch * 128:(ch + 1) * 128], ident_sb[:]),
                    waits=wt, inc="sT")
                tr_mark2[(t, ch)] = tm
                ceng = "V" if (ch % 2 == 0) else "A"
                dst = ln2_fm[:, ch * TOKW + t * 128: ch * TOKW + (t + 1) * 128]
                cm = c.op(ceng, lambda e, dst=dst, ch=ch: _copy(e, dst, ps_trD[ch % 4][:]),
                          waits=[("sT", tm)], inc="sV" if ceng == "V" else "sA")
                lncp2[(t, ch)] = (ceng, cm)
            wdots = [({"V": "sV", "A": "sA"}[lncp2[(t, ch)][0]], lncp2[(t, ch)][1])
                     for ch in (30, 31)]
            if t >= 1:
                wdots.append(("sV", dt_done[t - 1]))
            lastd = None
            for ch in range(32):
                lastd = c.op("T", lambda e, ch=ch, t=t: e.matmul(
                    ps_dots[:, 0:NCLUST], ln2_fm[:, ch * TOKW + t * 128: ch * TOKW + (t + 1) * 128],
                    cent_T[:, ch * NCLUST:(ch + 1) * NCLUST],
                    start=(ch == 0), stop=(ch == 31)),
                    waits=wdots if ch == 0 else [], inc="sT" if ch == 31 else None)
            d0 = c.op("V", lambda e: e.reduce_max(dstat[:, 0:1], ps_dots[:, 0:NCLUST], axis=AX),
                      waits=[("sT", lastd)], inc="sV")
            d1 = c.op("V", lambda e: e.tensor_scalar(dstat[:, 1:2], dstat[:, 0:1],
                                                     -TEMP_INV, None, ALU.mult),
                      waits=[("sV", d0)], inc="sV")
            d2 = c.op("A", lambda e: e.activation(out=dots_e[:], in_=ps_dots[:, 0:NCLUST], func=AF.Exp,
                                                  bias=dstat[:, 1:2], scale=TEMP_INV),
                      waits=[("sV", d1)], inc="sA")
            d3 = c.op("V", lambda e: e.reduce_sum(dstat[:, 2:3], dots_e[:], axis=AX),
                      waits=[("sA", d2)], inc="sV")
            d4 = c.op("V", lambda e: e.reciprocal(dstat[:, 3:4], dstat[:, 2:3]),
                      waits=[("sV", d3)], inc="sV")
            d5 = c.op("V", lambda e: e.tensor_scalar(dots_n[:], dots_e[:], dstat[:, 3:4],
                                                     None, ALU.mult),
                      waits=[("sV", d4)], inc="sV")
            d6 = c.op("V", lambda e: e.reduce_max(dstat[:, 4:5], dots_n[:], axis=AX),
                      waits=[("sV", d5)], inc="sV")
            c.op("V", lambda e, t=t: e.tensor_single_scalar(
                qs_all[:, t:t + 1], dstat[:, 4:5], THRESH, ALU.is_gt),
                waits=[("sV", d6)], inc="sV")
            td = c.op("T", lambda e: e.transpose(ps_dt[:], dots_n[:], identf_sb[:]),
                      waits=[("sV", d5), ("dP", idf_ready)], inc="sT")
            d8 = c.op("V", lambda e: e.reduce_max(cmax_t[:], ps_dt[:], axis=AX),
                      waits=[("sT", td)], inc="sV")
            d9 = c.op("V", lambda e: e.tensor_tensor(run_cmax[:], run_cmax[:], cmax_t[:],
                                                     ALU.max),
                      waits=[("sV", d8)], inc="sV")
            dt_done[t] = d9
        wt0w = [("dDg", restage_done),
                ("sV", ln_free_v[nT - 1]), ("sA", ln_free_a[nT - 1]),
                ("sV", ln_free_v[nT - 2]), ("sA", ln_free_a[nT - 2])]
        for ch in range(32):
            c.op("Y", lambda e, ch=ch: e.dma_start(
                out=wT_sb[0][:, ch, :],
                in_=w_bf_scr.ap()[0:512, ch * 128:(ch + 1) * 128],
                transpose=True),
                waits=wt0w if ch == 0 else [], inc="dWT0", inc_n=16)
        c.dma("G", cc_in.ap(), run_cmax[:], "dCC", waits=[("sV", dt_done[nT - 1])])
        ccm = c.op("G", lambda e: e.collective_compute(
            "AllReduce", ALU.max, replica_groups=[list(range(N_CORES))],
            ins=[cc_in.ap().opt()], outs=[cc_out.ap().opt()]),
            waits=[("dCC", c.c("dCC"))], inc="sG")
        c.dma("G", gmax_sb[:], cc_out.ap(), "dCC", waits=[("sG", ccm)])
        dcc2 = c.c("dCC")
        e0 = c.op("V", lambda e: e.tensor_single_scalar(cs_f[:], gmax_sb[:], THRESH,
                                                        ALU.is_gt),
                  waits=[("dCC", dcc2)], inc="sV")
        e1 = c.op("V", lambda e: e.tensor_copy(cs_bf[:], cs_f[:]),
                  waits=[("sV", e0)], inc="sV")
        for ocb in range(8):
            tb = c.op("T", lambda e, ocb=ocb: e.matmul(
                ps_dots[0:1, :], cs_bf[:], a1_sb[:, ocb * 512:(ocb + 1) * 512],
                start=True, stop=True),
                waits=[("sV", e1)] if ocb == 0 else [("sV", c.c("sV"))], inc="sT")
            vcp0 = c.op("V", lambda e: e.tensor_copy(rs_row_bf[:], ps_dots[0:1, :]),
                        waits=[("sT", tb)], inc="sV")
            tb2 = c.op("T", lambda e: e.matmul(
                ps_dots[:, :], ones_sb[0:1, :], rs_row_bf[:], start=True, stop=True),
                waits=[("sV", vcp0)], inc="sT")
            vcp1 = c.op("V", lambda e: e.tensor_copy(rs_stage[:], ps_dots[:, :]),
                        waits=[("sT", tb2), ("dDg", c.c("dDg"))], inc="sV")
            c.dma("G", rs_bc_scr.ap()[:, ocb * 512:(ocb + 1) * 512], rs_stage[:],
                  "dDg", waits=[("sV", vcp1)])
        wt1w = [("dDg", c.c("dDg")), ("sV", c.c("sV")), ("sA", c.c("sA")),
                ("sT", c.c("sT"))]
        for ch in range(32):
            c.op("Y", lambda e, ch=ch: e.dma_start(
                out=wT_sb[1][:, ch, :],
                in_=w_bf_scr.ap()[512:1024, ch * 128:(ch + 1) * 128],
                transpose=True),
                waits=wt1w if ch == 0 else [], inc="dWT1", inc_n=16)

        d_last_mm = {}
        dslot_v = {}
        x1b_free = {}
        outst_free = {}
        rs_used = {}

        pend_wT = []

        def load_wT(oc, queue):
            par = oc % 2
            wY = [("dDg", restage_done)]
            if oc >= 2:
                wY.append(("sT", d_last_mm[oc - 2]))
            for ch in range(32):
                def em1(ch=ch, oc=oc, par=par, w=tuple(wY if ch == 0 else ()),
                        queue=queue):
                    c.op(queue, lambda e, oc=oc, ch=ch, par=par: e.dma_start(
                        out=wT_sb[par][:, ch, :],
                        in_=w_bf_scr.ap()[oc * 512:(oc + 1) * 512, ch * 128:(ch + 1) * 128],
                        transpose=True),
                        waits=list(w), inc=f"dWT{par}", inc_n=16)
                pend_wT.append(em1)

        def load_rs(oc):
            par = oc % 2
            wG = [("dDg", c.c("dDg"))]
            if par in rs_used:
                wG.append(("sV", rs_used[par]))
            if oc >= 2:
                wG.append(("sT", d_last_mm[oc - 2]))
            c.dma("G", biasb_oc[par][:], ffn_b[0:1, oc * 512:(oc + 1) * 512],
                  f"dWB{par}", waits=wG)
            c.dma("G", rs_bc[par][:], rs_bc_scr.ap()[:, oc * 512:(oc + 1) * 512],
                  f"dWF{par}", waits=wG)

        load_rs(0)
        load_rs(1)
        for oc in range(8):
            par = oc % 2
            if oc >= 1 and oc + 1 < 8:
                load_wT(oc + 1, "Y")
            wt_ready = c.c(f"dWT{par}")
            for t in range(nT):
                for _ in range(-(-32 // nT)):
                    if pend_wT:
                        pend_wT.pop(0)()
                slot = (oc * nT + t) % 2
                sslot = (oc * nT + t) % 2
                wmm = []
                if t == 0:
                    wmm.append((f"dWT{par}", wt_ready))
                    wmm.append((f"dWB{par}", c.c(f"dWB{par}")))
                if oc == 0:
                    wmm.extend([({"V": "sV", "A": "sA"}[lncp2[(t, ch)][0]],
                                 lncp2[(t, ch)][1]) for ch in (30, 31)])
                if slot in dslot_v:
                    wmm.append(("sV", dslot_v[slot]))
                for ch in range(32):
                    c.op("T", lambda e, ch=ch, t=t, slot=slot, par=par: e.matmul(
                        ps_d[slot][:],
                        ln2_fm[:, ch * TOKW + t * 128: ch * TOKW + (t + 1) * 128],
                        wT_sb[par][:, ch, :],
                        start=(ch == 0), stop=False),
                        waits=wmm if ch == 0 else [], inc=None)
                last = c.op("T", lambda e, par=par, slot=slot: e.matmul(
                    ps_d[slot][:], ones_sb[0:1, 0:128], biasb_oc[par][:],
                    start=False, stop=True), inc="sT")
                d_last_mm[oc] = last
                wx1 = [("sV", x1b_free[sslot])] if sslot in x1b_free else []
                c.dma("Y", x1s_stage[sslot][:],
                      x1_scr.ap()[t * 128:(t + 1) * 128, oc * 512:(oc + 1) * 512],
                      f"dX1B{sslot}", waits=wx1)
                x1_ready = c.c(f"dX1B{sslot}")
                wmsk = [("sT", last), (f"dWF{par}", c.c(f"dWF{par}"))]
                if sslot in outst_free:
                    wmsk.append((f"dOUT{sslot}", outst_free[sslot]))
                v0_ = c.op("V", lambda e, t=t, slot=slot, sslot=sslot, par=par: e.scalar_tensor_tensor(
                    out_stage[sslot][:], ps_d[slot][:], qs_all[:, t:t + 1],
                    rs_bc[par][:], ALU.mult, ALU.mult),
                    waits=wmsk, inc="sV")
                dslot_v[slot] = v0_
                rs_used[par] = v0_
                v1_ = c.op("V", lambda e, sslot=sslot: e.tensor_tensor(
                    out_stage[sslot][:], out_stage[sslot][:], x1s_stage[sslot][:],
                    ALU.add), waits=[("sV", v0_), (f"dX1B{sslot}", x1_ready)], inc="sV")
                x1b_free[sslot] = v1_
                do = c.dma("Y", out[t * 128:(t + 1) * 128, oc * 512:(oc + 1) * 512],
                           out_stage[sslot][:], f"dOUT{sslot}", waits=[("sV", v1_)])
                outst_free[sslot] = do
            if oc + 2 < 8:
                load_rs(oc + 2)
        c.wait("Y", ("dOUT0", c.c("dOUT0")), ("dOUT1", c.c("dOUT1")))
        c.barrier(self_waits={"Y": [("dOUT0", c.c("dOUT0")), ("dOUT1", c.c("dOUT1")),
                                    ("dX1B0", c.c("dX1B0")), ("dX1B1", c.c("dX1B1")),
                                    ("dWT0", c.c("dWT0")), ("dWT1", c.c("dWT1"))],
                              "G": [("dWF0", c.c("dWF0")), ("dWF1", c.c("dWF1")),
                                    ("dWB0", c.c("dWB0")), ("dWB1", c.c("dWB1"))]})

    with nc.Block() as block:
        @block.tensor
        def _(eng):
            emit(Ctx(nc, "T", engines, sems))

        @block.vector
        def _(eng):
            emit(Ctx(nc, "V", engines, sems))

        @block.scalar
        def _(eng):
            emit(Ctx(nc, "A", engines, sems))

        @block.gpsimd
        def _(eng):
            emit(Ctx(nc, "G", engines, sems))

        @block.sync
        def _(eng):
            emit(Ctx(nc, "Y", engines, sems))

    return nc


def kernel(L0_attn_hidden_states, L0_attn_w_qkv, L0_attn_w_o, L0_ffn_weight,
           L0_ffn_bias, L0_ffn_centroids, L0_ffn_assignments):
    x = np.ascontiguousarray(np.asarray(L0_attn_hidden_states, dtype=np.float32))
    wq = np.ascontiguousarray(np.asarray(L0_attn_w_qkv, dtype=np.float32))
    wo = np.ascontiguousarray(np.asarray(L0_attn_w_o, dtype=np.float32))
    fw = np.ascontiguousarray(np.asarray(L0_ffn_weight, dtype=np.float32))
    fb = np.ascontiguousarray(np.asarray(L0_ffn_bias, dtype=np.float32)).reshape(1, -1)
    ce = np.ascontiguousarray(np.asarray(L0_ffn_centroids, dtype=np.float32))
    asg = np.ascontiguousarray(np.asarray(L0_ffn_assignments, dtype=np.int32)).reshape(1, -1)

    B, S, H_ = x.shape
    assert B == N_CORES and H_ == H

    nc = build_nc(S=S)
    in_maps = [{"x0": x[b], "w_qkv": wq, "w_o": wo, "ffn_w": fw,
                "ffn_b": fb, "cent": ce, "assign": asg} for b in range(B)]
    res = run_bass_kernel_spmd(nc, in_maps, core_ids=list(range(N_CORES)))
    return np.stack([res.results[b]["out"] for b in range(B)], axis=0)


# version 17 (final)
# speedup vs baseline: 1.1775x; 1.0166x over previous
import math
from contextlib import ExitStack

import numpy as np
import ml_dtypes

import concourse.bass as bass
import concourse.mybir as mybir
from concourse.bass_utils import run_bass_kernel_spmd

F32 = mybir.dt.float32
BF16 = mybir.dt.bfloat16
I32 = mybir.dt.int32
AF = mybir.ActivationFunctionType
ALU = mybir.AluOpType
AX = mybir.AxisListType.X

H = 4096
QKV = 6144
NH, NKV, D = 32, 8, 128
G_HEADS = 4
NCLUST = 64
TEMP_INV = 10.0
THRESH = 0.01
EPS = 1e-5
SCALE = D ** -0.5
C_SHIFT = 20.0
BIG = 1.0e33

N_CORES = 8

ENGS = ("T", "V", "A", "G", "Y")


def _alibi_slopes(n):
    cp2 = 2 ** math.floor(math.log2(n))
    base = 2.0 ** (-8.0 / cp2)
    slopes = base ** np.arange(1, cp2 + 1, dtype=np.float32)
    if cp2 != n:
        extra_base = 2.0 ** (-4.0 / cp2)
        extra = extra_base ** np.arange(1, 1 + 2 * (n - cp2), 2, dtype=np.float32)
        slopes = np.concatenate([slopes, extra])
    return slopes.astype(np.float32)


SLOPES = _alibi_slopes(NH)


def _sd_all(nq):
    ntile = 4 * nq + 4
    out = np.full((128, ntile * 128), BIG, dtype=np.float32)
    qr = np.arange(128)
    kr = np.arange(128)[:, None]
    for didx in range(ntile):
        delta = didx - 3
        rel = delta * 128 + qr[None, :] - kr
        blk = np.where(rel >= 0, np.sqrt(np.maximum(rel, 0).astype(np.float64)), BIG)
        out[:, didx * 128:(didx + 1) * 128] = blk.astype(np.float32)
    return out



def _copy(e, dst, src):
    if isinstance(e, bass.BassScalarEngine):
        return e.copy(dst, src)
    return e.tensor_copy(dst, src)

class Ctx:

    def __init__(self, nc, me, engines, sems):
        self.nc = nc
        self.me = me
        self.engines = engines
        self.sems = sems
        self.cnt = {k: 0 for k in sems}
        self.bar_k = 0

    def c(self, s):
        return self.cnt[s]

    def op(self, eng, fn, waits=(), inc=None, inc_n=1):
        if self.me == eng:
            e = self.engines[eng]
            for sname, val in waits:
                if val > 0:
                    e.wait_ge(self.sems[sname], val)
            inst = fn(e)
            if inc is not None:
                inst.then_inc(self.sems[inc], inc_n)
        if inc is not None:
            self.cnt[inc] += inc_n
        return self.cnt[inc] if inc is not None else None

    def wait(self, eng, *waits):
        if self.me == eng:
            e = self.engines[eng]
            for sname, val in waits:
                if val > 0:
                    e.wait_ge(self.sems[sname], val)

    def dma(self, eng, out, in_, sem, waits=()):
        return self.op(eng, lambda e: e.dma_start(out=out, in_=in_),
                       waits=waits, inc=sem, inc_n=16)

    def barrier(self, self_waits=None):
        self.bar_k += 1
        target = 5 * self.bar_k
        own = {"T": "sT", "V": "sV", "A": "sA", "G": "sG"}
        for eng in ENGS:
            w = list((self_waits or {}).get(eng, ()))
            if eng in own and self.cnt[own[eng]] > 0:
                w.append((own[eng], self.cnt[own[eng]]))
            self.op(eng, lambda e: e.nop(), waits=w, inc="bar")
        for eng in ENGS:
            self.wait(eng, ("bar", target))


def build_nc(S=2048):
    nT = S // 128
    nQ = S // 512
    TOKW = S
    ND = 4 * nQ + 4

    nc = bass.Bass("TRN2", target_bir_lowering=False)

    x0 = nc.dram_tensor("x0", [S, H], F32, kind="ExternalInput").ap()
    w_qkv = nc.dram_tensor("w_qkv", [H, QKV], F32, kind="ExternalInput").ap()
    w_o = nc.dram_tensor("w_o", [H, H], F32, kind="ExternalInput").ap()
    ffn_w = nc.dram_tensor("ffn_w", [H, H], F32, kind="ExternalInput").ap()
    ffn_b = nc.dram_tensor("ffn_b", [1, H], F32, kind="ExternalInput").ap()
    cent = nc.dram_tensor("cent", [NCLUST, H], F32, kind="ExternalInput").ap()
    assign = nc.dram_tensor("assign", [1, H], I32, kind="ExternalInput").ap()
    out = nc.dram_tensor("out", [S, H], F32, kind="ExternalOutput").ap()

    wqkv_r = w_qkv.rearrange("(c p) n -> p c n", p=128)
    wo_r = w_o.rearrange("(c p) n -> p c n", p=128)

    q_scr = nc.dram_tensor("q_scr", [NH, 128, S], BF16)
    k_scr = nc.dram_tensor("k_scr", [NKV, 128, S], BF16)
    v_scr = nc.dram_tensor("v_scr", [NKV, 128, S], BF16)
    x1_scr = nc.dram_tensor("x1_scr", [S, H], F32)
    w_bf_scr = nc.dram_tensor("w_bf_scr", [H, H], BF16)
    rs_bc_scr = nc.dram_tensor("rs_bc_scr", [128, H], BF16)
    cc_in = nc.dram_tensor("cc_in", [NCLUST, 1], F32)
    cc_out = nc.dram_tensor("cc_out", [NCLUST, 1], F32)

    ident_bf = nc.inline_tensor(np.eye(128, dtype=ml_dtypes.bfloat16), "ident_bf")
    ident_f = nc.inline_tensor(np.eye(128, dtype=np.float32), "ident_f")
    sd_const = nc.inline_tensor(_sd_all(nQ), "sd_const")
    ones_const = nc.inline_tensor(np.ones((1, 128), dtype=ml_dtypes.bfloat16), "ones_const")
    iota_const = nc.inline_tensor(
        np.arange(NCLUST, dtype=np.float32).reshape(NCLUST, 1), "iota_const")
    onescol_const = nc.inline_tensor(np.ones((128, 1), dtype=ml_dtypes.bfloat16), "onescol_const")
    ones128_const = nc.inline_tensor(np.ones((128, 128), dtype=ml_dtypes.bfloat16), "ones128_const")
    onesf_const = nc.inline_tensor(np.ones((1, 128), dtype=np.float32), "onesf_const")

    ident_sb = nc.alloc_sbuf_tensor("ident_sb", [128, 128], BF16)
    ones_sb = nc.alloc_sbuf_tensor("ones_sb", [1, 128], BF16)
    iota_sb = nc.alloc_sbuf_tensor("iota_sb", [NCLUST, 1], F32)
    qs_all = nc.alloc_sbuf_tensor("qs_all", [128, nT], F32)
    lnstat = nc.alloc_sbuf_tensor("lnstat", [128, 8], F32)
    lnsum_sb = nc.alloc_sbuf_tensor("lnsum_sb", [128, nT], F32)
    lnsq_sb = nc.alloc_sbuf_tensor("lnsq_sb", [128, nT], F32)
    mean_all = nc.alloc_sbuf_tensor("mean_all", [128, nT], F32)
    rstd_all = nc.alloc_sbuf_tensor("rstd_all", [128, nT], F32)
    tmpacc = nc.alloc_sbuf_tensor("tmpacc", [128, 4], F32)
    eps_sb = nc.alloc_sbuf_tensor("eps_sb", [128, 1], F32)
    onescol_sb = nc.alloc_sbuf_tensor("onescol_sb", [128, 1], BF16)
    ones128_sb = nc.alloc_sbuf_tensor("ones128_sb", [128, 128], BF16)
    onesf_sb = nc.alloc_sbuf_tensor("onesf_sb", [1, 128], F32)

    esA = ExitStack()
    ln1_fm = esA.enter_context(nc.sbuf_tensor("ln1_fm", [128, 32 * TOKW], BF16))
    xbuf = [esA.enter_context(nc.sbuf_tensor(f"xbuf{i}", [128, H], F32)) for i in range(2)]
    ln1_tm = [esA.enter_context(nc.sbuf_tensor(f"ln1_tm{i}", [128, H], BF16)) for i in range(2)]
    w_sbA = [esA.enter_context(nc.sbuf_tensor(f"w_sbA{i}", [128, 32, 128], BF16)) for i in range(2)]
    qkv_stage = [esA.enter_context(nc.sbuf_tensor(f"qkv_st{i}", [128, 512], BF16)) for i in range(2)]
    vstage = esA.enter_context(nc.sbuf_tensor("vstage", [128, 512], BF16))
    vtr_stage = [esA.enter_context(nc.sbuf_tensor(f"vtr_st{i}", [128, 512], BF16)) for i in range(2)]
    psA = ExitStack()
    ps_tr = [psA.enter_context(nc.psum_tensor(f"ps_tr{i}", [128, 128], BF16)) for i in range(4)]
    ps_qkv = [psA.enter_context(nc.psum_tensor(f"ps_qkv{i}", [128, 512], F32)) for i in range(4)]
    psA.close()
    esA.close()

    esBC = ExitStack()
    attn_fm = esBC.enter_context(nc.sbuf_tensor("attn_fm", [128, 32 * TOKW], BF16))
    esB = ExitStack()
    sd_sb = esB.enter_context(nc.sbuf_tensor("sd_sb", [128, ND * 128], F32))
    k_sb = [esB.enter_context(nc.sbuf_tensor(f"k_sb{i}", [128, S], BF16)) for i in range(2)]
    v_sb = [esB.enter_context(nc.sbuf_tensor(f"v_sb{i}", [128, S], BF16)) for i in range(2)]
    p_all = esB.enter_context(nc.sbuf_tensor("p_all", [128, nT * 512], BF16))
    q_tile = [esB.enter_context(nc.sbuf_tensor(f"q_tile{i}", [128, 512], BF16)) for i in range(2)]
    rec_row = esB.enter_context(nc.sbuf_tensor("rec_row", [1, 512], F32))
    rec_bc = [esB.enter_context(nc.sbuf_tensor(f"rec_bc{i}", [128, 512], F32)) for i in range(2)]
    psB = ExitStack()
    ps_s = [psB.enter_context(nc.psum_tensor(f"ps_s{i}", [128, 512], F32)) for i in range(4)]
    ps_avf = [psB.enter_context(nc.psum_tensor(f"ps_avf{i}", [128, 512], F32)) for i in range(2)]
    ps_dbc = [psB.enter_context(nc.psum_tensor(f"ps_dbc{i}", [128, 512], F32)) for i in range(2)]
    psB.close()
    esB.close()
    esC = ExitStack()
    wo_sb = [esC.enter_context(nc.sbuf_tensor(f"wo_sb{i}", [128, 32, 512], BF16)) for i in range(2)]
    x0_stage = [esC.enter_context(nc.sbuf_tensor(f"x0_st{i}", [128, 512], F32)) for i in range(2)]
    x1_stage = [esC.enter_context(nc.sbuf_tensor(f"x1_st{i}", [128, 512], F32)) for i in range(2)]
    scrA = esC.enter_context(nc.sbuf_tensor("scrA", [128, 512], BF16))
    psC = ExitStack()
    ps_x1 = [psC.enter_context(nc.psum_tensor(f"ps_x1{i}", [128, 512], F32)) for i in range(3)]
    psC.close()
    esC.close()
    esBC.close()

    esR = ExitStack()
    cent_T = esR.enter_context(nc.sbuf_tensor("cent_T", [128, 32 * NCLUST], BF16, side="right"))
    a1_sb = esR.enter_context(nc.sbuf_tensor("a1_sb", [NCLUST, H], BF16, side="right"))
    esD0 = ExitStack()
    cent_sb = esD0.enter_context(nc.sbuf_tensor("cent_sb", [NCLUST, H], BF16))
    asg_i = esD0.enter_context(nc.sbuf_tensor("asg_i", [1, H], I32))
    asg_f = esD0.enter_context(nc.sbuf_tensor("asg_f", [1, H], F32))
    asg_bf = esD0.enter_context(nc.sbuf_tensor("asg_bf", [1, H], BF16))
    esD0.close()
    esD = ExitStack()
    ln2_fm = esD.enter_context(nc.sbuf_tensor("ln2_fm", [128, 32 * TOKW], BF16))
    esD1 = ExitStack()
    x1buf = [esD1.enter_context(nc.sbuf_tensor(f"x1buf{i}", [128, H], F32)) for i in range(2)]
    ln2_tm = [esD1.enter_context(nc.sbuf_tensor(f"ln2_tm{i}", [128, H], BF16)) for i in range(2)]
    identf_sb = esD1.enter_context(nc.sbuf_tensor("identf_sb", [128, 128], F32))
    dots_e = esD1.enter_context(nc.sbuf_tensor("dots_e", [128, NCLUST], F32))
    dots_n = esD1.enter_context(nc.sbuf_tensor("dots_n", [128, NCLUST], F32))
    dstat = esD1.enter_context(nc.sbuf_tensor("dstat", [128, 8], F32))
    cmax_t = esD1.enter_context(nc.sbuf_tensor("cmax_t", [NCLUST, 1], F32))
    run_cmax = esD1.enter_context(nc.sbuf_tensor("run_cmax", [NCLUST, 1], F32))
    gmax_sb = esD1.enter_context(nc.sbuf_tensor("gmax_sb", [NCLUST, 1], F32))
    cs_f = esD1.enter_context(nc.sbuf_tensor("cs_f", [NCLUST, 1], F32))
    cs_bf = esD1.enter_context(nc.sbuf_tensor("cs_bf", [NCLUST, 1], BF16))
    rs_row_bf = esD1.enter_context(nc.sbuf_tensor("rs_row_bf", [1, 512], BF16))
    rs_stage = esD1.enter_context(nc.sbuf_tensor("rs_stage", [128, 512], BF16))
    esD1.close()
    esR.close()
    esD3 = ExitStack()
    wT_sb = [esD3.enter_context(nc.sbuf_tensor(f"wT_sb{i}", [128, 32, 512], BF16)) for i in range(2)]
    rs_bc = [esD3.enter_context(nc.sbuf_tensor(f"rs_bc{i}", [128, 512], BF16)) for i in range(2)]
    biasb_oc = [esD3.enter_context(nc.sbuf_tensor(f"biasb_oc{i}", [1, 512], BF16)) for i in range(2)]
    x1s_stage = [esD3.enter_context(nc.sbuf_tensor(f"x1s_st{i}", [128, 512], F32)) for i in range(2)]
    out_stage = [esD3.enter_context(nc.sbuf_tensor(f"out_st{i}", [128, 512], F32)) for i in range(2)]
    esD3.close()
    esD.close()
    psD = ExitStack()
    ps_trD = [psD.enter_context(nc.psum_tensor(f"ps_trD{i}", [128, 128], BF16)) for i in range(4)]
    ps_dots = psD.enter_context(nc.psum_tensor("ps_dots", [128, 512], F32))
    ps_dt = psD.enter_context(nc.psum_tensor("ps_dt", [NCLUST, 128], F32))
    ps_d = [psD.enter_context(nc.psum_tensor(f"ps_d{i}", [128, 512], F32)) for i in range(2)]
    psD.close()

    sem_names = ["bar", "sT", "sV", "sA", "sG", "dP", "dD", "dCC",
                 "dDg", "dXA0", "dXA1", "dW0", "dW1", "dSQ0", "dSQ1", "dSV0", "dSV1",
                 "dKV0", "dKV1", "dQL0", "dQL1", "dWO0", "dWO1", "dX00", "dX01",
                 "dX1S0", "dX1S1", "dX1L0", "dX1L1", "dWF0", "dWF1", "dWT0", "dWT1",
                 "dWB0", "dWB1", "dX1G0", "dX1G1",
                 "dX1B0", "dX1B1", "dOUT0", "dOUT1"]
    sems = {n: nc.alloc_semaphore(name=n) for n in sem_names}

    engines = {"T": nc.tensor, "V": nc.vector, "A": nc.scalar,
               "G": nc.gpsimd, "Y": nc.sync}

    def emit(c: Ctx):
        c.dma("Y", ident_sb[:], ident_bf.ap(), "dP")
        c.dma("Y", ones_sb[:], ones_const.ap(), "dP")
        c.dma("Y", iota_sb[:], iota_const.ap(), "dP")
        c.dma("Y", onescol_sb[:], onescol_const.ap(), "dP")
        c.dma("Y", ones128_sb[:], ones128_const.ap(), "dP")
        c.dma("Y", onesf_sb[:], onesf_const.ap(), "dP")
        c.op("V", lambda e: e.memset(eps_sb[:], EPS), inc="sV")
        p0_done = {"Y": [("dP", c.c("dP"))]}
        c.barrier(self_waits=p0_done)

        ln_free_v = {}
        ln_free_a = {}
        lncp = {}
        tr_mark = {}
        var_mark = {}
        for t in range(nT):
            w = []
            if t >= 2:
                w = [("sV", ln_free_v[t - 2]), ("sA", ln_free_a[t - 2])]
            xa = c.dma("Y", xbuf[t % 2][:], x0[t * 128:(t + 1) * 128, :],
                       f"dXA{t % 2}", waits=w)
            xb = xbuf[t % 2]
            lt = ln1_tm[t % 2]
            m0 = c.op("V", lambda e, xb=xb: e.reduce_sum(lnstat[:, 0:1], xb[:], axis=AX),
                      waits=[(f"dXA{t % 2}", xa)], inc="sV")
            wsq = [(f"dXA{t % 2}", xa)]
            if t >= 1:
                wsq.append(("sV", var_mark[t - 1]))
            if t >= 2:
                wsq.append(("sT", tr_mark[(t - 2, 31)]))
            a0 = c.op("A", lambda e, xb=xb, lt=lt: e.activation(
                out=lt[:], in_=xb[:], func=AF.Square, accum_out=lnstat[:, 2:3]),
                waits=wsq, inc="sA")
            ln_free_a[t] = a0
            m1 = c.op("V", lambda e: e.tensor_scalar(lnstat[:, 1:2], lnstat[:, 0:1],
                                                     1.0 / H, None, ALU.mult),
                      waits=[("sV", m0)], inc="sV")
            m1b = c.op("V", lambda e: e.tensor_scalar(lnstat[:, 5:6], lnstat[:, 1:2],
                                                      lnstat[:, 1:2], None, ALU.mult),
                      waits=[("sV", m1)], inc="sV")
            m1c = c.op("V", lambda e: e.tensor_scalar(lnstat[:, 6:7], lnstat[:, 2:3],
                                                      1.0 / H, lnstat[:, 5:6],
                                                      ALU.mult, ALU.subtract),
                      waits=[("sV", m1b), ("sA", a0)], inc="sV")
            var_mark[t] = m1c
            a1 = c.op("A", lambda e: e.activation(out=lnstat[:, 3:4], in_=lnstat[:, 6:7],
                                                  func=AF.Ln, bias=eps_sb[:]),
                      waits=[("sV", m1c)], inc="sA")
            a2 = c.op("A", lambda e: e.activation(out=lnstat[:, 4:5], in_=lnstat[:, 3:4],
                                                  func=AF.Exp, scale=-0.5),
                      waits=[("sA", a1)], inc="sA")
            m3 = c.op("V", lambda e, xb=xb, lt=lt: e.tensor_scalar(
                lt[:], xb[:], lnstat[:, 1:2], lnstat[:, 4:5], ALU.subtract, ALU.mult),
                waits=[("sA", a2)], inc="sV")
            ln_free_v[t] = m3
            for ch in range(32):
                wt = [("sV", m3)] if ch == 0 else []
                if (t, ch - 4) in lncp:
                    e_, cnt_ = lncp[(t, ch - 4)]
                    wt.append(("sV" if e_ == "V" else "sA", cnt_))
                elif t >= 1 and ch < 4:
                    e_, cnt_ = lncp[(t - 1, 28 + ch)]
                    wt.append(("sV" if e_ == "V" else "sA", cnt_))
                tm = c.op("T", lambda e, ch=ch, lt=lt: e.transpose(
                    ps_tr[ch % 4][:], lt[:, ch * 128:(ch + 1) * 128], ident_sb[:]),
                    waits=wt, inc="sT")
                tr_mark[(t, ch)] = tm
                ceng = "V" if (ch % 2 == 0) else "A"
                dst = ln1_fm[:, ch * TOKW + t * 128: ch * TOKW + (t + 1) * 128]
                cm = c.op(ceng, lambda e, dst=dst, ch=ch: _copy(e, dst, ps_tr[ch % 4][:]),
                          waits=[("sT", tm)], inc="sV" if ceng == "V" else "sA")
                lncp[(t, ch)] = (ceng, cm)

        NJG = QKV // 128
        grp_last_mm = {}
        slot_free = {}
        stage_free = {}
        vtr_free = {}
        ps_tr_cons = {sub: lncp[(nT - 1, 28 + sub)] for sub in range(4)}
        ln_chunk_ready = {n: [({"V": "sV", "A": "sA"}[lncp[(4 * n + 3, ch)][0]],
                               lncp[(4 * n + 3, ch)][1]) for ch in (30, 31)]
                          for n in range(nQ)}
        for jg in range(NJG):
            wv = []
            if jg >= 2:
                wv.append(("sT", grp_last_mm[jg - 2]))
            wld = c.dma("G", w_sbA[jg % 2][:], wqkv_r[:, :, jg * 128:(jg + 1) * 128],
                        f"dW{jg % 2}", waits=wv)
            if jg < 32:
                c.dma("G", w_bf_scr.ap()[jg * 128:(jg + 1) * 128, :],
                      ffn_w[jg * 128:(jg + 1) * 128, :], "dDg")
            j_abs = jg
            for n in range(nQ):
                slot = n % 4
                wmm = [(f"dW{jg % 2}", wld)]
                if jg < 2:
                    wmm.extend(ln_chunk_ready[n])
                if slot in slot_free:
                    e_, cnt_ = slot_free[slot]
                    wmm.append(({"V": "sV", "A": "sA", "T": "sT"}[e_], cnt_))
                last = None
                for ch in range(32):
                    last = c.op("T", lambda e, ch=ch, n=n, jg=jg, slot=slot: e.matmul(
                        ps_qkv[slot][:],
                        w_sbA[jg % 2][:, ch, :],
                        ln1_fm[:, ch * TOKW + n * 512: ch * TOKW + (n + 1) * 512],
                        start=(ch == 0), stop=(ch == 31)),
                        waits=wmm if ch == 0 else [],
                        inc="sT" if ch == 31 else None)
                grp_last_mm[jg] = last
                if j_abs < NH:
                    st = n % 2
                    wep = [("sT", last)]
                    if st in stage_free:
                        wep.append((f"dSQ{st}", stage_free[st]))
                    acp = c.op("A", lambda e, st=st, slot=slot: e.activation(
                        out=qkv_stage[st][:], in_=ps_qkv[slot][:], func=AF.Copy,
                        scale=SCALE),
                        waits=wep, inc="sA")
                    slot_free[slot] = ("A", acp)
                    dsp = c.dma("Y", q_scr.ap()[j_abs, :, n * 512:(n + 1) * 512],
                                qkv_stage[st][:], f"dSQ{st}", waits=[("sA", acp)])
                    stage_free[st] = dsp
                elif j_abs < NH + NKV:
                    hk = j_abs - NH
                    st = n % 2
                    wep = [("sT", last)]
                    if st in stage_free:
                        wep.append((f"dSQ{st}", stage_free[st]))
                    acp = c.op("A", lambda e, st=st, slot=slot: e.activation(
                        out=qkv_stage[st][:], in_=ps_qkv[slot][:], func=AF.Copy),
                        waits=wep, inc="sA")
                    slot_free[slot] = ("A", acp)
                    dsp = c.dma("Y", k_scr.ap()[hk, :, n * 512:(n + 1) * 512],
                                qkv_stage[st][:], f"dSQ{st}", waits=[("sA", acp)])
                    stage_free[st] = dsp
                else:
                    hk = j_abs - NH - NKV
                    wep = [("sT", last)]
                    if "vstage" in slot_free:
                        wep.append(("sT", slot_free["vstage"][1]))
                    vcp = c.op("A", lambda e, slot=slot: e.activation(
                        out=vstage[:], in_=ps_qkv[slot][:], func=AF.Copy),
                        waits=wep, inc="sA")
                    slot_free[slot] = ("A", vcp)
                    vslot = n % 2
                    trs = []
                    for sub in range(4):
                        wtr = [("sA", vcp)] if sub == 0 else []
                        if sub in ps_tr_cons:
                            e_, cnt_ = ps_tr_cons[sub]
                            wtr = wtr + [("sV" if e_ == "V" else "sA", cnt_)]
                        tm = c.op("T", lambda e, sub=sub: e.transpose(
                            ps_tr[sub % 4][:], vstage[:, sub * 128:(sub + 1) * 128],
                            ident_sb[:]), waits=wtr, inc="sT")
                        trs.append(tm)
                    slot_free["vstage"] = ("T", trs[-1])
                    wcp_extra = [(f"dSV{vslot}", vtr_free[vslot])] if vslot in vtr_free else []
                    vcps = []
                    for sub in range(4):
                        ceng = "V" if sub % 2 == 0 else "A"
                        cm = c.op(ceng, lambda e, sub=sub, vslot=vslot: _copy(e, vtr_stage[vslot][:, sub * 128:(sub + 1) * 128], ps_tr[sub % 4][:]),
                            waits=[("sT", trs[sub])] + (wcp_extra if sub < 2 else []),
                            inc="sV" if ceng == "V" else "sA")
                        vcps.append((ceng, cm))
                        ps_tr_cons[sub] = (ceng, cm)
                    wd = [({"V": "sV", "A": "sA"}[e_], cnt_) for e_, cnt_ in vcps[-2:]]
                    dsp = c.dma("Y", v_scr.ap()[hk, :, n * 512:(n + 1) * 512],
                                vtr_stage[vslot][:], f"dSV{vslot}", waits=wd)
                    vtr_free[vslot] = dsp
        c.barrier(self_waits={"G": [("dW0", c.c("dW0")), ("dW1", c.c("dW1"))],
                              "Y": [("dSQ0", c.c("dSQ0")), ("dSQ1", c.c("dSQ1")),
                                    ("dSV0", c.c("dSV0")), ("dSV1", c.c("dSV1"))]})

        c.dma("Y", sd_sb[:], sd_const.ap(), "dP")
        sd_ready = c.c("dP")

        kv_last_use = {}
        q_last = {}
        avf_free = {}
        den_free = {}
        prev_exp = {}
        pall_av_last = None
        pend_ep = [None]

        def flush_ep1():
            ep = pend_ep[0]
            if ep is None:
                return
            hp, np_, par, avf_stop, den_stop = ep
            wln = [("sT", den_stop)]
            if par in avf_free:
                wln.append(("sV", avf_free[par]))
            ra = c.op("A", lambda e, par=par: e.activation(
                out=rec_bc[par][:], in_=ps_dbc[par][:, :], func=AF.Ln),
                waits=wln, inc="sA")
            den_free[par] = ("sA", ra)
            r0 = c.op("A", lambda e, par=par: e.activation(
                out=rec_bc[par][:], in_=rec_bc[par][:], func=AF.Exp, scale=-1.0),
                waits=[("sA", ra)], inc="sA")
            pend_ep[0] = (hp, np_, par, avf_stop, r0)

        def flush_ep2():
            ep = pend_ep[0]
            if ep is None:
                return
            pend_ep[0] = None
            hp, np_, par, avf_stop, r0 = ep
            dst = attn_fm[:, hp * TOKW + np_ * 512: hp * TOKW + (np_ + 1) * 512]
            tt = c.op("V", lambda e, dst=dst, par=par: e.tensor_tensor(
                dst, ps_avf[par][:], rec_bc[par][:], ALU.mult),
                waits=[("sT", avf_stop), ("sA", r0)], inc="sV")
            avf_free[par] = tt

        chunks = [(hk, g, n) for hk in range(NKV) for g in range(G_HEADS)
                  for n in range(nQ)]
        q_rdy = {}

        def q_load(ci):
            hk, g, n = chunks[ci]
            h = hk * G_HEADS + g
            qslot = ci % 2
            wq = []
            if qslot in q_last:
                wq.append(("sT", q_last[qslot]))
            c.dma("Y", q_tile[qslot][:], q_scr.ap()[h, :, n * 512:(n + 1) * 512],
                  f"dQL{qslot}", waits=wq)
            q_rdy[ci] = c.c(f"dQL{qslot}")

        kv_rdy = {}
        for ci, (hk, g, n) in enumerate(chunks):
            if g == 0 and n == 0:
                wkv = []
                if hk >= 2:
                    wkv.append(("sT", kv_last_use[hk % 2]))
                c.dma("Y", k_sb[hk % 2][:], k_scr.ap()[hk, :, :], f"dKV{hk % 2}",
                      waits=wkv)
                c.dma("Y", v_sb[hk % 2][:], v_scr.ap()[hk, :, :], f"dKV{hk % 2}")
                kv_rdy[hk] = c.c(f"dKV{hk % 2}")
            kv_sem = f"dKV{hk % 2}"
            kv_ready = kv_rdy[hk]
            if ci == 0:
                q_load(0)
            h = hk * G_HEADS + g
            slope = -float(SLOPES[h])
            qslot = ci % 2
            q_sem = f"dQL{qslot}"
            q_ready = q_rdy[ci]
            par = ci % 2
            nkt = 4 * n + 4

            def score(kt):
                wsc = [(kv_sem, kv_ready), (q_sem, q_ready)]
                if (kt % 4) in prev_exp:
                    wsc.append(("sA", prev_exp[kt % 4]))
                sm = c.op("T", lambda e, kt=kt, hk=hk, qslot=qslot: e.matmul(
                    ps_s[kt % 4][:], k_sb[hk % 2][:, kt * 128:(kt + 1) * 128],
                    q_tile[qslot][:], start=True, stop=True),
                    waits=wsc, inc="sT")
                q_last[qslot] = sm
                kv_last_use[hk % 2] = sm
                return sm

            sms = {}
            for kt0 in range(min(3, nkt)):
                sms[kt0] = score(kt0)
            for kt in range(nkt):
                if kt + 3 < nkt:
                    sms[kt + 3] = score(kt + 3)
                didx = 4 * n - kt + 3
                wst = [("sT", sms[kt])]
                if kt == 0:
                    wst.append(("dP", sd_ready))
                tt = c.op("V", lambda e, kt=kt, didx=didx, slope=slope: e.scalar_tensor_tensor(
                    ps_s[kt % 4][:], sd_sb[:, didx * 128: didx * 128 + 512],
                    slope, ps_s[kt % 4][:], ALU.mult, ALU.add),
                    waits=wst, inc="sV")
                wex = [("sV", tt)]
                if kt == 0 and pall_av_last is not None:
                    wex.append(("sT", pall_av_last))
                em = c.op("A", lambda e, kt=kt: e.activation(
                    out=p_all[:, kt * 512:(kt + 1) * 512], in_=ps_s[kt % 4][:],
                    func=AF.Exp), waits=wex, inc="sA")
                prev_exp[kt % 4] = em
                if kt == 0 and ci + 1 < len(chunks):
                    q_load(ci + 1)
                if kt == 1:
                    flush_ep1()
                if kt == 3:
                    flush_ep2()
                wav = [("sA", em)]
                if kt == 0 and par in avf_free:
                    wav.append(("sV", avf_free[par]))
                c.op("T", lambda e, kt=kt, hk=hk, par=par, nkt=nkt: e.matmul(
                    ps_avf[par][:], v_sb[hk % 2][:, kt * 128:(kt + 1) * 128],
                    p_all[:, kt * 512:(kt + 1) * 512],
                    start=(kt == 0), stop=(kt == nkt - 1)),
                    waits=wav, inc="sT")
                avf_stop = c.c("sT")
                wdn = []
                if kt == 0 and par in den_free:
                    wdn.append(den_free[par])
                c.op("T", lambda e, kt=kt, par=par, nkt=nkt: e.matmul(
                    ps_dbc[par][:, :], ones128_sb[:], p_all[:, kt * 512:(kt + 1) * 512],
                    start=(kt == 0), stop=(kt == nkt - 1)),
                    waits=wdn, inc="sT")
                den_stop = c.c("sT")
                kv_last_use[hk % 2] = c.c("sT")
                if kt == nkt - 1:
                    pall_av_last = c.c("sT")
                    pend_ep[0] = (h, n, par, avf_stop, den_stop)
            if nkt == 4:
                pass
        flush_ep1()
        flush_ep2()
        c.barrier(self_waits={"Y": [("dKV0", c.c("dKV0")), ("dKV1", c.c("dKV1")),
                                    ("dQL0", c.c("dQL0")), ("dQL1", c.c("dQL1"))]})

        wo_last_mm = {}
        cslot_tt = {}
        x0s_free = {}
        x1s_free = {}
        x0_rdy = {}
        sq_mark = {}
        sqadd_mark = {}
        fin_a = {}
        fin_var_a = None
        c.op("V", lambda e: e.memset(lnsum_sb[:], 0.0), inc="sV")
        c.op("V", lambda e: e.memset(lnsq_sb[:], 0.0), inc="sV")

        def c_x0(oc, t):
            sslot = (oc * nT + t) % 2
            wx0 = [("sV", x0s_free[sslot])] if sslot in x0s_free else []
            c.dma("Y", x0_stage[sslot][:],
                  x0[t * 128:(t + 1) * 128, oc * 512:(oc + 1) * 512], f"dX0{sslot}",
                  waits=wx0)
            x0_rdy[(oc, t)] = c.c(f"dX0{sslot}")

        for oc in range(8):
            wv = []
            if oc >= 2:
                wv.append(("sT", wo_last_mm[oc - 2]))
            wold = c.dma("G", wo_sb[oc % 2][:], wo_r[:, :, oc * 512:(oc + 1) * 512],
                         f"dWO{oc % 2}", waits=wv)
            for t in range(nT):
                slot = (oc * nT + t) % 3
                sslot = (oc * nT + t) % 2
                if oc == 0 and t == 0:
                    c_x0(0, 0)
                wmm = [(f"dWO{oc % 2}", wold)]
                if slot in cslot_tt:
                    wmm.append(("sV", cslot_tt[slot]))
                last = None
                for ch in range(32):
                    last = c.op("T", lambda e, ch=ch, oc=oc, t=t, slot=slot: e.matmul(
                        ps_x1[slot][:],
                        attn_fm[:, ch * TOKW + t * 128: ch * TOKW + (t + 1) * 128],
                        wo_sb[oc % 2][:, ch, :],
                        start=(ch == 0), stop=(ch == 31)),
                        waits=wmm if ch == 0 else [],
                        inc="sT" if ch == 31 else None)
                wo_last_mm[oc] = last
                if t + 1 < nT:
                    c_x0(oc, t + 1)
                elif oc + 1 < 8:
                    c_x0(oc + 1, 0)
                wtt = [("sT", last), (f"dX0{sslot}", x0_rdy[(oc, t)])]
                if sslot in x1s_free:
                    wtt.append((f"dX1S{sslot}", x1s_free[sslot]))
                if sslot in sq_mark:
                    wtt.append(("sA", sq_mark[sslot]))
                vt = c.op("V", lambda e, slot=slot, sslot=sslot: e.tensor_tensor(
                    x1_stage[sslot][:], ps_x1[slot][:], x0_stage[sslot][:], ALU.add),
                    waits=wtt, inc="sV")
                cslot_tt[slot] = vt
                x0s_free[sslot] = vt
                d1 = c.dma("Y", x1_scr.ap()[t * 128:(t + 1) * 128, oc * 512:(oc + 1) * 512],
                           x1_stage[sslot][:], f"dX1S{sslot}", waits=[("sV", vt)])
                x1s_free[sslot] = d1
                rsum = c.op("V", lambda e, sslot=sslot: e.reduce_sum(
                    tmpacc[:, 0:1], x1_stage[sslot][:], axis=AX),
                    waits=[("sV", vt)], inc="sV")
                radd = c.op("V", lambda e, t=t: e.tensor_tensor(
                    lnsum_sb[:, t:t + 1], lnsum_sb[:, t:t + 1], tmpacc[:, 0:1],
                    ALU.add), waits=[("sV", rsum)], inc="sV")
                bpar = (oc * nT + t) % 2
                wsq = [("sV", vt)]
                if bpar in sqadd_mark:
                    wsq.append(("sV", sqadd_mark[bpar]))
                sq = c.op("A", lambda e, sslot=sslot, bpar=bpar: e.activation(
                    out=scrA[:], in_=x1_stage[sslot][:], func=AF.Square,
                    accum_out=tmpacc[:, 2 + bpar:3 + bpar]),
                    waits=wsq, inc="sA")
                sq_mark[sslot] = sq
                sadd = c.op("V", lambda e, t=t, bpar=bpar: e.tensor_tensor(
                    lnsq_sb[:, t:t + 1], lnsq_sb[:, t:t + 1],
                    tmpacc[:, 2 + bpar:3 + bpar], ALU.add),
                    waits=[("sA", sq)], inc="sV")
                sqadd_mark[bpar] = sadd
                if oc == 7:
                    fm1 = c.op("V", lambda e, t=t: e.tensor_scalar(
                        mean_all[:, t:t + 1], lnsum_sb[:, t:t + 1], 1.0 / H, None,
                        ALU.mult), waits=[("sV", radd)], inc="sV")
                    fm2 = c.op("V", lambda e, t=t: e.tensor_scalar(
                        tmpacc[:, 1:2], mean_all[:, t:t + 1], mean_all[:, t:t + 1],
                        None, ALU.mult), waits=[("sV", fm1)], inc="sV")
                    wvar = [("sV", fm2), ("sV", sadd)]
                    if fin_var_a is not None:
                        wvar.append(("sA", fin_var_a))
                    fm3 = c.op("V", lambda e, t=t: e.tensor_scalar(
                        lnstat[:, 6:7], lnsq_sb[:, t:t + 1], 1.0 / H,
                        tmpacc[:, 1:2], ALU.mult, ALU.subtract),
                        waits=wvar, inc="sV")
                    fa1 = c.op("A", lambda e: e.activation(
                        out=lnstat[:, 3:4], in_=lnstat[:, 6:7], func=AF.Ln,
                        bias=eps_sb[:]), waits=[("sV", fm3)], inc="sA")
                    fin_var_a = fa1
                    fa2 = c.op("A", lambda e, t=t: e.activation(
                        out=rstd_all[:, t:t + 1], in_=lnstat[:, 3:4], func=AF.Exp,
                        scale=-0.5), waits=[("sA", fa1)], inc="sA")
                    fin_a[t] = fa2
        c.barrier(self_waits={"G": [("dWO0", c.c("dWO0")), ("dWO1", c.c("dWO1")),
                                    ("dDg", c.c("dDg"))],
                              "Y": [("dX1S0", c.c("dX1S0")), ("dX1S1", c.c("dX1S1")),
                                    ("dX00", c.c("dX00")), ("dX01", c.c("dX01"))]})

        c.dma("G", cent_sb[:], cent, "dDg")
        c.dma("Y", asg_i[:], assign, "dD")
        dD0 = c.c("dD")
        dDg0 = c.c("dDg")
        v0 = c.op("V", lambda e: e.tensor_copy(asg_f[:], asg_i[:]),
                  waits=[("dD", dD0)], inc="sV")
        v1 = c.op("V", lambda e: e.tensor_copy(asg_bf[:], asg_f[:]),
                  waits=[("sV", v0)], inc="sV")
        for ocb in range(8):
            tb = c.op("T", lambda e, ocb=ocb: e.matmul(
                ps_d[0][:NCLUST, :], ones_sb[0:1, 0:NCLUST],
                asg_bf[0:1, ocb * 512:(ocb + 1) * 512], start=True, stop=True),
                waits=[("sV", v1)] if ocb == 0 else [("sV", c.c("sV"))], inc="sT")
            c.op("V", lambda e, ocb=ocb: e.tensor_scalar(
                a1_sb[:, ocb * 512:(ocb + 1) * 512], ps_d[0][:NCLUST, :],
                iota_sb[:], None, ALU.is_equal),
                waits=[("sT", tb)], inc="sV")
        ctp = {}
        for ch in range(32):
            wt = [("dDg", dDg0)] if ch == 0 else []
            if ch >= 4:
                e_, cnt_ = ctp[ch - 4]
                wt.append(("sA" if e_ == "A" else "sV", cnt_))
            tm = c.op("T", lambda e, ch=ch: e.transpose(
                ps_trD[ch % 4][:, 0:NCLUST],
                cent_sb[0:NCLUST, ch * 128:(ch + 1) * 128],
                ident_sb[0:NCLUST, 0:NCLUST]),
                waits=wt, inc="sT")
            ceng = "A" if ch % 2 == 0 else "V"
            cm = c.op(ceng, lambda e, ch=ch: _copy(e, cent_T[:, ch * NCLUST:(ch + 1) * NCLUST], ps_trD[ch % 4][:, 0:NCLUST]),
                waits=[("sT", tm)], inc="sA" if ceng == "A" else "sV")
            ctp[ch] = (ceng, cm)
        c.barrier(self_waits={"Y": [("dD", c.c("dD"))], "G": [("dDg", c.c("dDg"))]})

        c.dma("Y", identf_sb[:], ident_f.ap(), "dP")
        idf_ready = c.c("dP")
        restage_done = dDg0
        c.op("V", lambda e: e.memset(run_cmax[:], -1e30), inc="sV")
        ln_free_v.clear()
        ln_free_a.clear()
        var_mark.clear()
        lncp2 = {}
        tr_mark2 = {}
        dt_done = {}
        for t in range(nT):
            w = []
            if t >= 2:
                w = [("sV", ln_free_v[t - 2])]
            x1r = c.dma("Y", x1buf[t % 2][:, 0:H // 2],
                        x1_scr.ap()[t * 128:(t + 1) * 128, 0:H // 2],
                        f"dX1L{t % 2}", waits=w)
            x1g = c.dma("G", x1buf[t % 2][:, H // 2:],
                        x1_scr.ap()[t * 128:(t + 1) * 128, H // 2:],
                        f"dX1G{t % 2}", waits=list(w))
            xb = x1buf[t % 2]
            lt = ln2_tm[t % 2]
            wm3 = [(f"dX1L{t % 2}", x1r), (f"dX1G{t % 2}", x1g), ("sA", fin_a[t])]
            if t >= 2:
                wm3.append(("sT", tr_mark2[(t - 2, 31)]))
            m3 = c.op("V", lambda e, xb=xb, lt=lt, t=t: e.tensor_scalar(
                lt[:], xb[:], mean_all[:, t:t + 1], rstd_all[:, t:t + 1],
                ALU.subtract, ALU.mult),
                waits=wm3, inc="sV")
            ln_free_v[t] = m3
            for ch in range(32):
                wt = [("sV", m3)] if ch == 0 else []
                if (t, ch - 4) in lncp2:
                    e_, cnt_ = lncp2[(t, ch - 4)]
                    wt.append(("sV" if e_ == "V" else "sA", cnt_))
                elif t >= 1 and ch < 4:
                    e_, cnt_ = lncp2[(t - 1, 28 + ch)]
                    wt.append(("sV" if e_ == "V" else "sA", cnt_))
                tm = c.op("T", lambda e, ch=ch, lt=lt: e.transpose(
                    ps_trD[ch % 4][:], lt[:, ch * 128:(ch + 1) * 128], ident_sb[:]),
                    waits=wt, inc="sT")
                tr_mark2[(t, ch)] = tm
                ceng = "V" if (ch % 2 == 0) else "A"
                dst = ln2_fm[:, ch * TOKW + t * 128: ch * TOKW + (t + 1) * 128]
                cm = c.op(ceng, lambda e, dst=dst, ch=ch: _copy(e, dst, ps_trD[ch % 4][:]),
                          waits=[("sT", tm)], inc="sV" if ceng == "V" else "sA")
                lncp2[(t, ch)] = (ceng, cm)
            wdots = [({"V": "sV", "A": "sA"}[lncp2[(t, ch)][0]], lncp2[(t, ch)][1])
                     for ch in (30, 31)]
            if t >= 1:
                wdots.append(("sV", dt_done[t - 1]))
            lastd = None
            for ch in range(32):
                lastd = c.op("T", lambda e, ch=ch, t=t: e.matmul(
                    ps_dots[:, 0:NCLUST], ln2_fm[:, ch * TOKW + t * 128: ch * TOKW + (t + 1) * 128],
                    cent_T[:, ch * NCLUST:(ch + 1) * NCLUST],
                    start=(ch == 0), stop=(ch == 31)),
                    waits=wdots if ch == 0 else [], inc="sT" if ch == 31 else None)
            d0 = c.op("V", lambda e: e.reduce_max(dstat[:, 0:1], ps_dots[:, 0:NCLUST], axis=AX),
                      waits=[("sT", lastd)], inc="sV")
            d1 = c.op("V", lambda e: e.tensor_scalar(dstat[:, 1:2], dstat[:, 0:1],
                                                     -TEMP_INV, None, ALU.mult),
                      waits=[("sV", d0)], inc="sV")
            d2 = c.op("A", lambda e: e.activation(out=dots_e[:], in_=ps_dots[:, 0:NCLUST], func=AF.Exp,
                                                  bias=dstat[:, 1:2], scale=TEMP_INV),
                      waits=[("sV", d1)], inc="sA")
            d3 = c.op("V", lambda e: e.reduce_sum(dstat[:, 2:3], dots_e[:], axis=AX),
                      waits=[("sA", d2)], inc="sV")
            d4 = c.op("V", lambda e: e.reciprocal(dstat[:, 3:4], dstat[:, 2:3]),
                      waits=[("sV", d3)], inc="sV")
            d5 = c.op("V", lambda e: e.tensor_scalar(dots_n[:], dots_e[:], dstat[:, 3:4],
                                                     None, ALU.mult),
                      waits=[("sV", d4)], inc="sV")
            d6 = c.op("V", lambda e: e.reduce_max(dstat[:, 4:5], dots_n[:], axis=AX),
                      waits=[("sV", d5)], inc="sV")
            c.op("V", lambda e, t=t: e.tensor_single_scalar(
                qs_all[:, t:t + 1], dstat[:, 4:5], THRESH, ALU.is_gt),
                waits=[("sV", d6)], inc="sV")
            td = c.op("T", lambda e: e.transpose(ps_dt[:], dots_n[:], identf_sb[:]),
                      waits=[("sV", d5), ("dP", idf_ready)], inc="sT")
            d8 = c.op("V", lambda e: e.reduce_max(cmax_t[:], ps_dt[:], axis=AX),
                      waits=[("sT", td)], inc="sV")
            d9 = c.op("V", lambda e: e.tensor_tensor(run_cmax[:], run_cmax[:], cmax_t[:],
                                                     ALU.max),
                      waits=[("sV", d8)], inc="sV")
            dt_done[t] = d9
        wt0w = [("dDg", restage_done),
                ("sV", ln_free_v[nT - 1]), ("sV", ln_free_v[nT - 2])]
        for ch in range(32):
            c.op("Y", lambda e, ch=ch: e.dma_start(
                out=wT_sb[0][:, ch, :],
                in_=w_bf_scr.ap()[0:512, ch * 128:(ch + 1) * 128],
                transpose=True),
                waits=wt0w if ch == 0 else [], inc="dWT0", inc_n=16)
        c.dma("G", cc_in.ap(), run_cmax[:], "dCC", waits=[("sV", dt_done[nT - 1])])
        ccm = c.op("G", lambda e: e.collective_compute(
            "AllReduce", ALU.max, replica_groups=[list(range(N_CORES))],
            ins=[cc_in.ap().opt()], outs=[cc_out.ap().opt()]),
            waits=[("dCC", c.c("dCC"))], inc="sG")
        c.dma("G", gmax_sb[:], cc_out.ap(), "dCC", waits=[("sG", ccm)])
        dcc2 = c.c("dCC")
        e0 = c.op("V", lambda e: e.tensor_single_scalar(cs_f[:], gmax_sb[:], THRESH,
                                                        ALU.is_gt),
                  waits=[("dCC", dcc2)], inc="sV")
        e1 = c.op("V", lambda e: e.tensor_copy(cs_bf[:], cs_f[:]),
                  waits=[("sV", e0)], inc="sV")
        for ocb in range(8):
            tb = c.op("T", lambda e, ocb=ocb: e.matmul(
                ps_dots[0:1, :], cs_bf[:], a1_sb[:, ocb * 512:(ocb + 1) * 512],
                start=True, stop=True),
                waits=[("sV", e1)] if ocb == 0 else [("sV", c.c("sV"))], inc="sT")
            vcp0 = c.op("V", lambda e: e.tensor_copy(rs_row_bf[:], ps_dots[0:1, :]),
                        waits=[("sT", tb)], inc="sV")
            tb2 = c.op("T", lambda e: e.matmul(
                ps_dots[:, :], ones_sb[0:1, :], rs_row_bf[:], start=True, stop=True),
                waits=[("sV", vcp0)], inc="sT")
            vcp1 = c.op("V", lambda e: e.tensor_copy(rs_stage[:], ps_dots[:, :]),
                        waits=[("sT", tb2), ("dDg", c.c("dDg"))], inc="sV")
            c.dma("G", rs_bc_scr.ap()[:, ocb * 512:(ocb + 1) * 512], rs_stage[:],
                  "dDg", waits=[("sV", vcp1)])
        wt1w = [("dDg", c.c("dDg")), ("sV", c.c("sV")), ("sA", c.c("sA")),
                ("sT", c.c("sT"))]
        for ch in range(32):
            c.op("Y", lambda e, ch=ch: e.dma_start(
                out=wT_sb[1][:, ch, :],
                in_=w_bf_scr.ap()[512:1024, ch * 128:(ch + 1) * 128],
                transpose=True),
                waits=wt1w if ch == 0 else [], inc="dWT1", inc_n=16)

        d_last_mm = {}
        dslot_v = {}
        x1b_free = {}
        outst_free = {}
        rs_used = {}

        pend_wT = []

        def load_wT(oc, queue):
            par = oc % 2
            wY = [("dDg", restage_done)]
            if oc >= 2:
                wY.append(("sT", d_last_mm[oc - 2]))
            for ch in range(32):
                def em1(ch=ch, oc=oc, par=par, w=tuple(wY if ch == 0 else ()),
                        queue=queue):
                    c.op(queue, lambda e, oc=oc, ch=ch, par=par: e.dma_start(
                        out=wT_sb[par][:, ch, :],
                        in_=w_bf_scr.ap()[oc * 512:(oc + 1) * 512, ch * 128:(ch + 1) * 128],
                        transpose=True),
                        waits=list(w), inc=f"dWT{par}", inc_n=16)
                pend_wT.append(em1)

        def load_rs(oc):
            par = oc % 2
            wG = [("dDg", c.c("dDg"))]
            if par in rs_used:
                wG.append(("sV", rs_used[par]))
            if oc >= 2:
                wG.append(("sT", d_last_mm[oc - 2]))
            c.dma("G", biasb_oc[par][:], ffn_b[0:1, oc * 512:(oc + 1) * 512],
                  f"dWB{par}", waits=wG)
            c.dma("G", rs_bc[par][:], rs_bc_scr.ap()[:, oc * 512:(oc + 1) * 512],
                  f"dWF{par}", waits=wG)

        load_rs(0)
        load_rs(1)
        for oc in range(8):
            par = oc % 2
            if oc >= 1 and oc + 1 < 8:
                load_wT(oc + 1, "Y")
            wt_ready = c.c(f"dWT{par}")
            for t in range(nT):
                for _ in range(-(-32 // nT)):
                    if pend_wT:
                        pend_wT.pop(0)()
                slot = (oc * nT + t) % 2
                sslot = (oc * nT + t) % 2
                wmm = []
                if t == 0:
                    wmm.append((f"dWT{par}", wt_ready))
                    wmm.append((f"dWB{par}", c.c(f"dWB{par}")))
                if oc == 0:
                    wmm.extend([({"V": "sV", "A": "sA"}[lncp2[(t, ch)][0]],
                                 lncp2[(t, ch)][1]) for ch in (30, 31)])
                if slot in dslot_v:
                    wmm.append(("sV", dslot_v[slot]))
                for ch in range(32):
                    c.op("T", lambda e, ch=ch, t=t, slot=slot, par=par: e.matmul(
                        ps_d[slot][:],
                        ln2_fm[:, ch * TOKW + t * 128: ch * TOKW + (t + 1) * 128],
                        wT_sb[par][:, ch, :],
                        start=(ch == 0), stop=False),
                        waits=wmm if ch == 0 else [], inc=None)
                last = c.op("T", lambda e, par=par, slot=slot: e.matmul(
                    ps_d[slot][:], ones_sb[0:1, 0:128], biasb_oc[par][:],
                    start=False, stop=True), inc="sT")
                d_last_mm[oc] = last
                wx1 = [("sV", x1b_free[sslot])] if sslot in x1b_free else []
                c.dma("Y", x1s_stage[sslot][:],
                      x1_scr.ap()[t * 128:(t + 1) * 128, oc * 512:(oc + 1) * 512],
                      f"dX1B{sslot}", waits=wx1)
                x1_ready = c.c(f"dX1B{sslot}")
                wmsk = [("sT", last), (f"dWF{par}", c.c(f"dWF{par}"))]
                if sslot in outst_free:
                    wmsk.append((f"dOUT{sslot}", outst_free[sslot]))
                v0_ = c.op("V", lambda e, t=t, slot=slot, sslot=sslot, par=par: e.scalar_tensor_tensor(
                    out_stage[sslot][:], ps_d[slot][:], qs_all[:, t:t + 1],
                    rs_bc[par][:], ALU.mult, ALU.mult),
                    waits=wmsk, inc="sV")
                dslot_v[slot] = v0_
                rs_used[par] = v0_
                v1_ = c.op("V", lambda e, sslot=sslot: e.tensor_tensor(
                    out_stage[sslot][:], out_stage[sslot][:], x1s_stage[sslot][:],
                    ALU.add), waits=[("sV", v0_), (f"dX1B{sslot}", x1_ready)], inc="sV")
                x1b_free[sslot] = v1_
                do = c.dma("Y", out[t * 128:(t + 1) * 128, oc * 512:(oc + 1) * 512],
                           out_stage[sslot][:], f"dOUT{sslot}", waits=[("sV", v1_)])
                outst_free[sslot] = do
            if oc + 2 < 8:
                load_rs(oc + 2)
        c.wait("Y", ("dOUT0", c.c("dOUT0")), ("dOUT1", c.c("dOUT1")))
        c.barrier(self_waits={"Y": [("dOUT0", c.c("dOUT0")), ("dOUT1", c.c("dOUT1")),
                                    ("dX1B0", c.c("dX1B0")), ("dX1B1", c.c("dX1B1")),
                                    ("dWT0", c.c("dWT0")), ("dWT1", c.c("dWT1"))],
                              "G": [("dWF0", c.c("dWF0")), ("dWF1", c.c("dWF1")),
                                    ("dWB0", c.c("dWB0")), ("dWB1", c.c("dWB1")),
                                    ("dX1G0", c.c("dX1G0")), ("dX1G1", c.c("dX1G1"))]})

    with nc.Block() as block:
        @block.tensor
        def _(eng):
            emit(Ctx(nc, "T", engines, sems))

        @block.vector
        def _(eng):
            emit(Ctx(nc, "V", engines, sems))

        @block.scalar
        def _(eng):
            emit(Ctx(nc, "A", engines, sems))

        @block.gpsimd
        def _(eng):
            emit(Ctx(nc, "G", engines, sems))

        @block.sync
        def _(eng):
            emit(Ctx(nc, "Y", engines, sems))

    return nc


def kernel(L0_attn_hidden_states, L0_attn_w_qkv, L0_attn_w_o, L0_ffn_weight,
           L0_ffn_bias, L0_ffn_centroids, L0_ffn_assignments):
    x = np.ascontiguousarray(np.asarray(L0_attn_hidden_states, dtype=np.float32))
    wq = np.ascontiguousarray(np.asarray(L0_attn_w_qkv, dtype=np.float32))
    wo = np.ascontiguousarray(np.asarray(L0_attn_w_o, dtype=np.float32))
    fw = np.ascontiguousarray(np.asarray(L0_ffn_weight, dtype=np.float32))
    fb = np.ascontiguousarray(np.asarray(L0_ffn_bias, dtype=np.float32)).reshape(1, -1)
    ce = np.ascontiguousarray(np.asarray(L0_ffn_centroids, dtype=np.float32))
    asg = np.ascontiguousarray(np.asarray(L0_ffn_assignments, dtype=np.int32)).reshape(1, -1)

    B, S, H_ = x.shape
    assert B == N_CORES and H_ == H

    nc = build_nc(S=S)
    in_maps = [{"x0": x[b], "w_qkv": wq, "w_o": wo, "ffn_w": fw,
                "ffn_b": fb, "cent": ce, "assign": asg} for b in range(B)]
    res = run_bass_kernel_spmd(nc, in_maps, core_ids=list(range(N_CORES)))
    return np.stack([res.results[b]["out"] for b in range(B)], axis=0)
